# version 06
# baseline (speedup 1.0000x reference)
import contextlib
import numpy as np
import ml_dtypes
import concourse.bass as bass
import concourse.mybir as mybir
from concourse.bass_utils import run_bass_kernel_spmd

F32 = mybir.dt.float32
BF16 = mybir.dt.bfloat16
I32 = mybir.dt.int32
AF = mybir.ActivationFunctionType
ALU = mybir.AluOpType
AX = mybir.AxisListType

T = 4096
F = 1024
H = 16
HD = 64


class _FakeInstr:
    def then_inc(self, sem, n):
        return self


_FAKE_INSTR = _FakeInstr()


class _FakeEng:
    def __getattr__(self, name):
        def f(*a, **k):
            return _FAKE_INSTR
        return f


class Ctx:

    RINGS = {"sync": tuple(f"s{i}" for i in range(8)),
             "gpsimd": tuple(f"g{i}" for i in range(6))}

    def __init__(self, live_name, eng, sems):
        fake = _FakeEng()
        for nm in ("sync", "tensor", "vector", "scalar", "gpsimd"):
            setattr(self, nm, fake)
        setattr(self, live_name, eng)
        self.live = live_name
        self.sem = sems
        self.n = {k: 0 for k in sems}
        self._rr = {"sync": 0, "gpsimd": 0}

    def bump(self, name, amount=1):
        self.n[name] += amount
        return self.sem[name], amount

    def dma(self, eng_name, out, in_):
        ring = self.RINGS[eng_name]
        k = self._rr[eng_name]
        self._rr[eng_name] = (k + 1) % len(ring)
        s = ring[k]
        eng = getattr(self, eng_name)
        eng.wait_ge(self.sem[s], self.n[s])
        eng.dma_start(out=out, in_=in_).then_inc(*self.bump(s, 16))
        return self.sem[s], self.n[s]

    def dma_group(self, eng_name, transfers):
        assert eng_name == "sync"
        ring = self.RINGS[eng_name]
        k = self._rr[eng_name]
        self._rr[eng_name] = (k + 1) % len(ring)
        s = ring[k]
        eng = getattr(self, eng_name)
        eng.wait_ge(self.sem[s], self.n[s])
        for out, in_ in transfers:
            eng.dma_start(out=out, in_=in_).then_inc(*self.bump(s, 16))
        return self.sem[s], self.n[s]

    def gdma(self, out, in_):
        return self.dma("gpsimd", out, in_)

    def sdma(self, out, in_):
        return self.dma("sync", out, in_)

    def wait(self, eng_name, pair):
        getattr(self, eng_name).wait_ge(*pair)

    def drain_dmas(self):
        for ring in self.RINGS.values():
            for s in ring:
                self.sync.wait_ge(self.sem[s], self.n[s])


DMA_SEMS = [f"s{i}" for i in range(8)] + [f"g{i}" for i in range(6)]


def run_program(nc, program, sem_names):
    with contextlib.ExitStack() as st:
        sems = {nm: st.enter_context(nc.semaphore(nm))
                for nm in list(sem_names) + DMA_SEMS}
        block = st.enter_context(nc.Block())
        for eng_name in ("sync", "tensor", "vector", "scalar", "gpsimd"):
            getattr(block, eng_name)(
                lambda eng, _nm=eng_name: program(Ctx(_nm, eng, sems)))


def _b(c, name, amount=1):
    return c.bump(name, amount)



def emit_ln0_and_layout(c, b):
    x_hbm = b["x"]
    xf_hbm = b["xf"]
    xnf_hbm = b["xnf"]
    xt = b["xt"]
    xnt = b["xnt"]
    st1 = b["st1"]
    psT = b["psT"]
    psT2 = b["psT2"]
    ev = b["evac"]
    ev2 = b["evac2"]

    id_w = c.sdma(b["identsb"][:, :], b["ident"][:, :])
    c.wait("tensor", id_w)

    _sv = c.bump("vs")
    c.vector.memset(st1[:, 14:15], 1e-5).then_inc(*_sv)

    for g in range(8):
        c.gpsimd.wait_ge(c.sem["pe"], c.n["pe"])
        lds = []
        for j in range(4):
            tt = g * 4 + j
            lds.append(c.gdma(xt[:, j * 1024:(j + 1) * 1024],
                              x_hbm[tt * 128:(tt + 1) * 128, :]))
        for p_ in lds:
            c.wait("vector", p_)
        c.vector.wait_ge(c.sem["pe"], c.n["pe"])
        for j in range(4):
            xs = xt[:, j * 1024:(j + 1) * 1024]
            xns = xnt[:, j * 1024:(j + 1) * 1024]
            c.vector.bn_stats(out=st1[:, 0:6], in_=xs[:, 0:512])
            _sv = c.bump("vs")
            c.vector.bn_stats(out=st1[:, 6:12], in_=xs[:, 512:1024]).then_inc(*_sv)
            c.vector.wait_ge(c.sem["vs"], c.n["vs"])
            _sv = c.bump("vs")
            c.vector.bn_aggr(out=st1[:, 12:14], in_=st1[:, 0:12]).then_inc(*_sv)
            c.scalar.wait_ge(c.sem["vs"], c.n["vs"])
            _ss = c.bump("ss")
            c.scalar.activation(out=st1[:, 15:16], in_=st1[:, 13:14], func=AF.Ln,
                                bias=st1[:, 14:15], scale=1.0).then_inc(*_ss)
            c.scalar.wait_ge(c.sem["ss"], c.n["ss"])
            _ss = c.bump("ss")
            c.scalar.activation(out=st1[:, 16:17], in_=st1[:, 15:16], func=AF.Exp,
                                bias=0.0, scale=-0.5).then_inc(*_ss)
            c.vector.wait_ge(c.sem["ss"], c.n["ss"])
            _sv = c.bump("vs")
            c.vector.tensor_scalar(out=xns, in0=xs, scalar1=st1[:, 12:13],
                                   scalar2=st1[:, 16:17],
                                   op0=ALU.subtract, op1=ALU.mult).then_inc(*_sv)
        c.tensor.wait_ge(c.sem["vs"], c.n["vs"])
        for src, dst_hbm, ps, evb in ((xt, xf_hbm, psT, ev), (xnt, xnf_hbm, psT2, ev2)):
            for ch in range(8):
                c.tensor.wait_ge(c.sem["ev"], c.n["ev"])
                for j in range(4):
                    _sp = c.bump("pe")
                    c.tensor.transpose(
                        ps[:, j * 128:(j + 1) * 128],
                        src[:, j * 1024 + ch * 128: j * 1024 + (ch + 1) * 128],
                        b["identsb"][:, :],
                    ).then_inc(*_sp)
                c.vector.wait_ge(c.sem["pe"], c.n["pe"])
                _se = c.bump("ev")
                c.vector.tensor_copy(out=evb[:, :], in_=ps[:, 0:512]).then_inc(*_se)
                c.sync.wait_ge(c.sem["ev"], c.n["ev"])
                st_w = c.sdma(dst_hbm[ch * 128:(ch + 1) * 128, g * 512:(g + 1) * 512],
                              evb[:, :])
                c.wait("vector", st_w)
    c.drain_dmas()


def build_launch1():
    nc = bass.Bass("TRN2", target_bir_lowering=False, debug=False)
    x = nc.dram_tensor("x", [T, F], F32, kind="ExternalInput").ap()
    ident = nc.dram_tensor("ident", [128, 128], BF16, kind="ExternalInput").ap()
    xf = nc.dram_tensor("xf", [F, T], BF16, kind="ExternalOutput").ap()
    xnf = nc.dram_tensor("xnf", [F, T], BF16, kind="ExternalOutput").ap()
    with contextlib.ExitStack() as stk:
        bufs = {
            "x": x, "xf": xf, "xnf": xnf, "ident": ident,
            "xt": stk.enter_context(nc.sbuf_tensor("xt", [128, 4096], BF16)),
            "xnt": stk.enter_context(nc.sbuf_tensor("xnt", [128, 4096], BF16)),
            "st1": stk.enter_context(nc.sbuf_tensor("st1", [128, 18], F32)),
            "identsb": stk.enter_context(nc.sbuf_tensor("identsb", [128, 128], BF16)),
            "evac": stk.enter_context(nc.sbuf_tensor("evac", [128, 512], BF16)),
            "evac2": stk.enter_context(nc.sbuf_tensor("evac2", [128, 512], BF16)),
            "psT": stk.enter_context(nc.psum_tensor("psT", [128, 1024], BF16)),
            "psT2": stk.enter_context(nc.psum_tensor("psT2", [128, 1024], BF16)),
        }
        run_program(nc, lambda c: emit_ln0_and_layout(c, bufs),
                    ["vs", "ss", "pe", "ev"])
    return nc


def host_launch1(x_all):
    nc = build_launch1()
    identv = np.eye(128, dtype=np.float32).astype(ml_dtypes.bfloat16)
    in_maps = [{"x": np.ascontiguousarray(x_all[cc]), "ident": identv} for cc in range(8)]
    res = run_bass_kernel_spmd(nc, in_maps, core_ids=list(range(8)))
    return res.results



def emit_l0_ffn(c, b, wd_pre=None):
    xnf_hbm = b["xnf"]; xf_hbm = b["xf"]; y1_hbm = b["y1"]
    wgT = b["wgT"]; wuT = b["wuT"]; wdT = b["wdT"]
    xn = b["xn_sb"]
    wg_sb = b["wg_sb"]
    wu_sb = b["wu_sb"]
    wd_sb = b["wd_sb"]
    hbuf = b["h_sb"]
    sg = b["sg_sb"]
    xfa = b["xf_sb"]
    y1sb = b["y1_sb"]
    psG = b["psG"]; psU = b["psU"]; psD = b["psD"]
    TTILE = 1024
    INV127 = 1.0 / 127.0

    xn_w = c.dma_group("sync", [
        (xn[:, ch * T:(ch + 1) * T], xnf_hbm[ch * 128:(ch + 1) * 128, :])
        for ch in range(8)])
    if wd_pre is not None:
        wd_waits = wd_pre
    else:
        wd_waits = []
        for hc in range(24):
            wd_waits.append(c.gdma(wd_sb[:, hc * 1024:(hc + 1) * 1024],
                                   wdT[hc * 128:(hc + 1) * 128, :]))

    wbuf_free_at = {0: 0, 1: 0}

    for tt in range(4):
        t0 = tt * TTILE
        for hc in range(24):
            par = hc % 2
            c.gpsimd.wait_ge(c.sem["mm"], wbuf_free_at[par])
            w1 = c.gdma(wg_sb[:, par * 1024:(par + 1) * 1024].rearrange("p (c h) -> p c h", h=128),
                        wgT[:, hc * 128:(hc + 1) * 128].rearrange("(c p) h -> p c h", p=128))
            w2 = c.gdma(wu_sb[:, par * 1024:(par + 1) * 1024].rearrange("p (c h) -> p c h", h=128),
                        wuT[:, hc * 128:(hc + 1) * 128].rearrange("(c p) h -> p c h", p=128))
            c.wait("tensor", w1)
            c.wait("tensor", w2)
            if tt == 0 and hc == 0:
                c.wait("tensor", xn_w)
            for tb in range(2):
                tb0 = t0 + tb * 512
                c.tensor.wait_ge(c.sem["ss"], c.n["ss"])
                c.tensor.wait_ge(c.sem["hv"], c.n["hv"])
                for cc in range(8):
                    c.tensor.matmul(psG[:, :],
                                    wg_sb[:, par * 1024 + cc * 128:par * 1024 + (cc + 1) * 128],
                                    xn[:, cc * T + tb0: cc * T + tb0 + 512],
                                    start=(cc == 0), stop=(cc == 7))
                ins = None
                for cc in range(8):
                    ins = c.tensor.matmul(psU[:, :],
                                          wu_sb[:, par * 1024 + cc * 128:par * 1024 + (cc + 1) * 128],
                                          xn[:, cc * T + tb0: cc * T + tb0 + 512],
                                          start=(cc == 0), stop=(cc == 7))
                ins.then_inc(*_b(c, "mm"))
                c.scalar.wait_ge(c.sem["mm"], c.n["mm"])
                c.scalar.wait_ge(c.sem["hv"], c.n["hv"])
                _ss = c.bump("ss")
                c.scalar.activation(out=sg[:, :], in_=psG[:, :], func=AF.Silu,
                                    bias=0.0, scale=INV127).then_inc(*_ss)
                c.vector.wait_ge(c.sem["ss"], c.n["ss"])
                _sh = c.bump("hv")
                c.vector.scalar_tensor_tensor(
                    out=hbuf[:, hc * 1024 + tb * 512: hc * 1024 + tb * 512 + 512],
                    in0=psU[:, :], scalar=INV127, in1=sg[:, :],
                    op0=ALU.mult, op1=ALU.mult).then_inc(*_sh)
            wbuf_free_at[par] = c.n["mm"]
        if tt == 0:
            for p_ in wd_waits:
                c.wait("tensor", p_)
        c.tensor.wait_ge(c.sem["hv"], c.n["hv"])
        for fo in range(8):
            for tb in range(2):
                tb0 = t0 + tb * 512
                pp = (fo * 2 + tb) % 2
                xfw = c.sdma(xfa[:, pp * 512:(pp + 1) * 512],
                             xf_hbm[fo * 128:(fo + 1) * 128, tb0:tb0 + 512])
                c.tensor.wait_ge(c.sem["yv"], c.n["yv"])
                ins = None
                for hc in range(24):
                    ins = c.tensor.matmul(psD[:, :],
                                          wd_sb[:, hc * 1024 + fo * 128: hc * 1024 + (fo + 1) * 128],
                                          hbuf[:, hc * 1024 + tb * 512: hc * 1024 + tb * 512 + 512],
                                          start=(hc == 0), stop=(hc == 23))
                ins.then_inc(*_b(c, "mm"))
                c.vector.wait_ge(c.sem["mm"], c.n["mm"])
                c.wait("vector", xfw)
                _sy = c.bump("yv")
                c.vector.scalar_tensor_tensor(
                    out=y1sb[:, pp * 512:(pp + 1) * 512],
                    in0=psD[:, :], scalar=INV127, in1=xfa[:, pp * 512:(pp + 1) * 512],
                    op0=ALU.mult, op1=ALU.add).then_inc(*_sy)
                c.sync.wait_ge(c.sem["yv"], c.n["yv"])
                stw = c.sdma(y1_hbm[fo * 128:(fo + 1) * 128, tb0:tb0 + 512],
                             y1sb[:, pp * 512:(pp + 1) * 512])
                c.wait("vector", stw)
        c.vector.wait_ge(c.sem["mm"], c.n["mm"])
    c.drain_dmas()


def build_launch2():
    nc = bass.Bass("TRN2", target_bir_lowering=False, debug=False)
    xnf = nc.dram_tensor("xnf", [F, T], BF16, kind="ExternalInput").ap()
    xf = nc.dram_tensor("xf", [F, T], BF16, kind="ExternalInput").ap()
    wgT = nc.dram_tensor("wgT", [1024, 3072], I32, kind="ExternalInput").ap()
    wuT = nc.dram_tensor("wuT", [1024, 3072], I32, kind="ExternalInput").ap()
    wdT = nc.dram_tensor("wdT", [3072, 1024], I32, kind="ExternalInput").ap()
    y1 = nc.dram_tensor("y1", [F, T], BF16, kind="ExternalOutput").ap()
    with contextlib.ExitStack() as stk:
        bufs = {
            "xnf": xnf, "xf": xf, "y1": y1, "wgT": wgT, "wuT": wuT, "wdT": wdT,
            "xn_sb": stk.enter_context(nc.sbuf_tensor("xn_sb", [128, 8 * T], BF16)),
            "wg_sb": stk.enter_context(nc.sbuf_tensor("wg_sb", [128, 2 * 1024], BF16)),
            "wu_sb": stk.enter_context(nc.sbuf_tensor("wu_sb", [128, 2 * 1024], BF16)),
            "wd_sb": stk.enter_context(nc.sbuf_tensor("wd_sb", [128, 24 * 1024], BF16)),
            "h_sb": stk.enter_context(nc.sbuf_tensor("h_sb", [128, 24 * 1024], BF16)),
            "sg_sb": stk.enter_context(nc.sbuf_tensor("sg_sb", [128, 512], BF16)),
            "xf_sb": stk.enter_context(nc.sbuf_tensor("xf_sb", [128, 2 * 512], BF16)),
            "y1_sb": stk.enter_context(nc.sbuf_tensor("y1_sb", [128, 2 * 512], BF16)),
            "psG": stk.enter_context(nc.psum_tensor("psG", [128, 512], F32)),
            "psU": stk.enter_context(nc.psum_tensor("psU", [128, 512], F32)),
            "psD": stk.enter_context(nc.psum_tensor("psD", [128, 512], F32)),
        }
        run_program(nc, lambda c: emit_l0_ffn(c, bufs),
                    ["mm", "ss", "hv", "yv"])
    return nc



def emit_l0_attn(c, b, n_pairs=8, n_qt=32, stage=99):
    xnf_hbm = b["xnf"]; y1_hbm = b["y1"]; x1_hbm = b["x1"]
    WqT = b["WqT"]; WkT = b["WkT"]; WvT = b["WvT"]; WoT = b["WoT"]
    bqkv = b["bqkv"]
    bo_sb = b["bo_sb"]
    xnc = b["xnc"]
    wq_sb = b["wq_sb"]
    wk_sb = b["wk_sb"]; wv_sb = b["wv_sb"]
    wo_sb = b["wo_sb"]
    Qp = b["Qp"]; Kp = b["Kp"]; Vp = b["Vp"]
    vt = b["vt"]
    eS = b["eS"]; pA = b["pA"]; pB = b["pB"]
    t8 = b["t8"]
    zi = b["zi"]
    zrow = b["zrow"]
    zbc = b["zbc"]
    ones1 = b["ones1"]
    acc = b["acc"]
    y1t = b["y1t"]
    x1t = b["x1t"]
    psS = b["psS"]
    psT = b["psT"]
    psPV = b["psPV"]
    psB = b["psB"]
    psP = b["psP"]

    _sv = c.bump("vD")
    c.vector.memset(vt[:, :].rearrange("p (kc h d) -> p kc h d", h=2, d=65)[:, :, :, 64:65], 1.0)
    c.vector.memset(ones1[:, :], 1.0).then_inc(*_sv)
    bw = c.dma_group("sync", [(bqkv[:, :], b["bqkv_hbm"][:, :]),
                              (bo_sb[:, :], b["bo_hbm"][:, :])])
    c.wait("vector", bw)

    xnc_free_at = {0: 0, 1: 0}
    for p in range(n_pairs):
        c.gpsimd.wait_ge(c.sem["tP"], c.n["tP"])
        wqw = c.gdma(wq_sb.rearrange("p (cc o) -> p cc o", o=128),
                     WqT[:, p * 128:(p + 1) * 128].rearrange("(cc p2) o -> p2 cc o", p2=128))
        wkw = c.gdma(wk_sb.rearrange("p (cc o) -> p cc o", o=128),
                     WkT[:, p * 128:(p + 1) * 128].rearrange("(cc p2) o -> p2 cc o", p2=128))
        wvw = c.gdma(wv_sb.rearrange("p (cc o) -> p cc o", o=128),
                     WvT[:, p * 128:(p + 1) * 128].rearrange("(cc p2) o -> p2 cc o", p2=128))
        wow = c.gdma(wo_sb[:, :], WoT[p * 128:(p + 1) * 128, :])
        for pair_ in (wqw, wkw, wvw):
            c.wait("tensor", pair_)

        for tb in range(8):
            tb0 = tb * 512
            parx = tb % 2
            c.sync.wait_ge(c.sem["tP"], xnc_free_at[parx])
            xw = c.dma_group("sync", [
                (xnc[:, parx * 4096 + cc * 512: parx * 4096 + (cc + 1) * 512],
                 xnf_hbm[cc * 128:(cc + 1) * 128, tb0:tb0 + 512]) for cc in range(8)])
            c.wait("tensor", xw)
            for dst, wsb, bcol, scale in ((Qp, wq_sb, 0, 0.125), (Kp, wk_sb, 8, None), (Vp, wv_sb, 16, None)):
                c.tensor.wait_ge(c.sem["vD"], c.n["vD"])
                c.tensor.wait_ge(c.sem["wv"], c.n["wv"])
                ins = None
                for cc in range(8):
                    ins = c.tensor.matmul(psP[:, :], wsb[:, cc * 128:(cc + 1) * 128],
                                          xnc[:, parx * 4096 + cc * 512: parx * 4096 + (cc + 1) * 512],
                                          start=(cc == 0), stop=(cc == 7))
                ins.then_inc(*_b(c, "tP"))
                c.vector.wait_ge(c.sem["tP"], c.n["tP"])
                _sv = c.bump("vD")
                if scale is None:
                    c.vector.tensor_scalar(out=dst[:, tb0:tb0 + 512], in0=psP[:, :],
                                           scalar1=bqkv[:, bcol + p:bcol + p + 1], scalar2=None,
                                           op0=ALU.add).then_inc(*_sv)
                else:
                    c.vector.tensor_scalar(out=dst[:, tb0:tb0 + 512], in0=psP[:, :],
                                           scalar1=bqkv[:, bcol + p:bcol + p + 1], scalar2=scale,
                                           op0=ALU.add, op1=ALU.mult).then_inc(*_sv)
            xnc_free_at[parx] = c.n["tP"]

        if stage < 2:
            continue
        c.tensor.wait_ge(c.sem["vD"], c.n["vD"])
        for kc in range(32):
            for hh in range(2):
                _sp = c.bump("tP")
                c.tensor.transpose(psT[:, hh * 512:hh * 512 + 64],
                                   Vp[hh * 64:(hh + 1) * 64, kc * 128:(kc + 1) * 128],
                                   b["identsb"][hh * 64:(hh + 1) * 64, hh * 64:(hh + 1) * 64]).then_inc(*_sp)
                c.tensor.wait_ge(c.sem["tP"], c.n["tP"])
            c.vector.wait_ge(c.sem["tP"], c.n["tP"])
            _sv = c.bump("vD")
            c.vector.tensor_copy(out=vt[:, kc * 130: kc * 130 + 64], in_=psT[:, 0:64]).then_inc(*_sv)
            _sv = c.bump("vD")
            c.vector.tensor_copy(out=vt[:, kc * 130 + 65: kc * 130 + 129], in_=psT[:, 512:576]).then_inc(*_sv)
            c.tensor.wait_ge(c.sem["vD"], c.n["vD"])

        if stage < 3:
            continue
        for hh in range(2):
            hbase = hh * 64
            for qt in range(n_qt):
                q0 = qt * 128
                for half in range(2):
                    k0 = half * 2048
                    c.tensor.wait_ge(c.sem["sA"], c.n["sA"])
                    ins = None
                    for kk in range(4):
                        ins = c.tensor.matmul(psS[:, kk * 512:(kk + 1) * 512],
                                              Qp[hbase:hbase + 64, q0:q0 + 128],
                                              Kp[hbase:hbase + 64, k0 + kk * 512: k0 + (kk + 1) * 512],
                                              start=True, stop=True)
                    ins.then_inc(*_b(c, "tP"))
                    c.scalar.wait_ge(c.sem["tP"], c.n["tP"])
                    _ss = c.bump("sA")
                    c.scalar.activation(out=eS[:, k0:k0 + 2048], in_=psS[:, :],
                                        func=AF.Exp, bias=0.0, scale=1.0).then_inc(*_ss)
                if stage < 4:
                    continue
                c.vector.wait_ge(c.sem["sA"], c.n["sA"])
                src = eS
                dst = pA
                for rnd in range(4):
                    _sv = c.bump("vD")
                    c.vector.max(out=t8[:, :], in_=src[:, :]).then_inc(*_sv)
                    c.vector.wait_ge(c.sem["vD"], c.n["vD"])
                    _sv = c.bump("vD")
                    c.vector.match_replace(out=dst[:, :], in_to_replace=t8[:, :],
                                           in_values=src[:, :], imm_value=0.0).then_inc(*_sv)
                    c.vector.wait_ge(c.sem["vD"], c.n["vD"])
                    src, dst = dst, (pB if dst is pA else pA)
                _sv = c.bump("vD")
                c.vector.tensor_tensor(out=dst[:, :], in0=eS[:, :], in1=src[:, :],
                                       op=ALU.subtract).then_inc(*_sv)
                Pq = dst
                if stage < 5:
                    continue
                c.tensor.wait_ge(c.sem["vD"], c.n["vD"])
                for kg in range(8):
                    for j in range(4):
                        _sp = c.bump("tP")
                        c.tensor.transpose(psT[:, j * 128:(j + 1) * 128],
                                           Pq[:, (kg * 4 + j) * 128:(kg * 4 + j + 1) * 128],
                                           b["identsb"][:, :]).then_inc(*_sp)
                    c.vector.wait_ge(c.sem["tP"], c.n["tP"])
                    _sv = c.bump("vD")
                    c.vector.tensor_copy(out=pA[:, kg * 512:(kg + 1) * 512] if Pq is pB else pB[:, kg * 512:(kg + 1) * 512],
                                         in_=psT[:, 0:512]).then_inc(*_sv)
                    c.tensor.wait_ge(c.sem["vD"], c.n["vD"])
                Pt = pA if Pq is pB else pB
                if stage < 6:
                    continue
                c.tensor.wait_ge(c.sem["vD"], c.n["vD"])
                ins = None
                for kc in range(32):
                    ins = c.tensor.matmul(psPV[0:65, 0:128],
                                          vt[:, kc * 130 + hh * 65: kc * 130 + hh * 65 + 65],
                                          Pt[:, kc * 128:(kc + 1) * 128],
                                          start=(kc == 0), stop=(kc == 31))
                ins.then_inc(*_b(c, "tP"))
                if stage < 7:
                    continue
                c.vector.wait_ge(c.sem["tP"], c.n["tP"])
                _sv = c.bump("vD")
                c.vector.reciprocal(out=zrow[:, :], in_=psPV[64:65, 0:128]).then_inc(*_sv)
                c.tensor.wait_ge(c.sem["vD"], c.n["vD"])
                _sp = c.bump("tP")
                c.tensor.matmul(psB[0:64, 0:128], ones1[:, :], zrow[:, :],
                                start=True, stop=True).then_inc(*_sp)
                c.vector.wait_ge(c.sem["tP"], c.n["tP"])
                _sv = c.bump("vD")
                c.vector.tensor_copy(out=zbc[:, :], in_=psB[0:64, 0:128]).then_inc(*_sv)
                c.vector.wait_ge(c.sem["vD"], c.n["vD"])
                _sv = c.bump("vD")
                c.vector.tensor_tensor(out=acc[hbase:hbase + 64, q0:q0 + 128],
                                       in0=psPV[0:64, 0:128], in1=zbc[:, :], op=ALU.mult).then_inc(*_sv)
        if stage < 8:
            continue
        c.wait("tensor", wow)
        c.tensor.wait_ge(c.sem["vD"], c.n["vD"])
        for fo in range(8):
            for tb in range(8):
                tb0 = tb * 512
                c.tensor.wait_ge(c.sem["wv"], c.n["wv"])
                _sp = c.bump("tP")
                c.tensor.matmul(psP[:, :], wo_sb[:, fo * 128:(fo + 1) * 128],
                                acc[:, :][0:128, tb0:tb0 + 512],
                                start=True, stop=True).then_inc(*_sp)
                c.vector.wait_ge(c.sem["tP"], c.n["tP"])
                _sw = c.bump("wv")
                if p == 0:
                    c.vector.tensor_copy(out=b["wacc"][:, fo * 4096 + tb0: fo * 4096 + tb0 + 512],
                                         in_=psP[:, :]).then_inc(*_sw)
                else:
                    c.vector.tensor_tensor(out=b["wacc"][:, fo * 4096 + tb0: fo * 4096 + tb0 + 512],
                                           in0=b["wacc"][:, fo * 4096 + tb0: fo * 4096 + tb0 + 512],
                                           in1=psP[:, :], op=ALU.add).then_inc(*_sw)
    for fo in range(8):
        for tb in range(8):
            tb0 = tb * 512
            pp = tb % 2
            yw = c.sdma(y1t[:, pp * 512:(pp + 1) * 512],
                        y1_hbm[fo * 128:(fo + 1) * 128, tb0:tb0 + 512])
            c.wait("vector", yw)
            c.vector.wait_ge(c.sem["wv"], c.n["wv"])
            _sv = c.bump("vD")
            c.vector.scalar_tensor_tensor(
                out=x1t[:, pp * 512:(pp + 1) * 512],
                in0=b["wacc"][:, fo * 4096 + tb0: fo * 4096 + tb0 + 512],
                scalar=bo_sb[:, fo:fo + 1], in1=y1t[:, pp * 512:(pp + 1) * 512],
                op0=ALU.add, op1=ALU.add).then_inc(*_sv)
            c.sync.wait_ge(c.sem["vD"], c.n["vD"])
            stw = c.sdma(x1_hbm[fo * 128:(fo + 1) * 128, tb0:tb0 + 512],
                         x1t[:, pp * 512:(pp + 1) * 512])
            c.wait("vector", stw)
    c.drain_dmas()


def build_launch3(n_pairs=8, n_qt=32, stage=99, v2=False):
    nc = bass.Bass("TRN2", target_bir_lowering=False, debug=False)
    xnf = nc.dram_tensor("xnf", [F, T], BF16, kind="ExternalInput").ap()
    y1 = nc.dram_tensor("y1", [F, T], BF16, kind="ExternalInput").ap()
    WqT = nc.dram_tensor("WqT", [F, F], F32, kind="ExternalInput").ap()
    WkT = nc.dram_tensor("WkT", [F, F], F32, kind="ExternalInput").ap()
    WvT = nc.dram_tensor("WvT", [F, F], F32, kind="ExternalInput").ap()
    WoT = nc.dram_tensor("WoT", [F, F], F32, kind="ExternalInput").ap()
    bqkv_hbm = nc.dram_tensor("bqkv", [128, 24], F32, kind="ExternalInput").ap()
    bo_hbm = nc.dram_tensor("bo", [128, 8], F32, kind="ExternalInput").ap()
    ident = nc.dram_tensor("ident", [128, 128], BF16, kind="ExternalInput").ap()
    identf32_h = nc.dram_tensor("identf32", [128, 128], F32, kind="ExternalInput").ap()
    x1 = nc.dram_tensor("x1", [F, T], BF16, kind="ExternalOutput").ap()
    with contextlib.ExitStack() as stk:
        sb = lambda name, shape, dt: stk.enter_context(nc.sbuf_tensor(name, shape, dt))
        bufs = {
            "xnf": xnf, "y1": y1, "x1": x1,
            "WqT": WqT, "WkT": WkT, "WvT": WvT, "WoT": WoT,
            "bqkv_hbm": bqkv_hbm, "bo_hbm": bo_hbm,
            "bqkv": sb("bqkvs", [128, 24], F32),
            "bo_sb": sb("bos", [128, 8], F32),
            "xnc": sb("xnc", [128, 2 * 4096], BF16),
            "wq_sb": sb("wq_sb", [128, 1024], BF16),
            "wk_sb": sb("wk_sb", [128, 1024], BF16),
            "wv_sb": sb("wv_sb", [128, 1024], BF16),
            "wo_sb": sb("wo_sb", [128, 1024], BF16),
            "Qp": sb("Qp", [128, 4096], BF16),
            "Kp": sb("Kp", [128, 4096], BF16),
            "Vp": sb("Vp", [128, 4096], BF16),
            "vt": sb("vt", [128, 32 * 130], BF16),
            "eS": sb("eS", [128, 4096], BF16),
            "pA": sb("pA", [128, 4096], BF16),
            "pB": sb("pB", [128, 4096], BF16),
            "eS2": sb("eS2", [128, 4096], BF16),
            "pA2": sb("pA2", [128, 4096], BF16),
            "pB2": sb("pB2", [128, 4096], BF16),
            "t8": sb("t8", [128, 8], BF16),
            "zrow": sb("zrow", [1, 128], F32),
            "zbc": sb("zbc", [64, 128], F32),
            "zi": sb("zi", [128, 4], F32),
            "ones1": sb("ones1", [1, 64], F32),
            "e65": sb("e65", [65, 128], F32),
            "qn_sb": sb("qn_sb", [128, 64], BF16),
            "rzt": sb("rzt", [128, 1], F32),
            "lnt": sb("lnt", [128, 1], F32),
            "wacc": sb("wacc", [128, 8 * 4096], BF16),
            "wo_all": sb("wo_all", [128, 8 * 1024], BF16),
            "y1t": sb("y1t", [128, 2 * 512], BF16),
            "x1t": sb("x1t", [128, 2 * 512], BF16),
            "identsb": sb("identsb", [128, 128], BF16),
            "identf": sb("idf_sb", [128, 128], F32),
            "psS": stk.enter_context(nc.psum_tensor("psS", [128, 2048], F32)),
            "psT": stk.enter_context(nc.psum_tensor("psT", [128, 1024], BF16)),
            "psPV": stk.enter_context(nc.psum_tensor("psPV", [128, 512], F32)),
            "psB": stk.enter_context(nc.psum_tensor("psB", [128, 512], F32)),
            "psP": stk.enter_context(nc.psum_tensor("psP", [128, 1024], BF16)),
        }
        bufs["ident"] = ident

        def prog(c):
            idw = c.sdma(bufs["identsb"][:, :], ident[:, :])
            idw2 = c.sdma(bufs["identf"][:, :], identf32_h[:, :])
            c.wait("tensor", idw)
            c.wait("tensor", idw2)
            if v2:
                emit_l0_attn_v2(c, bufs, n_pairs=n_pairs)
            else:
                emit_l0_attn(c, bufs, n_pairs=n_pairs, n_qt=n_qt, stage=stage)

        run_program(nc, prog, ["tP", "vD", "sA", "wv", "scA", "plV", "tv", "ev2"])
    return nc



def emit_ln_fm(c, b, x_tiles, xn_tiles):
    psA = b["psA"]; psB = b["psB"]
    rows = b["rows"]
    sq = b["sq"]
    t1 = b["t1"]
    onesb = b["onesb"]
    ones1rF = b["ones1rF"]
    eps1 = b["eps1"]
    S = 512

    for tb in range(8):
        t0 = tb * S
        c.tensor.wait_ge(c.sem["vD"], c.n["vD"])
        c.tensor.wait_ge(c.sem["sA"], c.n["sA"])
        ins = None
        for ch in range(8):
            ins = c.tensor.matmul(psA[0:1, :], onesb[:, :], x_tiles[ch][:, t0:t0 + S],
                                  start=(ch == 0), stop=(ch == 7))
        ins.then_inc(*_b(c, "tP"))
        for ch in range(8):
            if ch > 0:
                c.vector.wait_ge(c.sem["tP"], c.n["tP"])
            _sv = c.bump("vD")
            c.vector.tensor_tensor(out=sq[:, :], in0=x_tiles[ch][:, t0:t0 + S],
                                   in1=x_tiles[ch][:, t0:t0 + S], op=ALU.mult).then_inc(*_sv)
            c.tensor.wait_ge(c.sem["vD"], c.n["vD"])
            ins = c.tensor.matmul(psB[0:1, :], onesb[:, :], sq[:, :],
                                  start=(ch == 0), stop=(ch == 7))
            ins.then_inc(*_b(c, "tP"))
        c.vector.wait_ge(c.sem["tP"], c.n["tP"])
        _sv = c.bump("vD")
        c.vector.tensor_scalar(out=rows[0:1, 2 * S:3 * S], in0=psA[0:1, :], scalar1=1.0 / F,
                               scalar2=None, op0=ALU.mult).then_inc(*_sv)
        c.vector.wait_ge(c.sem["vD"], c.n["vD"])
        _sv = c.bump("vD")
        c.vector.tensor_tensor(out=rows[0:1, 3 * S:4 * S], in0=rows[0:1, 2 * S:3 * S],
                               in1=rows[0:1, 2 * S:3 * S], op=ALU.mult).then_inc(*_sv)
        c.vector.wait_ge(c.sem["vD"], c.n["vD"])
        _sv = c.bump("vD")
        c.vector.scalar_tensor_tensor(out=rows[0:1, 1 * S:2 * S], in0=psB[0:1, :], scalar=1.0 / F,
                                      in1=rows[0:1, 3 * S:4 * S], op0=ALU.mult,
                                      op1=ALU.subtract).then_inc(*_sv)
        c.scalar.wait_ge(c.sem["vD"], c.n["vD"])
        _ss = c.bump("sA")
        c.scalar.activation(out=rows[0:1, 0:S], in_=rows[0:1, 1 * S:2 * S], func=AF.Ln,
                            bias=eps1[:, :], scale=1.0).then_inc(*_ss)
        c.scalar.wait_ge(c.sem["sA"], c.n["sA"])
        _ss = c.bump("sA")
        c.scalar.activation(out=rows[0:1, 3 * S:4 * S], in_=rows[0:1, 0:S], func=AF.Exp,
                            bias=0.0, scale=-0.5).then_inc(*_ss)
        c.vector.wait_ge(c.sem["sA"], c.n["sA"])
        _sv = c.bump("vD")
        c.vector.tensor_tensor(out=rows[0:1, 1 * S:2 * S], in0=rows[0:1, 2 * S:3 * S],
                               in1=rows[0:1, 3 * S:4 * S], op=ALU.mult).then_inc(*_sv)
        c.tensor.wait_ge(c.sem["vD"], c.n["vD"])
        _sp = c.bump("tP")
        c.tensor.matmul(psA[:, :], ones1rF[:, :], rows[0:1, 3 * S:4 * S],
                        start=True, stop=True)
        c.tensor.matmul(psB[:, :], ones1rF[:, :], rows[0:1, 1 * S:2 * S],
                        start=True, stop=True).then_inc(*_sp)
        c.vector.wait_ge(c.sem["tP"], c.n["tP"])
        for ch in range(8):
            c.vector.wait_ge(c.sem["vD"], c.n["vD"])
            _sv = c.bump("vD")
            c.vector.tensor_tensor(out=t1[:, :], in0=x_tiles[ch][:, t0:t0 + S],
                                   in1=psA[:, :], op=ALU.mult).then_inc(*_sv)
            c.vector.wait_ge(c.sem["vD"], c.n["vD"])
            _sv = c.bump("vD")
            c.vector.tensor_tensor(out=xn_tiles[ch][:, t0:t0 + S], in0=t1[:, :],
                                   in1=psB[:, :], op=ALU.subtract).then_inc(*_sv)



def emit_l1_pre(c, b, stage=99):
    x1_hbm = b["x1"]; xn1_hbm = b["xn1"]
    q_hbm = {"q": b["qf"], "k": b["kf"], "v": b["vf"]}
    Wq = {"q": b["Wq"], "k": b["Wk"], "v": b["Wv"]}
    convr = b["convr"]
    x1_t = [b["x1_sb"][:, ch * T:(ch + 1) * T] for ch in range(8)]
    xn_t = [b["xn_sb"][:, ch * T:(ch + 1) * T] for ch in range(8)]
    w_sb = b["w_sb"]
    ch_a = b["ch_a"]
    ch_b = b["ch_b"]
    lrs = b["lr_sb"]
    eps3 = b["eps3"]
    psP = b["psA"]
    psL = b["psB"]

    x1w = c.dma_group("sync", [(x1_t[ch], x1_hbm[ch * 128:(ch + 1) * 128, :])
                               for ch in range(8)])
    c.wait("vector", x1w)
    c.wait("tensor", x1w)
    emit_ln_fm(c, b, x1_t, xn_t)
    for ch in range(8):
        c.sync.wait_ge(c.sem["vD"], c.n["vD"])
        c.sdma(xn1_hbm[ch * 128:(ch + 1) * 128, :], xn_t[ch])

    if stage < 2:
        c.drain_dmas()
        return
    wlrw = c.gdma(b["wlr_sb"][:, :].rearrange("p (cc o) -> p cc o", o=32),
                  b["Wlr"][:, :].rearrange("(cc p) o -> p cc o", p=128))
    c.wait("tensor", wlrw)
    for tb in range(8):
        t0 = tb * 512
        c.tensor.wait_ge(c.sem["sA"], c.n["sA"])
        c.tensor.wait_ge(c.sem["vD"], c.n["vD"])
        ins = None
        for cc in range(8):
            ins = c.tensor.matmul(psL[0:32, :], b["wlr_sb"][:, cc * 32:(cc + 1) * 32],
                                  xn_t[cc][:, t0:t0 + 512], start=(cc == 0), stop=(cc == 7))
        ins.then_inc(*_b(c, "tP"))
        c.vector.wait_ge(c.sem["tP"], c.n["tP"])
        _sv = c.bump("vD")
        c.vector.tensor_scalar(out=b["lrt3"][:, 0:512], in0=psL[0:32, :], scalar1=eps3[:, :],
                               scalar2=0.0, op0=ALU.add, op1=ALU.max).then_inc(*_sv)
        c.scalar.wait_ge(c.sem["vD"], c.n["vD"])
        _ss = c.bump("sA")
        c.scalar.activation(out=b["lrt1"][:, 0:512],
                            in_=psL[0:32, :], func=AF.Abs, bias=eps3[:, :], scale=1.0).then_inc(*_ss)
        c.scalar.wait_ge(c.sem["sA"], c.n["sA"])
        _ss = c.bump("sA")
        c.scalar.activation(out=b["lrt2"][:, 0:512], in_=b["lrt1"][:, 0:512], func=AF.Exp,
                            bias=0.0, scale=-1.0).then_inc(*_ss)
        c.scalar.wait_ge(c.sem["sA"], c.n["sA"])
        _ss = c.bump("sA")
        c.scalar.activation(out=b["lrt1"][:, 0:512], in_=b["lrt2"][:, 0:512], func=AF.Ln,
                            bias=1.0, scale=1.0).then_inc(*_ss)
        c.vector.wait_ge(c.sem["sA"], c.n["sA"])
        c.vector.wait_ge(c.sem["vD"], c.n["vD"])
        _sv = c.bump("vD")
        c.vector.tensor_tensor(out=lrs[0:32, t0:t0 + 512], in0=b["lrt3"][:, 0:512],
                               in1=b["lrt1"][:, 0:512], op=ALU.add).then_inc(*_sv)
    c.sync.wait_ge(c.sem["vD"], c.n["vD"])
    c.sdma(b["lrf"][:, :], lrs[0:32, 0:T])

    if stage < 3:
        c.drain_dmas()
        return
    for pi, pk in enumerate(("q", "k", "v")):
        for oc in range(8):
            par = (pi * 8 + oc) % 2
            c.gpsimd.wait_ge(c.sem["tP"], c.n["tP"])
            ww = c.gdma(w_sb[:, par * 1024:(par + 1) * 1024].rearrange("p (cc o) -> p cc o", o=128),
                        Wq[pk][:, oc * 128:(oc + 1) * 128].rearrange("(cc p) o -> p cc o", p=128))
            c.wait("tensor", ww)
            for tb in range(8):
                t0 = tb * 512
                c.tensor.wait_ge(c.sem["vD"], c.n["vD"])
                ins = None
                for cc in range(8):
                    ins = c.tensor.matmul(psP[:, :], w_sb[:, par * 1024 + cc * 128:par * 1024 + (cc + 1) * 128],
                                          xn_t[cc][:, t0:t0 + 512], start=(cc == 0), stop=(cc == 7))
                ins.then_inc(*_b(c, "tP"))
                c.vector.wait_ge(c.sem["tP"], c.n["tP"])
                _sv = c.bump("vD")
                c.vector.tensor_copy(out=ch_a[:, t0:t0 + 512], in_=psP[:, :]).then_inc(*_sv)
            base = pi * 32 + oc * 4
            c.vector.wait_ge(c.sem["vD"], c.n["vD"])
            _sv = c.bump("vD")
            c.vector.scalar_tensor_tensor(out=ch_b[:, :], in0=ch_a[:, :],
                                          scalar=convr[:, base + 3:base + 4], in1=ch_a[:, :],
                                          op0=ALU.mult, op1=ALU.add).then_inc(*_sv)
            for j in (2, 1, 0):
                sh = 3 - j
                c.vector.wait_ge(c.sem["vD"], c.n["vD"])
                _sv = c.bump("vD")
                c.vector.scalar_tensor_tensor(out=ch_b[:, sh:T], in0=ch_a[:, 0:T - sh],
                                              scalar=convr[:, base + j:base + j + 1],
                                              in1=ch_b[:, sh:T],
                                              op0=ALU.mult, op1=ALU.add).then_inc(*_sv)
            c.sync.wait_ge(c.sem["vD"], c.n["vD"])
            stw = c.sdma(q_hbm[pk][oc * 128:(oc + 1) * 128, :], ch_b[:, :])
            c.wait("vector", stw)
    c.drain_dmas()


def build_launch4(stage=99):
    nc = bass.Bass("TRN2", target_bir_lowering=False, debug=False)
    x1 = nc.dram_tensor("x1", [F, T], BF16, kind="ExternalInput").ap()
    Wq_ = nc.dram_tensor("Wq", [F, F], F32, kind="ExternalInput").ap()
    Wk_ = nc.dram_tensor("Wk", [F, F], F32, kind="ExternalInput").ap()
    Wv_ = nc.dram_tensor("Wv", [F, F], F32, kind="ExternalInput").ap()
    Wlr = nc.dram_tensor("Wlr", [F, 32], F32, kind="ExternalInput").ap()
    convr_h = nc.dram_tensor("convr", [128, 96], F32, kind="ExternalInput").ap()
    xn1 = nc.dram_tensor("xn1", [F, T], BF16, kind="ExternalOutput").ap()
    qf = nc.dram_tensor("qf", [F, T], BF16, kind="ExternalOutput").ap()
    kf = nc.dram_tensor("kf", [F, T], BF16, kind="ExternalOutput").ap()
    vf = nc.dram_tensor("vf", [F, T], BF16, kind="ExternalOutput").ap()
    lrf = nc.dram_tensor("lrf", [32, T], F32, kind="ExternalOutput").ap()
    with contextlib.ExitStack() as stk:
        sb = lambda name, shape, dt: stk.enter_context(nc.sbuf_tensor(name, shape, dt))
        bufs = {
            "x1": x1, "xn1": xn1, "qf": qf, "kf": kf, "vf": vf, "lrf": lrf,
            "Wq": Wq_, "Wk": Wk_, "Wv": Wv_, "Wlr": Wlr,
            "x1_sb": sb("x1_sb", [128, 8 * T], BF16),
            "xn_sb": sb("xn_sb", [128, 8 * T], BF16),
            "w_sb": sb("w_sb", [128, 2 * 1024], BF16),
            "wlr_sb": sb("wlr_sb", [128, 256], BF16),
            "ch_a": sb("ch_a", [128, T], BF16),
            "ch_b": sb("ch_b", [128, T], BF16),
            "lr_sb": sb("lr_sb", [32, T + 32], F32),
            "lrt1": sb("lrt1", [32, 512], F32),
            "lrt2": sb("lrt2", [32, 512], F32),
            "lrt3": sb("lrt3", [32, 512], F32),
            "convr": sb("convrs", [128, 96], F32),
            "rows": sb("rows", [1, 4 * 512], F32),
            "sq": sb("sq", [128, 512], BF16),
            "t1": sb("t1", [128, 512], BF16),
            "onesb": sb("onesb", [128, 1], BF16),
            "ones1rF": sb("ones1rF", [1, 128], F32),
            "eps1": sb("eps1", [1, 1], F32),
            "eps3": sb("eps3", [32, 1], F32),
            "psA": stk.enter_context(nc.psum_tensor("psA", [128, 512], F32)),
            "psB": stk.enter_context(nc.psum_tensor("psB", [128, 512], F32)),
        }

        def prog(c):
            _sv = c.bump("vD")
            c.vector.memset(bufs["onesb"][:, :], 1.0)
            c.vector.memset(bufs["ones1rF"][:, :], 1.0)
            c.vector.memset(bufs["eps1"][:, :], 1e-5)
            c.vector.memset(bufs["eps3"][:, :], 0.001).then_inc(*_sv)
            cw = c.sdma(bufs["convr"][:, :], convr_h[:, :])
            c.wait("vector", cw)
            c.tensor.wait_ge(c.sem["vD"], c.n["vD"])
            emit_l1_pre(c, bufs, stage=stage)

        run_program(nc, prog, ["tP", "vD", "sA"])
    return nc



def emit_l1_ffn_final(c, b):
    xm = b["xm_sb"]
    S = 512
    w1_free = {0: 0, 1: 0}
    x1t_guard = [0]
    w2ws = []
    for p9 in range(9):
        w2ws.append(c.gdma(b["w2_sb"][:, p9 * 2048:(p9 + 1) * 2048].rearrange("p (mc o) -> p mc o", o=1024),
                           b["W2"][p9].rearrange("(mc p) o -> p mc o", p=128)))
    wo1w = c.gdma(b["wo1_sb"][:, :].rearrange("p (cc m) -> p cc m", m=256),
                  b["Wo1"][:, :].rearrange("(cc p) m -> p cc m", p=128))
    wo2w = c.gdma(b["wo2_sb"][:, :].rearrange("p (mc s) -> p mc s", s=9),
                  b["Wo2"][:, :].rearrange("(mc p) s -> p mc s", p=128))
    b2w = c.gdma(b["b2_sb"][0:9, :], b["b2"][:, :])
    wow = c.gdma(b["wo_sb2"][:, :].rearrange("p (h o) -> p h o", o=1024),
                 b["WoL1"][:, :].rearrange("(h p) o -> p h o", p=64))
    smallw = c.dma_group("sync", [
        (b["b1_sb"][:, :], b["b1r"][:, :]),
        (b["gb_sb"][:, :], b["gbr"][:, :]),
        (b["mixr_sb"][:, :], b["mixr"][:, :]),
        (b["bo1_sb"][:, :], b["bo1r"][:, :]),
        (b["identf"][:, :], b["identf32"][:, :]),
        (b["selbc_sb"][0:9, :], b["selbc"][:, :]),
    ])
    bo2w = c.gdma(b["bo2_sb"][0:1, :], b["bo2r"][:, :])
    c.wait("tensor", bo2w)
    c.wait("vector", smallw)
    c.wait("tensor", smallw)
    c.wait("scalar", smallw)

    for tb in range(8):
        t0 = tb * S
        hl = 1 if t0 > 0 else 0
        hr = 1 if t0 + S < T else 0
        ln_ = S + hl + hr
        c.sync.wait_ge(c.sem["zz"], c.n["zz"])
        c.sync.wait_ge(c.sem["vD"], c.n["vD"])
        xnw = c.dma_group("sync", [
            (b["xntb"][:, ch * 520:ch * 520 + ln_],
             b["xn1"][ch * 128:(ch + 1) * 128, t0 - hl:t0 - hl + ln_]) for ch in range(8)])
        c.wait("vector", xnw)
        c.wait("tensor", xnw)
        xn_t = [b["xntb"][:, ch * 520 + hl: ch * 520 + hl + S] for ch in range(8)]
        for ch in range(8):
            xs_full = b["xntb"][:, ch * 520: ch * 520 + ln_]
            xmc = xm[:, ch * S:(ch + 1) * S]
            c.vector.wait_ge(c.sem["tP"], c.n["tP"])
            _sv = c.bump("vD")
            c.vector.tensor_scalar(out=xmc, in0=xn_t[ch],
                                   scalar1=b["mixr_sb"][:, ch * 3 + 1:ch * 3 + 2], scalar2=None,
                                   op0=ALU.mult).then_inc(*_sv)
            j0 = 1 - hl if hl == 0 else 0
            j0 = 0 if hl == 1 else 1
            nl = S - j0
            c.vector.wait_ge(c.sem["vD"], c.n["vD"])
            _sv = c.bump("vD")
            c.vector.scalar_tensor_tensor(out=xmc[:, j0:j0 + nl],
                                          in0=b["xntb"][:, ch * 520 + hl + j0 - 1: ch * 520 + hl + j0 - 1 + nl],
                                          scalar=b["mixr_sb"][:, ch * 3 + 0:ch * 3 + 1],
                                          in1=xmc[:, j0:j0 + nl],
                                          op0=ALU.mult, op1=ALU.add).then_inc(*_sv)
            nr = S - (1 - hr)
            c.vector.wait_ge(c.sem["vD"], c.n["vD"])
            _sv = c.bump("vD")
            c.vector.scalar_tensor_tensor(out=xmc[:, 0:nr],
                                          in0=b["xntb"][:, ch * 520 + hl + 1: ch * 520 + hl + 1 + nr],
                                          scalar=b["mixr_sb"][:, ch * 3 + 2:ch * 3 + 3],
                                          in1=xmc[:, 0:nr],
                                          op0=ALU.mult, op1=ALU.add).then_inc(*_sv)
        if tb == 0:
            c.wait("tensor", wo1w)
            c.wait("tensor", wo2w)
            c.wait("tensor", b2w)
        for mc in range(2):
            c.tensor.wait_ge(c.sem["vD"], c.n["vD"])
            c.tensor.wait_ge(c.sem["sA"], c.n["sA"])
            ins = None
            for cc in range(8):
                ins = c.tensor.matmul(b["psA"][:, :],
                                      b["wo1_sb"][:, cc * 256 + mc * 128: cc * 256 + (mc + 1) * 128],
                                      xm[:, cc * S:(cc + 1) * S],
                                      start=(cc == 0), stop=(cc == 7))
            ins.then_inc(*_b(c, "tP"))
            c.scalar.wait_ge(c.sem["tP"], c.n["tP"])
            _ss = c.bump("sA")
            c.scalar.activation(out=b["th_sb"][:, mc * S:(mc + 1) * S], in_=b["psA"][:, :],
                                func=AF.Tanh, bias=b["bo1_sb"][:, mc:mc + 1], scale=1.0).then_inc(*_ss)
        c.tensor.wait_ge(c.sem["sA"], c.n["sA"])
        for ts4 in range(4):
            c.tensor.wait_ge(c.sem["vD"], c.n["vD"])
            for mc in range(2):
                c.tensor.matmul(b["psB"][0:128, 0:9],
                                b["th_sb"][:, mc * S + ts4 * 128: mc * S + (ts4 + 1) * 128],
                                b["wo2_sb"][:, mc * 9:(mc + 1) * 9],
                                start=(mc == 0), stop=False)
            ins = c.tensor.matmul(b["psB"][0:128, 0:9], b["ones1rB"][:, :],
                                  b["bo2_sb"][0:1, :], start=False, stop=True)
            ins.then_inc(*_b(c, "tP"))
            c.vector.wait_ge(c.sem["tP"], c.n["tP"])
            _sv = c.bump("vD")
            c.vector.reduce_max(out=b["mx_sb"][:, 0:1], in_=b["psB"][0:128, 0:9], axis=AX.X).then_inc(*_sv)
            c.vector.wait_ge(c.sem["vD"], c.n["vD"])
            _sv = c.bump("vD")
            c.vector.tensor_scalar(out=b["oh_sb"][:, 0:9], in0=b["psB"][0:128, 0:9],
                                   scalar1=b["mx_sb"][:, 0:1], scalar2=None,
                                   op0=ALU.is_ge).then_inc(*_sv)
            c.tensor.wait_ge(c.sem["vD"], c.n["vD"])
            _sp = c.bump("tP")
            c.tensor.transpose(b["psOH"][0:9, ts4 * 128:(ts4 + 1) * 128], b["oh_sb"][:, 0:9],
                               b["identsb"][:, :]).then_inc(*_sp)
        c.vector.wait_ge(c.sem["tP"], c.n["tP"])
        _sv = c.bump("vD")
        c.vector.tensor_copy(out=b["ohT_sb"][0:9, 0:S], in_=b["psOH"][0:9, 0:S]).then_inc(*_sv)
        for p9 in range(9):
            par = p9 % 2
            c.gpsimd.wait_ge(c.sem["tP"], w1_free[par])
            w1w = c.gdma(b["w1_sb"][:, par * 2048:(par + 1) * 2048].rearrange("p (cc m) -> p cc m", m=256),
                         b["W1"][p9].rearrange("(cc p) m -> p cc m", p=128))
            c.tensor.wait_ge(c.sem["vD"], c.n["vD"])
            _sp = c.bump("tP")
            c.tensor.matmul(b["psB"][:, :], b["selbc_sb"][0:9, p9 * 128:(p9 + 1) * 128],
                            b["ohT_sb"][0:9, 0:S],
                            start=True, stop=True).then_inc(*_sp)
            c.vector.wait_ge(c.sem["tP"], c.n["tP"])
            _sv = c.bump("vD")
            c.vector.tensor_copy(out=b["ohbc_sb"][:, :], in_=b["psB"][:, :]).then_inc(*_sv)
            c.wait("tensor", w1w)
            for mc in range(2):
                c.tensor.wait_ge(c.sem["vD"], c.n["vD"])
                c.tensor.wait_ge(c.sem["sA"], c.n["sA"])
                ins = None
                for cc in range(8):
                    ins = c.tensor.matmul(b["psA"][:, :],
                                          b["w1_sb"][:, par * 2048 + cc * 256 + mc * 128: par * 2048 + cc * 256 + (mc + 1) * 128],
                                          xm[:, cc * S:(cc + 1) * S],
                                          start=(cc == 0), stop=(cc == 7))
                ins.then_inc(*_b(c, "tP"))
                c.vector.wait_ge(c.sem["tP"], c.n["tP"])
                _sv = c.bump("vD")
                c.vector.scalar_tensor_tensor(
                    out=b["ht_sb"][:, p9 * 1024 + mc * S: p9 * 1024 + mc * S + S],
                    in0=b["psA"][:, :], scalar=b["b1_sb"][:, p9 * 2 + mc:p9 * 2 + mc + 1],
                    in1=b["ohbc_sb"][:, :], op0=ALU.add, op1=ALU.mult).then_inc(*_sv)
            w1_free[par] = c.n["tP"]
        for p_ in w2ws:
            if tb == 0:
                c.wait("tensor", p_)
        c.tensor.wait_ge(c.sem["vD"], c.n["vD"])
        for fo in range(8):
            c.tensor.wait_ge(c.sem["zz"], c.n["zz"])
            ins = None
            for p9 in range(9):
                for mc in range(2):
                    ins = c.tensor.matmul(b["psC"][:, :],
                                          b["w2_sb"][:, p9 * 2048 + mc * 1024 + fo * 128: p9 * 2048 + mc * 1024 + (fo + 1) * 128],
                                          b["ht_sb"][:, p9 * 1024 + mc * S: p9 * 1024 + mc * S + S],
                                          start=(p9 == 0 and mc == 0), stop=False)
            ins = c.tensor.matmul(b["psC"][:, :], b["b2_sb"][0:9, fo * 128:(fo + 1) * 128],
                                  b["ohT_sb"][0:9, 0:S], start=False, stop=True)
            ins.then_inc(*_b(c, "tP"))
            c.vector.wait_ge(c.sem["tP"], c.n["tP"])
            _sz = c.bump("zz")
            c.vector.tensor_tensor(out=b["zb_sb"][:, fo * S:(fo + 1) * S],
                                   in0=b["psC"][:, :], in1=xn_t[fo],
                                   op=ALU.add).then_inc(*_sz)
        zt = [b["zb_sb"][:, fo * S:(fo + 1) * S] for fo in range(8)]
        emit_ln_gb(c, b, zt, tb)
        if tb == 0:
            c.wait("tensor", wow)
        osw = c.gdma(b["osc_sb"][0:64, :].rearrange("p (h t2) -> p h t2", t2=S),
                     b["o_scan"][:, :].rearrange("p (h t2) -> p h t2", t2=T)[:, :, t0:t0 + S])
        c.wait("tensor", osw)
        for fo in range(8):
            c.sync.wait_ge(c.sem["vD"], x1t_guard[0])
            x1w = c.sdma(b["x1t_sb"][:, :], b["x1"][fo * 128:(fo + 1) * 128, t0:t0 + S])
            c.tensor.wait_ge(c.sem["fz"], c.n["fz"])
            ins = None
            for hh in range(16):
                ins = c.tensor.matmul(b["psF"][:, :],
                                      b["wo_sb2"][:, hh * 1024 + fo * 128: hh * 1024 + (fo + 1) * 128],
                                      b["osc_sb"][0:64, hh * S:(hh + 1) * S],
                                      start=(hh == 0), stop=(hh == 15))
            ins.then_inc(*_b(c, "tP"))
            c.vector.wait_ge(c.sem["tP"], c.n["tP"])
            c.wait("vector", x1w)
            _sv = c.bump("vD")
            c.vector.tensor_tensor(out=b["of1_sb"][:, :], in0=b["psF"][:, :],
                                   in1=b["x1t_sb"][:, :], op=ALU.add).then_inc(*_sv)
            x1t_guard[0] = c.n["vD"]
            c.vector.wait_ge(c.sem["vD"], c.n["vD"])
            _sf = c.bump("fz")
            c.vector.tensor_tensor(out=b["ofm_sb"][:, fo * S:(fo + 1) * S].bitcast(F32) if False else b["ofm_sb"][:, fo * S:(fo + 1) * S],
                                   in0=b["of1_sb"][:, :], in1=b["fb_sb"][:, fo * S:(fo + 1) * S],
                                   op=ALU.add).then_inc(*_sf)
        c.tensor.wait_ge(c.sem["fz"], c.n["fz"])
        for ts4 in range(4):
            for fo in range(8):
                c.tensor.wait_ge(c.sem["ev"], c.n["ev"])
                _sp = c.bump("tP")
                c.tensor.transpose(b["psTf"][:, fo * 64:(fo + 1) * 64][0:128, 0:64] if False else b["psTf"][:, (fo % 4) * 128:(fo % 4 + 1) * 128],
                                   b["ofm_sb"][:, fo * S + ts4 * 128: fo * S + (ts4 + 1) * 128],
                                   b["identf"][:, :]).then_inc(*_sp)
                if fo % 4 == 3:
                    c.vector.wait_ge(c.sem["tP"], c.n["tP"])
                    _se = c.bump("ev")
                    c.vector.tensor_copy(out=b["otok_sb"][:, (fo - 3) * 128:(fo + 1) * 128],
                                         in_=b["psTf"][:, :]).then_inc(*_se)
            c.sync.wait_ge(c.sem["ev"], c.n["ev"])
            ow = c.sdma(b["out"][t0 + ts4 * 128: t0 + (ts4 + 1) * 128, :], b["otok_sb"][:, :])
            c.wait("vector", ow)
    c.drain_dmas()


def emit_ln_gb(c, b, z_tiles, tb):
    S = 512
    rows = b["rows"]
    c.tensor.wait_ge(c.sem["lnv"], c.n["lnv"])
    c.tensor.wait_ge(c.sem["vD"], c.n["vD"])
    c.tensor.wait_ge(c.sem["zz"], c.n["zz"])
    c.vector.wait_ge(c.sem["zz"], c.n["zz"])
    ins = None
    for ch in range(8):
        ins = c.tensor.matmul(b["psA"][0:1, :], b["onesb"][:, :], z_tiles[ch],
                              start=(ch == 0), stop=(ch == 7))
    ins.then_inc(*_b(c, "tP"))
    for ch in range(8):
        if ch > 0:
            c.vector.wait_ge(c.sem["tP"], c.n["tP"])
        _sv = c.bump("vD")
        c.vector.tensor_tensor(out=b["sq"][:, :], in0=z_tiles[ch], in1=z_tiles[ch],
                               op=ALU.mult).then_inc(*_sv)
        c.tensor.wait_ge(c.sem["vD"], c.n["vD"])
        ins = c.tensor.matmul(b["psB"][0:1, :], b["onesb"][:, :], b["sq"][:, :],
                              start=(ch == 0), stop=(ch == 7))
        ins.then_inc(*_b(c, "tP"))
    c.vector.wait_ge(c.sem["tP"], c.n["tP"])
    _sv = c.bump("vD")
    c.vector.tensor_scalar(out=rows[0:1, 2 * S:3 * S], in0=b["psA"][0:1, :], scalar1=1.0 / F,
                           scalar2=None, op0=ALU.mult).then_inc(*_sv)
    c.vector.wait_ge(c.sem["vD"], c.n["vD"])
    _sv = c.bump("vD")
    c.vector.tensor_tensor(out=rows[0:1, 3 * S:4 * S], in0=rows[0:1, 2 * S:3 * S],
                           in1=rows[0:1, 2 * S:3 * S], op=ALU.mult).then_inc(*_sv)
    c.vector.wait_ge(c.sem["vD"], c.n["vD"])
    _sv = c.bump("vD")
    c.vector.scalar_tensor_tensor(out=rows[0:1, 1 * S:2 * S], in0=b["psB"][0:1, :], scalar=1.0 / F,
                                  in1=rows[0:1, 3 * S:4 * S], op0=ALU.mult,
                                  op1=ALU.subtract).then_inc(*_sv)
    c.scalar.wait_ge(c.sem["vD"], c.n["vD"])
    _ss = c.bump("sA")
    c.scalar.activation(out=rows[0:1, 0:S], in_=rows[0:1, 1 * S:2 * S], func=AF.Ln,
                        bias=b["eps1"][:, :], scale=1.0).then_inc(*_ss)
    c.scalar.wait_ge(c.sem["sA"], c.n["sA"])
    _ss = c.bump("sA")
    c.scalar.activation(out=rows[0:1, 3 * S:4 * S], in_=rows[0:1, 0:S], func=AF.Exp,
                        bias=0.0, scale=-0.5).then_inc(*_ss)
    c.vector.wait_ge(c.sem["sA"], c.n["sA"])
    _sv = c.bump("vD")
    c.vector.tensor_tensor(out=rows[0:1, 1 * S:2 * S], in0=rows[0:1, 2 * S:3 * S],
                           in1=rows[0:1, 3 * S:4 * S], op=ALU.mult).then_inc(*_sv)
    c.tensor.wait_ge(c.sem["vD"], c.n["vD"])
    _sp = c.bump("tP")
    c.tensor.matmul(b["psA"][:, :], b["ones1rF"][:, :], rows[0:1, 3 * S:4 * S],
                    start=True, stop=True)
    c.tensor.matmul(b["psB"][:, :], b["ones1rF"][:, :], rows[0:1, 1 * S:2 * S],
                    start=True, stop=True).then_inc(*_sp)
    c.vector.wait_ge(c.sem["tP"], c.n["tP"])
    for ch in range(8):
        c.vector.wait_ge(c.sem["vD"], c.n["vD"])
        c.vector.wait_ge(c.sem["lnv"], c.n["lnv"])
        _sv = c.bump("vD")
        c.vector.tensor_tensor(out=b["t1"][:, :], in0=z_tiles[ch],
                               in1=b["psA"][:, :], op=ALU.mult).then_inc(*_sv)
        c.vector.wait_ge(c.sem["vD"], c.n["vD"])
        _sv = c.bump("vD")
        c.vector.tensor_tensor(out=b["t1"][:, :], in0=b["t1"][:, :],
                               in1=b["psB"][:, :], op=ALU.subtract).then_inc(*_sv)
        c.vector.wait_ge(c.sem["vD"], c.n["vD"])
        _sl = c.bump("lnv")
        c.vector.tensor_scalar(out=b["fb_sb"][:, ch * S:(ch + 1) * S], in0=b["t1"][:, :],
                               scalar1=b["gb_sb"][:, ch:ch + 1],
                               scalar2=b["gb_sb"][:, 8 + ch:9 + ch],
                               op0=ALU.mult, op1=ALU.add).then_inc(*_sl)


def build_launch6():
    nc = bass.Bass("TRN2", target_bir_lowering=False, debug=False)
    x1 = nc.dram_tensor("x1", [F, T], BF16, kind="ExternalInput").ap()
    xn1 = nc.dram_tensor("xn1", [F, T], BF16, kind="ExternalInput").ap()
    o_scan = nc.dram_tensor("o_scan", [64, 16 * T], F32, kind="ExternalInput").ap()
    WoL1 = nc.dram_tensor("WoL1", [F, F], F32, kind="ExternalInput").ap()
    W1 = nc.dram_tensor("W1", [9, F, 256], F32, kind="ExternalInput").ap()
    W2 = nc.dram_tensor("W2", [9, 256, F], F32, kind="ExternalInput").ap()
    Wo1 = nc.dram_tensor("Wo1", [F, 256], F32, kind="ExternalInput").ap()
    Wo2 = nc.dram_tensor("Wo2", [256, 9], F32, kind="ExternalInput").ap()
    b1r = nc.dram_tensor("b1r", [128, 18], F32, kind="ExternalInput").ap()
    b2 = nc.dram_tensor("b2", [9, F], F32, kind="ExternalInput").ap()
    bo1r = nc.dram_tensor("bo1r", [128, 2], F32, kind="ExternalInput").ap()
    bo2r = nc.dram_tensor("bo2r", [1, 9], F32, kind="ExternalInput").ap()
    gbr = nc.dram_tensor("gbr", [128, 16], F32, kind="ExternalInput").ap()
    mixr = nc.dram_tensor("mixr", [128, 24], F32, kind="ExternalInput").ap()
    ident = nc.dram_tensor("ident", [128, 128], BF16, kind="ExternalInput").ap()
    identf32 = nc.dram_tensor("identf32", [128, 128], F32, kind="ExternalInput").ap()
    selbc = nc.dram_tensor("selbc", [9, 9 * 128], BF16, kind="ExternalInput").ap()
    out = nc.dram_tensor("out", [T, F], F32, kind="ExternalOutput").ap()
    with contextlib.ExitStack() as stk:
        sb = lambda name, shape, dt: stk.enter_context(nc.sbuf_tensor(name, shape, dt))
        bufs = {
            "x1": x1, "xn1": xn1, "o_scan": o_scan, "WoL1": WoL1,
            "W1": W1, "W2": W2, "Wo1": Wo1, "Wo2": Wo2,
            "b1r": b1r, "b2": b2, "bo1r": bo1r, "bo2r": bo2r, "gbr": gbr,
            "mixr": mixr, "identf32": identf32, "selbc": selbc, "out": out,
            "xntb": sb("xntb", [128, 8 * 520], BF16),
            "xm_sb": sb("xm_sb", [128, 8 * 512], BF16),
            "w1_sb": sb("w1_sb", [128, 2 * 2048], BF16),
            "w2_sb": sb("w2_sb", [128, 9 * 2048], BF16),
            "wo1_sb": sb("wo1_sb", [128, 2048], BF16),
            "wo2_sb": sb("wo2_sb", [128, 18], BF16),
            "wo_sb2": sb("wo_sb2", [64, 16 * 1024], BF16),
            "b1_sb": sb("b1_sb", [128, 18], F32),
            "b2_sb": sb("b2_sb", [9, 1024], BF16),
            "bo1_sb": sb("bo1_sb", [128, 2], F32),
            "bo2_sb": sb("bo2_sb", [1, 9], BF16),
            "gb_sb": sb("gb_sb", [128, 16], F32),
            "mixr_sb": sb("mixr_sb", [128, 24], F32),
            "th_sb": sb("th_sb", [128, 2 * 512], BF16),
            "mx_sb": sb("mx_sb", [128, 1], F32),
            "oh_sb": sb("oh_sb", [128, 9], BF16),
            "ohT_sb": sb("ohT_sb", [9, 512], BF16),
            "ohbc_sb": sb("ohbc_sb", [128, 512], BF16),
            "ht_sb": sb("ht_sb", [128, 9 * 1024], BF16),
            "zb_sb": sb("zb_sb", [128, 8 * 512], BF16),
            "fb_sb": sb("fb_sb", [128, 8 * 512], BF16),
            "osc_sb": sb("osc_sb", [64, 16 * 512], BF16),
            "x1t_sb": sb("x1t_sb", [128, 512], BF16),
            "of1_sb": sb("of1_sb", [128, 512], F32),
            "ofm_sb": sb("ofm_sb", [128, 8 * 512], F32),
            "otok_sb": sb("otok_sb", [128, 1024], F32),
            "rows": sb("rows", [1, 4 * 512], F32),
            "sq": sb("sq", [128, 512], BF16),
            "t1": sb("t1", [128, 512], F32),
            "onesb": sb("onesb", [128, 1], BF16),
            "ones1rF": sb("ones1rF", [1, 128], F32),
            "ones1rB": sb("ones1rB", [1, 128], BF16),
            "eps1": sb("eps1", [1, 1], F32),
            "identsb": sb("identsb", [128, 128], BF16),
            "identf": sb("idf_sb", [128, 128], F32),
            "identf": sb("identf", [128, 128], F32),
            "selbc_sb": sb("selbc_sb", [9, 9 * 128], BF16),
            "psA": stk.enter_context(nc.psum_tensor("psA", [128, 512], F32)),
            "psB": stk.enter_context(nc.psum_tensor("psB", [128, 512], F32)),
            "psC": stk.enter_context(nc.psum_tensor("psC", [128, 512], F32)),
            "psF": stk.enter_context(nc.psum_tensor("psF", [128, 512], F32)),
            "psOH": stk.enter_context(nc.psum_tensor("psOH", [9, 1024], BF16)),
            "psTf": stk.enter_context(nc.psum_tensor("psTf", [128, 512], F32)),
            
        }

        def prog(c):
            _sv = c.bump("vD")
            c.vector.memset(bufs["onesb"][:, :], 1.0)
            c.vector.memset(bufs["ones1rF"][:, :], 1.0)
            c.vector.memset(bufs["ones1rB"][:, :], 1.0)
            c.vector.memset(bufs["eps1"][:, :], 1e-5).then_inc(*_sv)
            idw = c.sdma(bufs["identsb"][:, :], ident[:, :])
            c.wait("tensor", idw)
            c.tensor.wait_ge(c.sem["vD"], c.n["vD"])
            emit_l1_ffn_final(c, bufs)

        run_program(nc, prog, ["tP", "vD", "sA", "zz", "fz", "ev", "lnv"])
    return nc



def apx(handle, n_part, dims, col_off=0, base_part=0):
    a = handle[:, :]
    pstep = a.ap[0][0]
    ap = [[pstep, n_part]] + [[s, n] for (s, n) in dims]
    return bass.AP(tensor=a.tensor, offset=base_part * pstep + col_off, ap=ap)


def emit_scan(c, b, n_seg=8):
    SEG = 512
    NCH = SEG // 16
    lrs = b["lr_sb"]
    winbd = b["winbd"]; woutbd = b["woutbd"]; woutd4 = b["woutd4"]
    eA = b["eA"]; lrt = b["lrt"]; sums = b["sums"]
    kqT = b["kqT"]; qkm = b["qkm"]
    ktok = b["ktok"]; vtok = b["vtok"]
    eC = b["eC"]; aN = b["aN"]; t128 = b["t128"]
    oTs = b["oTs"]
    psE = b["psE"]; psT4 = b["psT4"]; psQ = b["psQ"]; psO = b["psO"]
    psK = b["psK"]; psU = b["psU"]; psD = b["psD"]; psR = b["psR"]
    idf = b["identf"]
    ones4 = b["ones4"]; ones16 = b["ones16"]
    ones1x4 = b["ones1x4"]; ones1x16 = b["ones1x16"]
    maskT = b["maskT"]

    stw = c.dma_group("sync", [
        (winbd[:, :], b["winbd0"][:, :]),
        (woutbd[:, :], b["woutbd0"][:, :]),
        (woutd4[0:4, :], b["woutd40"][:, :]),
        (lrs[0:32, :], b["lrf"][:, :]),
        (maskT[0:16, :], b["maskT0"][:, :]),
    ])
    c.wait("tensor", stw)
    c.wait("vector", stw)

    segw = {}

    def emit_H(s_, tl_):
        par_ = tl_ % 2
        qof_ = (s_ % 2) * 4096
        col_ = tl_ * 16
        kch_ = [b["kseg"][:, qof_ + j * SEG + col_: qof_ + j * SEG + col_ + 16] for j in range(8)]
        vch_ = [b["vseg"][:, qof_ + j * SEG + col_: qof_ + j * SEG + col_ + 16] for j in range(8)]
        for (tok_, src_ch) in ((ktok, kch_), (vtok, vch_)):
            for half in range(2):
                c.tensor.wait_ge(c.sem["kv2"], c.n["kv2"])
                ins = None
                for j4 in range(4):
                    j = half * 4 + j4
                    ins = c.tensor.transpose(psK[0:16, j4 * 128:(j4 + 1) * 128],
                                             src_ch[j], idf[:, :])
                ins.then_inc(*_b(c, "tP"))
                c.scalar.wait_ge(c.sem["tP"], c.n["tP"])
                _se = c.bump("kv2")
                c.scalar.copy(out=tok_[0:16, par_ * 1024 + half * 512: par_ * 1024 + (half + 1) * 512],
                              in_=psK[0:16, 0:512]).then_inc(*_se)

    def prefetch(s):
        par = s % 2
        qof = par * 4096
        ws = []
        for (dst, src) in ((b["qseg"], b["qf"]), (b["kseg"], b["kf"]), (b["vseg"], b["vf"])):
            for ch in range(8):
                ws.append(c.gdma(dst[:, qof + ch * SEG:qof + (ch + 1) * SEG],
                                 src[ch * 128:(ch + 1) * 128, s * SEG:(s + 1) * SEG]))
        segw[s] = ws

    prefetch(0)
    if n_seg > 1:
        prefetch(1)

    for s in range(n_seg):
        par = s % 2
        qof = par * 4096
        for p_ in segw.pop(s):
            c.wait("tensor", p_)
            c.wait("vector", p_)
        oseg_guard = c.n["ov"]
        emit_H(s, 0)
        for tl in range(NCH):
            t = s * NCH + tl
            col = tl * 16
            kch = [b["kseg"][:, qof + j * SEG + col: qof + j * SEG + col + 16] for j in range(8)]
            qch = [b["qseg"][:, qof + j * SEG + col: qof + j * SEG + col + 16] for j in range(8)]
            vch = [b["vseg"][:, qof + j * SEG + col: qof + j * SEG + col + 16] for j in range(8)]
            vto = (tl % 2) * 1024
            c.tensor.wait_ge(c.sem["sA"], c.n["sA"])
            c.tensor.wait_ge(c.sem["vD"], c.n["vD"])
            for j in range(8):
                c.tensor.matmul(psE[0:16, 8 * j:8 * j + 8], kch[j], winbd[:, 8 * j:8 * j + 8],
                                start=True, stop=True)
            for j in range(8):
                c.tensor.matmul(psE[0:16, 64 + 8 * j:64 + 8 * j + 8], qch[j],
                                winbd[:, 8 * j:8 * j + 8], start=True, stop=True)
            ins = c.tensor.transpose(psE[0:16, 128:160], lrs[0:32, 16 * t:16 * t + 16],
                                     idf[0:32, 0:32])
            ins.then_inc(*_b(c, "tP"))
            c.vector.wait_ge(c.sem["tP"], c.n["tP"])
            _sv = c.bump("vD")
            c.vector.tensor_copy(out=lrt[0:16, :], in_=psE[0:16, 128:160]).then_inc(*_sv)
            _sv = c.bump("vD")
            c.vector.tensor_reduce(out=b["mxA"][0:16, :],
                                   in_=psE[0:16, 0:128].rearrange("p (g d) -> p g d", d=4),
                                   axis=AX.X, op=ALU.max).then_inc(*_sv)
            c.vector.wait_ge(c.sem["vD"], c.n["vD"])
            _sv = c.bump("vD")
            c.vector.tensor_tensor(out=b["eAr"][0:16, :], in0=psE[0:16, 0:128],
                                   in1=apx(b["mxA"], 16, [(1, 32), (0, 4)]),
                                   op=ALU.subtract).then_inc(*_sv)
            c.scalar.wait_ge(c.sem["vD"], c.n["vD"])
            _ss = c.bump("sA")
            c.scalar.activation(out=eA[0:16, :], in_=b["eAr"][0:16, :], func=AF.Exp,
                                bias=0.0, scale=1.0).then_inc(*_ss)
            c.vector.wait_ge(c.sem["sA"], c.n["sA"])
            _sv = c.bump("vD")
            c.vector.tensor_reduce(out=sums[0:16, 0:32],
                                   in_=eA[0:16, :].rearrange("p (g d) -> p g d", d=4),
                                   axis=AX.X, op=ALU.add).then_inc(*_sv)
            c.vector.wait_ge(c.sem["vD"], c.n["vD"])
            _sv = c.bump("vD")
            c.vector.reciprocal(out=sums[0:16, 0:32], in_=sums[0:16, 0:32]).then_inc(*_sv)
            c.vector.wait_ge(c.sem["vD"], c.n["vD"])
            _sv = c.bump("vD")
            c.vector.tensor_tensor(out=eA[0:16, :], in0=eA[0:16, :],
                                   in1=apx(sums, 16, [(1, 32), (0, 4)]),
                                   op=ALU.mult).then_inc(*_sv)
            c.tensor.wait_ge(c.sem["dv"], c.n["dv"])
            c.tensor.wait_ge(c.sem["vD"], c.n["vD"])
            ins = None
            for h in range(16):
                j, h2 = h // 2, h % 2
                ins = c.tensor.transpose(psT4[0:4, 16 * h:16 * h + 16],
                                         eA[0:16, 8 * j + 4 * h2:8 * j + 4 * h2 + 4],
                                         idf[0:16, 0:16])
            for h in range(16):
                j, h2 = h // 2, h % 2
                ins = c.tensor.transpose(psT4[0:4, 256 + 16 * h:256 + 16 * h + 16],
                                         eA[0:16, 64 + 8 * j + 4 * h2:64 + 8 * j + 4 * h2 + 4],
                                         idf[0:16, 0:16])
            ins.then_inc(*_b(c, "tP"))
            c.vector.wait_ge(c.sem["tP"], c.n["tP"])
            _sv = c.bump("dv")
            c.vector.tensor_copy(out=kqT[0:4, :], in_=psT4[0:4, :]).then_inc(*_sv)
            c.vector.wait_ge(c.sem["dv"], c.n["dv"])
            _sv = c.bump("vD")
            c.vector.tensor_tensor(out=eA[0:16, 0:64], in0=eA[0:16, 0:64],
                                   in1=apx(lrt, 16, [(4, 8), (2, 2), (0, 4)], col_off=1),
                                   op=ALU.mult).then_inc(*_sv)
            c.tensor.wait_ge(c.sem["dv"], c.n["dv"])
            c.tensor.wait_ge(c.sem["qv"], c.n["qv"])
            ins = None
            for h in range(16):
                ins = c.tensor.matmul(psQ[0:16, 16 * h:16 * h + 16],
                                      kqT[0:4, 16 * h:16 * h + 16],
                                      kqT[0:4, 256 + 16 * h:256 + 16 * h + 16],
                                      start=True, stop=True)
            ins.then_inc(*_b(c, "tP"))
            c.vector.wait_ge(c.sem["tP"], c.n["tP"])
            c.vector.wait_ge(c.sem["kv"], c.n["kv"])
            _sv = c.bump("vD")
            c.vector.tensor_tensor(out=qkm[0:16, :], in0=psQ[0:16, 0:256], in1=maskT[0:16, :],
                                   op=ALU.mult).then_inc(*_sv)
            c.vector.wait_ge(c.sem["vD"], c.n["vD"])
            _sv = c.bump("qv")
            c.vector.tensor_tensor(out=qkm[0:16, :], in0=qkm[0:16, :],
                                   in1=apx(lrt, 16, [(2, 16), (0, 16)], col_off=1),
                                   op=ALU.mult).then_inc(*_sv)
            c.tensor.wait_ge(c.sem["ov"], c.n["ov"])
            c.tensor.wait_ge(c.sem["qv"], c.n["qv"])
            ins = None
            for h in range(16):
                c.tensor.matmul(psO[0:64, 16 * h:16 * h + 16],
                                woutd4[0:4, 64 * h:64 * h + 64],
                                kqT[0:4, 256 + 16 * h:256 + 16 * h + 16],
                                start=True, stop=False)
                ins = c.tensor.matmul(psO[0:64, 16 * h:16 * h + 16],
                                      vtok[0:16, vto + 64 * h:vto + 64 * h + 64],
                                      qkm[0:16, 16 * h:16 * h + 16],
                                      start=False, stop=True)
            ins.then_inc(*_b(c, "tP"))
            c.vector.wait_ge(c.sem["tP"], c.n["tP"])
            _sv = c.bump("ov")
            c.vector.tensor_copy(out=apx(oTs, 64, [(512, 16), (1, 16)], col_off=col),
                                 in_=psO[0:64, 0:256]).then_inc(*_sv)
            c.tensor.wait_ge(c.sem["uv"], c.n["uv"])
            ins = None
            for h in range(16):
                j, h2 = h // 2, h % 2
                ins = c.tensor.matmul(psU[64 * h2:64 * h2 + 64, 4 * j:4 * j + 4],
                                      vtok[0:16, vto + 64 * h:vto + 64 * h + 64],
                                      eA[0:16, 8 * j + 4 * h2:8 * j + 4 * h2 + 4],
                                      start=True, stop=True)
            ins.then_inc(*_b(c, "tP"))
            for half in range(2):
                c.tensor.wait_ge(c.sem["uv"], c.n["uv"])
                ins = None
                for h in range(half * 8, half * 8 + 8):
                    j, h2 = h // 2, h % 2
                    ins = c.tensor.matmul(psD[0:4, (h - half * 8) * 64:(h - half * 8) * 64 + 64],
                                          eA[0:16, 8 * j + 4 * h2:8 * j + 4 * h2 + 4],
                                          vtok[0:16, vto + 64 * h:vto + 64 * h + 64],
                                          start=True, stop=True)
                ins.then_inc(*_b(c, "tP"))
                c.vector.wait_ge(c.sem["tP"], c.n["tP"])
                _sv = c.bump("uv")
                c.vector.tensor_tensor(out=woutd4[0:4, half * 512:(half + 1) * 512],
                                       in0=woutd4[0:4, half * 512:(half + 1) * 512],
                                       in1=psD[0:4, 0:512], op=ALU.add).then_inc(*_sv)
            c.vector.wait_ge(c.sem["tP"], c.n["tP"])
            for h2 in range(2):
                _sv = c.bump("uv")
                c.vector.tensor_tensor(
                    out=apx(woutbd, 64, [(8, 8), (1, 4)], col_off=4 * h2, base_part=64 * h2),
                    in0=apx(woutbd, 64, [(8, 8), (1, 4)], col_off=4 * h2, base_part=64 * h2),
                    in1=apx(psU, 64, [(4, 8), (1, 4)], base_part=64 * h2),
                    op=ALU.add).then_inc(*_sv)
            c.vector.wait_ge(c.sem["rv"], c.n["rv"])
            _sv = c.bump("vD")
            c.vector.tensor_copy(out=b["lrrow"][0:1, :],
                                 in_=apx(lrt, 1, [(-1, 2), (2, 16), (0, 4)], col_off=1)).then_inc(*_sv)
            c.tensor.wait_ge(c.sem["uv"], c.n["uv"])
            c.tensor.wait_ge(c.sem["vD"], c.n["vD"])
            _sp = c.bump("tP")
            c.tensor.matmul(psU[:, 16:17], b["lrrow"][0:1, :],
                            b["ones1x1"][0:1, 0:1], start=True, stop=True).then_inc(*_sp)
            c.vector.wait_ge(c.sem["tP"], c.n["tP"])
            _sv = c.bump("rv")
            c.vector.tensor_copy(out=b["lrN"][0:128, 0:1], in_=psU[:, 16:17]).then_inc(*_sv)
            for it in range(2):
                c.tensor.wait_ge(c.sem["sA"], c.n["sA"])
                c.tensor.wait_ge(c.sem["uv"], c.n["uv"])
                ins = None
                for h2sel in range(2):
                    for hp in range(32):
                        h = hp % 16
                        j, h2 = h // 2, h % 2
                        if h2 != h2sel:
                            continue
                        wsrc = winbd if hp < 16 else woutbd
                        ksrc = kch[j] if hp < 16 else vch[j]
                        ins = c.tensor.matmul(psE[0:16, 4 * hp:4 * hp + 4],
                                              ksrc[64 * h2:64 * h2 + 64, :],
                                              wsrc[64 * h2:64 * h2 + 64, 8 * j + 4 * h2:8 * j + 4 * h2 + 4],
                                              start=True, stop=True)
                    if h2sel == 0:
                        ins.then_inc(*_b(c, "tP"))
                        c.tensor.wait_ge(c.sem["tP"], c.n["tP"])
                ins.then_inc(*_b(c, "tP"))
                att_tp = c.n["tP"]
                if it == 0 and tl + 1 < NCH:
                    emit_H(s, tl + 1)
                c.vector.wait_ge(c.sem["tP"], att_tp)
                _sv = c.bump("cv")
                c.vector.tensor_copy(out=b["sC"][0:16, :], in_=psE[0:16, 0:128]).then_inc(*_sv)
                c.tensor.wait_ge(c.sem["cv"], c.n["cv"])
                c.tensor.wait_ge(c.sem["uv"], c.n["uv"])
                c.tensor.wait_ge(c.sem["rv"], c.n["rv"])
                _sp = c.bump("tP")
                c.tensor.transpose(psU[:, 0:16], b["sC"][0:16, :],
                                   idf[0:16, 0:16]).then_inc(*_sp)
                c.vector.wait_ge(c.sem["tP"], c.n["tP"])
                _sv = c.bump("cv")
                c.vector.tensor_reduce(out=b["nmx"][0:128, 0:1], in_=psU[:, 0:16],
                                       axis=AX.X, op=ALU.max, negate=True).then_inc(*_sv)
                c.vector.wait_ge(c.sem["cv"], c.n["cv"])
                _sv = c.bump("cv")
                c.vector.tensor_scalar(out=b["nmx"][0:128, 0:1], in0=b["nmx"][0:128, 0:1],
                                       scalar1=0.125, scalar2=None, op0=ALU.mult).then_inc(*_sv)
                c.scalar.wait_ge(c.sem["cv"], c.n["cv"])
                _ss = c.bump("sA")
                c.scalar.activation(out=b["eCT"][0:128, :], in_=psU[:, 0:16], func=AF.Exp,
                                    bias=b["nmx"][0:128, 0:1], scale=0.125).then_inc(*_ss)
                c.vector.wait_ge(c.sem["sA"], c.n["sA"])
                _sv = c.bump("cv")
                c.vector.tensor_reduce(out=b["sCs"][0:128, 0:1], in_=b["eCT"][0:128, :],
                                       axis=AX.X, op=ALU.add).then_inc(*_sv)
                c.vector.wait_ge(c.sem["cv"], c.n["cv"])
                _sv = c.bump("cv")
                c.vector.reciprocal(out=b["sCs"][0:128, 0:1], in_=b["sCs"][0:128, 0:1]).then_inc(*_sv)
                c.vector.wait_ge(c.sem["cv"], c.n["cv"])
                c.vector.wait_ge(c.sem["rv"], c.n["rv"])
                _sv = c.bump("cv")
                c.vector.tensor_scalar(out=b["eCT"][0:128, :], in0=b["eCT"][0:128, :],
                                       scalar1=b["sCs"][0:128, 0:1], scalar2=b["lrN"][0:128, 0:1],
                                       op0=ALU.mult, op1=ALU.mult).then_inc(*_sv)
                c.tensor.wait_ge(c.sem["cv"], c.n["cv"])
                c.tensor.wait_ge(c.sem["sA"], c.n["sA"])
                _sp = c.bump("tP")
                c.tensor.transpose(psE[0:16, 0:128], b["eCT"][0:128, :],
                                   idf[:, :]).then_inc(*_sp)
                c.vector.wait_ge(c.sem["tP"], c.n["tP"])
                _sv = c.bump("rv")
                c.vector.tensor_copy(out=aN[0:16, :], in_=psE[0:16, 0:128]).then_inc(*_sv)
                c.vector.wait_ge(c.sem["rv"], c.n["rv"])
                c.tensor.wait_ge(c.sem["rv"], c.n["rv"])
                c.tensor.wait_ge(c.sem["uv"], c.n["uv"])
                ins = None
                for hp in range(16):
                    h = hp
                    j, h2 = h // 2, h % 2
                    ins = c.tensor.matmul(psU[64 * h2:64 * h2 + 64, 4 * j:4 * j + 4],
                                          vtok[0:16, vto + 64 * h:vto + 64 * h + 64],
                                          aN[0:16, 4 * hp:4 * hp + 4],
                                          start=True, stop=True)
                ins.then_inc(*_b(c, "tP"))
                c.vector.wait_ge(c.sem["tP"], c.n["tP"])
                for h2 in range(2):
                    _sv = c.bump("uv")
                    c.vector.tensor_tensor(
                        out=apx(woutbd, 64, [(8, 8), (1, 4)], col_off=4 * h2, base_part=64 * h2),
                        in0=apx(woutbd, 64, [(8, 8), (1, 4)], col_off=4 * h2, base_part=64 * h2),
                        in1=apx(psU, 64, [(4, 8), (1, 4)], base_part=64 * h2),
                        op=ALU.add).then_inc(*_sv)
                c.tensor.wait_ge(c.sem["uv"], c.n["uv"])
                ins = None
                for hp in range(16, 32):
                    h = hp - 16
                    j, h2 = h // 2, h % 2
                    ins = c.tensor.matmul(psU[64 * h2:64 * h2 + 64, 4 * j:4 * j + 4],
                                          ktok[0:16, vto + 64 * h:vto + 64 * h + 64],
                                          aN[0:16, 4 * hp:4 * hp + 4],
                                          start=True, stop=True)
                ins.then_inc(*_b(c, "tP"))
                c.vector.wait_ge(c.sem["tP"], c.n["tP"])
                for h2 in range(2):
                    _sv = c.bump("uv")
                    c.vector.tensor_tensor(
                        out=apx(winbd, 64, [(8, 8), (1, 4)], col_off=4 * h2, base_part=64 * h2),
                        in0=apx(winbd, 64, [(8, 8), (1, 4)], col_off=4 * h2, base_part=64 * h2),
                        in1=apx(psU, 64, [(4, 8), (1, 4)], base_part=64 * h2),
                        op=ALU.add).then_inc(*_sv)
                for half in range(2):
                    c.tensor.wait_ge(c.sem["uv"], c.n["uv"])
                    ins = None
                    for h in range(half * 8, half * 8 + 8):
                        ins = c.tensor.matmul(psD[0:4, (h - half * 8) * 64:(h - half * 8) * 64 + 64],
                                              aN[0:16, 4 * h:4 * h + 4],
                                              vtok[0:16, vto + 64 * h:vto + 64 * h + 64],
                                              start=True, stop=True)
                    ins.then_inc(*_b(c, "tP"))
                    c.vector.wait_ge(c.sem["tP"], c.n["tP"])
                    _sv = c.bump("uv")
                    c.vector.tensor_tensor(out=woutd4[0:4, half * 512:(half + 1) * 512],
                                           in0=woutd4[0:4, half * 512:(half + 1) * 512],
                                           in1=psD[0:4, 0:512], op=ALU.add).then_inc(*_sv)
        c.sync.wait_ge(c.sem["ov"], c.n["ov"])
        osw = c.sdma(b["o_scan"][:, :].rearrange("p (h t2) -> p h t2", t2=T)[:, :, s * SEG:(s + 1) * SEG],
                     oTs[0:64, :].rearrange("p (h t2) -> p h t2", t2=SEG))
        c.wait("vector", osw)
        if s + 2 < n_seg:
            c.gpsimd.wait_ge(c.sem["tP"], c.n["tP"])
            prefetch(s + 2)
    c.drain_dmas()


def build_launch5(n_seg=8):
    nc = bass.Bass("TRN2", target_bir_lowering=False, debug=False)
    qf = nc.dram_tensor("qf", [F, T], BF16, kind="ExternalInput").ap()
    kf = nc.dram_tensor("kf", [F, T], BF16, kind="ExternalInput").ap()
    vf = nc.dram_tensor("vf", [F, T], BF16, kind="ExternalInput").ap()
    lrf = nc.dram_tensor("lrf", [32, T], F32, kind="ExternalInput").ap()
    winbd0 = nc.dram_tensor("winbd0", [128, 64], F32, kind="ExternalInput").ap()
    woutbd0 = nc.dram_tensor("woutbd0", [128, 64], F32, kind="ExternalInput").ap()
    woutd40 = nc.dram_tensor("woutd40", [4, 1024], F32, kind="ExternalInput").ap()
    maskT0 = nc.dram_tensor("maskT0", [16, 256], F32, kind="ExternalInput").ap()
    identf32 = nc.dram_tensor("identf32", [128, 128], F32, kind="ExternalInput").ap()
    o_scan = nc.dram_tensor("o_scan", [64, 16 * T], F32, kind="ExternalOutput").ap()
    with contextlib.ExitStack() as stk:
        sb = lambda name, shape, dt: stk.enter_context(nc.sbuf_tensor(name, shape, dt))
        bufs = {
            "qf": qf, "kf": kf, "vf": vf, "lrf": lrf,
            "winbd0": winbd0, "woutbd0": woutbd0, "woutd40": woutd40,
            "maskT0": maskT0, "o_scan": o_scan,
            "qseg": sb("qseg", [128, 2 * 4096], F32),
            "kseg": sb("kseg", [128, 2 * 4096], F32),
            "vseg": sb("vseg", [128, 2 * 4096], F32),
            "lr_sb": sb("lr_sb", [32, T], F32),
            "winbd": sb("winbd", [128, 64], F32),
            "woutbd": sb("woutbd", [128, 64], F32),
            "woutd4": sb("woutd4", [4, 1024], F32),
            "eA": sb("eA", [16, 128], F32),
            "eAr": sb("eAr", [16, 128], F32),
            "mxA": sb("mxA", [16, 32], F32),
            "mxr": sb("mxr", [16, 1], F32),
            "gneg": sb("gneg", [1, 1], F32),
            "sC": sb("sC", [16, 128], F32),
            "nmx": sb("nmx", [128, 1], F32),
            "eCT": sb("eCT", [128, 16], F32),
            "sCs": sb("sCs", [128, 1], F32),
            "lrN": sb("lrN", [128, 1], F32),
            "ones1x1": sb("ones1x1", [1, 1], F32),
            "lrrow": sb("lrrow", [1, 128], F32),
            "negb": sb("negb", [16, 1], F32),
            "lrt": sb("lrt", [16, 32], F32),
            "sums": sb("sums", [16, 32], F32),
            "kqT": sb("kqT", [4, 512], F32),
            "recT": sb("recT", [1, 512], F32),
            "qkm": sb("qkm", [16, 256], F32),
            "ktok": sb("ktok", [16, 2 * 1024], F32),
            "vtok": sb("vtok", [16, 2 * 1024], F32),
            "eC": sb("eC", [16, 128], F32),
            "recC": sb("recC", [1, 128], F32),
            "aN": sb("aN", [16, 128], F32),
            "t128": sb("t128", [16, 128], F32),
            "oTs": sb("oTs", [64, 16 * 512], F32),
            "maskT": sb("maskT", [16, 256], F32),
            "ones4": sb("ones4", [4, 1], F32),
            "ones16": sb("ones16", [16, 1], F32),
            "ones1x4": sb("ones1x4", [1, 4], F32),
            "ones1x16": sb("ones1x16", [1, 16], F32),
            "identf": sb("identf", [128, 128], F32),
            "psE": stk.enter_context(nc.psum_tensor("psE", [16, 512], F32)),
            "psT4": stk.enter_context(nc.psum_tensor("psT4", [4, 512], F32)),
            "psQ": stk.enter_context(nc.psum_tensor("psQ", [16, 512], F32)),
            "psO": stk.enter_context(nc.psum_tensor("psO", [64, 512], F32)),
            "psK": stk.enter_context(nc.psum_tensor("psK", [16, 512], F32)),
            "psU": stk.enter_context(nc.psum_tensor("psU", [128, 512], F32)),
            "psD": stk.enter_context(nc.psum_tensor("psD", [4, 512], F32)),
            "psR": stk.enter_context(nc.psum_tensor("psR", [16, 512], F32)),
        }

        def prog(c):
            _sv = c.bump("vD")
            c.vector.memset(bufs["ones4"][:, :], 1.0)
            c.vector.memset(bufs["ones16"][:, :], 1.0)
            c.vector.memset(bufs["ones1x4"][:, :], 1.0)
            c.vector.memset(bufs["ones1x16"][:, :], 1.0)
            c.vector.memset(bufs["ones1x1"][:, :], 1.0).then_inc(*_sv)
            idw = c.sdma(bufs["identf"][:, :], identf32[:, :])
            c.wait("tensor", idw)
            c.tensor.wait_ge(c.sem["vD"], c.n["vD"])
            emit_scan(c, bufs, n_seg=n_seg)

        run_program(nc, prog, ["tP", "vD", "sA", "dv", "rv", "kv", "qv", "ov", "uv", "cv", "kv2"])
    return nc



def emit_l0_attn_v2(c, b, n_pairs=8):
    xnf_hbm = b["xnf"]; y1_hbm = b["y1"]; x1_hbm = b["x1"]
    WqT = b["WqT"]; WkT = b["WkT"]; WvT = b["WvT"]; WoT = b["WoT"]
    bqkv = b["bqkv"]; bo_sb = b["bo_sb"]
    xnc = b["xnc"]; wq_sb = b["wq_sb"]; wk_sb = b["wk_sb"]; wv_sb = b["wv_sb"]
    wo_sb = b["wo_sb"]
    Qp = b["Qp"]; Kp = b["Kp"]; Vp = b["Vp"]; vt = b["vt"]
    eS = [b["eS"], b["eS2"]]
    pA = [b["pA"], b["pA2"]]
    pB = [b["pB"], b["pB2"]]
    t8 = b["t8"]
    zrow = b["zrow"]; zbc = b["zbc"]; ones1 = b["ones1"]
    psS = b["psS"]; psT = b["psT"]; psPV = b["psPV"]; psB = b["psB"]; psP = b["psP"]

    _sv = c.bump("vD")
    c.vector.memset(vt[:, :].rearrange("p (kc h d) -> p kc h d", h=2, d=65)[:, :, :, 64:65], 1.0)
    c.vector.memset(ones1[:, :], 1.0).then_inc(*_sv)
    bw = c.dma_group("sync", [(bqkv[:, :], b["bqkv_hbm"][:, :]),
                              (bo_sb[:, :], b["bo_hbm"][:, :])])
    c.wait("vector", bw)

    xnc_free_at = {0: 0, 1: 0}
    for p in range(n_pairs):
        c.gpsimd.wait_ge(c.sem["tP"], c.n["tP"])
        wqw = c.gdma(wq_sb[:, :].rearrange("p (cc o) -> p cc o", o=128),
                     WqT[:, p * 128:(p + 1) * 128].rearrange("(cc p2) o -> p2 cc o", p2=128))
        wkw = c.gdma(wk_sb[:, :].rearrange("p (cc o) -> p cc o", o=128),
                     WkT[:, p * 128:(p + 1) * 128].rearrange("(cc p2) o -> p2 cc o", p2=128))
        wvw = c.gdma(wv_sb[:, :].rearrange("p (cc o) -> p cc o", o=128),
                     WvT[:, p * 128:(p + 1) * 128].rearrange("(cc p2) o -> p2 cc o", p2=128))
        wow = c.gdma(wo_sb[:, :], WoT[p * 128:(p + 1) * 128, :])
        for pair_ in (wqw, wkw, wvw):
            c.wait("tensor", pair_)
        vd_free = [0, 0, 0, 0]
        for tb in range(8):
            tb0 = tb * 512
            parx = tb % 2
            c.sync.wait_ge(c.sem["tP"], xnc_free_at[parx])
            xw = c.dma_group("sync", [
                (xnc[:, parx * 4096 + cc * 512: parx * 4096 + (cc + 1) * 512],
                 xnf_hbm[cc * 128:(cc + 1) * 128, tb0:tb0 + 512]) for cc in range(8)])
            c.wait("tensor", xw)
            for jj, (dst, wsb, bcol, scale) in enumerate(
                    ((Qp, wq_sb, 0, 0.125), (Kp, wk_sb, 8, None), (Vp, wv_sb, 16, None))):
                jb = (tb * 3 + jj) % 4
                psJ = psS[:, jb * 512:(jb + 1) * 512]
                c.tensor.wait_ge(c.sem["vD"], vd_free[jb])
                ins = None
                for cc in range(8):
                    ins = c.tensor.matmul(psJ,
                                          wsb[:, cc * 128:(cc + 1) * 128],
                                          xnc[:, parx * 4096 + cc * 512: parx * 4096 + (cc + 1) * 512],
                                          start=(cc == 0), stop=(cc == 7))
                ins.then_inc(*_b(c, "tP"))
                c.vector.wait_ge(c.sem["tP"], c.n["tP"])
                _sv = c.bump("vD")
                if scale is None:
                    c.vector.tensor_scalar(out=dst[:, tb0:tb0 + 512], in0=psJ,
                                           scalar1=bqkv[:, bcol + p:bcol + p + 1], scalar2=None,
                                           op0=ALU.add).then_inc(*_sv)
                else:
                    c.vector.tensor_scalar(out=dst[:, tb0:tb0 + 512], in0=psJ,
                                           scalar1=bqkv[:, bcol + p:bcol + p + 1], scalar2=scale,
                                           op0=ALU.add, op1=ALU.mult).then_inc(*_sv)
                vd_free[jb] = c.n["vD"]
            xnc_free_at[parx] = c.n["tP"]
        c.tensor.wait_ge(c.sem["vD"], c.n["vD"])
        vt4 = vt[:, :].rearrange("p (kc h d) -> p kc h d", h=2, d=65)
        for g4 in range(4):
            kc0 = g4 * 8
            c.tensor.wait_ge(c.sem["ev2"], c.n["ev2"])
            for hh2 in range(2):
                if hh2 == 1:
                    c.tensor.wait_ge(c.sem["tP"], c.n["tP"])
                ins = None
                for kci in range(8):
                    ins = c.tensor.transpose(
                        psT[:, hh2 * 512 + kci * 64: hh2 * 512 + (kci + 1) * 64],
                        Vp[hh2 * 64:(hh2 + 1) * 64, (kc0 + kci) * 128:(kc0 + kci + 1) * 128],
                        b["identsb"][hh2 * 64:(hh2 + 1) * 64, hh2 * 64:(hh2 + 1) * 64])
                    ins.then_inc(*_b(c, "tP"))
            c.scalar.wait_ge(c.sem["tP"], c.n["tP"])
            for hh2 in range(2):
                _se = c.bump("ev2")
                c.scalar.copy(out=vt4[:, kc0:kc0 + 8, hh2, 0:64],
                              in_=psT[:, hh2 * 512:(hh2 + 1) * 512]
                              .rearrange("p (kc d) -> p kc d", d=64)).then_inc(*_se)
        c.tensor.wait_ge(c.sem["ev2"], c.n["ev2"])

        tiles = [(hh, qt) for hh in range(2) for qt in range(32)]
        ntile = len(tiles)
        base_scA = c.n["scA"]
        base_plV = c.n["plV"]
        base_tv = c.n["tv"]

        def S_phase(k):
            hh, qt = tiles[k]
            hbase = hh * 64
            q0 = qt * 128
            par = k % 2
            for half in range(2):
                k0 = half * 2048
                c.tensor.wait_ge(c.sem["scA"], base_scA + 2 * k + half)
                ins = None
                for kk in range(4):
                    ins = c.tensor.matmul(psS[:, kk * 512:(kk + 1) * 512],
                                          Qp[hbase:hbase + 64, q0:q0 + 128],
                                          Kp[hbase:hbase + 64, k0 + kk * 512: k0 + (kk + 1) * 512],
                                          start=True, stop=True)
                ins.then_inc(*_b(c, "tP"))
                if k >= 2:
                    c.scalar.wait_ge(c.sem["plV"], base_plV + (k - 2) + 1)
                c.scalar.wait_ge(c.sem["tP"], c.n["tP"])
                _ss = c.bump("scA")
                c.scalar.activation(out=eS[par][:, k0:k0 + 2048], in_=psS[:, :],
                                    func=AF.Exp, bias=0.0, scale=1.0).then_inc(*_ss)

        def L_phase(k):
            par = k % 2
            c.vector.wait_ge(c.sem["scA"], base_scA + 2 * (k + 1))
            if k >= 2:
                c.vector.wait_ge(c.sem["tv"], base_tv + (k - 2) + 1)
            src = eS[par]
            dst = pA[par]
            for rnd in range(3):
                _sv = c.bump("vD")
                c.vector.max(out=t8[:, :], in_=src[:, :]).then_inc(*_sv)
                c.vector.wait_ge(c.sem["vD"], c.n["vD"])
                _sv = c.bump("vD")
                c.vector.match_replace(out=dst[:, :], in_to_replace=t8[:, :],
                                       in_values=src[:, :], imm_value=0.0).then_inc(*_sv)
                c.vector.wait_ge(c.sem["vD"], c.n["vD"])
                src, dst = dst, (pB[par] if dst is pA[par] else pA[par])
            _sv = c.bump("vD")
            c.vector.max(out=t8[:, :], in_=src[:, :]).then_inc(*_sv)
            c.vector.wait_ge(c.sem["vD"], c.n["vD"])
            _sp2 = c.bump("plV")
            c.vector.scalar_tensor_tensor(out=dst[:, :], in0=eS[par][:, :],
                                          scalar=t8[:, 7:8], in1=eS[par][:, :],
                                          op0=ALU.is_ge, op1=ALU.mult).then_inc(*_sp2)
            c.vector.wait_ge(c.sem["plV"], c.n["plV"])
            return dst

        P_of = {}
        nrm_guard = [0, 0]

        def V_phase(k, Pq):
            hh, qt = tiles[k]
            hbase = hh * 64
            q0 = qt * 128
            par = k % 2
            Pt = pA[par] if Pq is pB[par] else pB[par]
            c.tensor.wait_ge(c.sem["plV"], base_plV + k + 1)
            for kg in range(8):
                for j in range(4):
                    _sp = c.bump("tP")
                    c.tensor.transpose(psT[:, j * 128:(j + 1) * 128],
                                       Pq[:, (kg * 4 + j) * 128:(kg * 4 + j + 1) * 128],
                                       b["identsb"][:, :]).then_inc(*_sp)
                c.scalar.wait_ge(c.sem["tP"], c.n["tP"])
                _se = c.bump("ev2")
                c.scalar.copy(out=Pt[:, kg * 512:(kg + 1) * 512],
                              in_=psT[:, 0:512]).then_inc(*_se)
                c.tensor.wait_ge(c.sem["ev2"], c.n["ev2"])
            c.tensor.wait_ge(c.sem["ev2"], c.n["ev2"])
            ins = None
            for kc in range(32):
                ins = c.tensor.matmul(psPV[0:65, 0:128],
                                      vt[:, kc * 130 + hh * 65: kc * 130 + hh * 65 + 65],
                                      Pt[:, kc * 128:(kc + 1) * 128],
                                      start=(kc == 0), stop=(kc == 31))
            ins.then_inc(*_b(c, "tv"))
            c.scalar.wait_ge(c.sem["tv"], c.n["tv"])
            c.scalar.wait_ge(c.sem["tP"], nrm_guard[0])
            _se = c.bump("ev2")
            c.scalar.copy(out=b["e65"][:, :], in_=psPV[0:65, 0:128]).then_inc(*_se)
            c.tensor.wait_ge(c.sem["ev2"], c.n["ev2"])
            _sp = c.bump("tP")
            c.tensor.transpose(psB[:, 0:65], b["e65"][:, :],
                               b["identf"][0:65, 0:65]).then_inc(*_sp)
            nrm_guard[0] = c.n["tP"]
            c.scalar.wait_ge(c.sem["tP"], c.n["tP"])
            _se = c.bump("ev2")
            c.scalar.activation(out=b["lnt"][:, :], in_=psB[:, 64:65],
                                func=AF.Ln, bias=0.0, scale=1.0).then_inc(*_se)
            c.scalar.wait_ge(c.sem["ev2"], c.n["ev2"])
            _se = c.bump("ev2")
            c.scalar.activation(out=b["rzt"][:, :], in_=b["lnt"][:, :],
                                func=AF.Exp, bias=0.0, scale=-1.0).then_inc(*_se)
            c.scalar.wait_ge(c.sem["ev2"], c.n["ev2"])
            c.scalar.wait_ge(c.sem["tP"], nrm_guard[1])
            _se = c.bump("ev2")
            c.scalar.activation(out=b["qn_sb"][:, :], in_=psB[:, 0:64],
                                func=AF.Copy, bias=0.0,
                                scale=b["rzt"][:, 0:1]).then_inc(*_se)
            c.tensor.wait_ge(c.sem["ev2"], c.n["ev2"])
            _sp = c.bump("tP")
            c.tensor.transpose(psP[0:64, 0:128], b["qn_sb"][:, :],
                               b["identsb"][:, :]).then_inc(*_sp)
            nrm_guard[1] = c.n["tP"]
            c.scalar.wait_ge(c.sem["tP"], c.n["tP"])
            _se = c.bump("ev2")
            c.scalar.copy(out=b["wacc"][hbase:hbase + 64, p * 4096 + q0: p * 4096 + q0 + 128],
                          in_=psP[0:64, 0:128]).then_inc(*_se)


        for k in range(ntile + 2):
            if k < ntile:
                S_phase(k)
            if 1 <= k <= ntile:
                P_of[k - 1] = L_phase(k - 1)
            if k >= 2:
                V_phase(k - 2, P_of.pop(k - 2))
        c.wait("scalar", wow)
        _se = c.bump("ev2")
        c.scalar.copy(out=b["wo_all"][:, p * 1024:(p + 1) * 1024],
                      in_=wo_sb[:, :]).then_inc(*_se)
        c.gpsimd.wait_ge(c.sem["ev2"], c.n["ev2"])
    c.tensor.wait_ge(c.sem["vD"], c.n["vD"])
    c.tensor.wait_ge(c.sem["ev2"], c.n["ev2"])
    x1t_guard = [0, 0]
    ps_free = [0, 0]
    for fo in range(8):
        for tb in range(8):
            tb0 = tb * 512
            pp = tb % 2
            c.sync.wait_ge(c.sem["vD"], x1t_guard[pp])
            yw = c.sdma(b["y1t"][:, pp * 512:(pp + 1) * 512],
                        y1_hbm[fo * 128:(fo + 1) * 128, tb0:tb0 + 512])
            c.tensor.wait_ge(c.sem["vD"], ps_free[pp])
            ins = None
            for pr in range(n_pairs):
                ins = c.tensor.matmul(psS[:, pp * 512:(pp + 1) * 512],
                                      b["wo_all"][:, pr * 1024 + fo * 128: pr * 1024 + (fo + 1) * 128],
                                      b["wacc"][:, pr * 4096 + tb0: pr * 4096 + tb0 + 512],
                                      start=(pr == 0), stop=(pr == n_pairs - 1))
            ins.then_inc(*_b(c, "tP"))
            c.wait("vector", yw)
            c.vector.wait_ge(c.sem["tP"], c.n["tP"])
            _sv = c.bump("vD")
            c.vector.scalar_tensor_tensor(
                out=b["x1t"][:, pp * 512:(pp + 1) * 512],
                in0=psS[:, pp * 512:(pp + 1) * 512],
                scalar=bo_sb[:, fo:fo + 1], in1=b["y1t"][:, pp * 512:(pp + 1) * 512],
                op0=ALU.add, op1=ALU.add).then_inc(*_sv)
            x1t_guard[pp] = c.n["vD"]
            ps_free[pp] = c.n["vD"]
            c.sync.wait_ge(c.sem["vD"], c.n["vD"])
            stw = c.sdma(x1_hbm[fo * 128:(fo + 1) * 128, tb0:tb0 + 512],
                         b["x1t"][:, pp * 512:(pp + 1) * 512])
            c.wait("vector", stw)
    c.drain_dmas()


BFC = ml_dtypes.bfloat16


def prep_consts():
    identv = np.eye(128, dtype=np.float32).astype(BFC)
    identf32 = np.eye(128, dtype=np.float32)
    selbc = np.zeros((9, 9 * 128), np.float32)
    for p in range(9):
        selbc[p, p * 128:(p + 1) * 128] = 1.0
    maskT0 = np.zeros((16, 256), np.float32)
    for kk in range(16):
        for h in range(16):
            maskT0[kk, 16 * h + kk:16 * h + 16] = 1.0
    return identv, identf32, selbc.astype(BFC), maskT0

"""Fused single-launch megakernel: all 6 stages in ONE program.
Intermediates go through Internal DRAM (first fusion step); stage boundaries
are full barriers; SBUF/psum tiles are stage-scoped (freed + reused)."""

ENGINES = ("sync", "tensor", "vector", "scalar", "gpsimd")

ALL_SEMS = sorted(set(
    ["vs", "ss", "pe", "ev",
     "mm", "hv", "yv",
     "tP", "vD", "sA", "wv", "scA", "plV", "tv", "ev2",
     "zz", "fz", "lnv",
     "dv", "rv", "kv", "kv2", "qv", "ov", "uv", "cv"]))


class FCtx(Ctx):

    def __init__(self, live_name, eng, sems, n_shared, rr_shared):
        super().__init__(live_name, eng, sems)
        self.n = n_shared
        self._rr = rr_shared


def emit_barrier(c):
    for nm in sorted(c.n.keys()):
        for e in ENGINES:
            getattr(c, e).wait_ge(c.sem[nm], c.n[nm])


def build_fused(debug_outs=True):
    nc = bass.Bass("TRN2", target_bir_lowering=False, debug=False)
    dt = lambda name, shape, dty, kind: nc.dram_tensor(name, shape, dty, kind=kind).ap()
    E, O, N = "ExternalInput", "ExternalOutput", "Internal"
    d = {}
    d["x"] = dt("x", [T, F], F32, E)
    d["ident"] = dt("ident", [128, 128], BF16, E)
    d["identf32"] = dt("identf32", [128, 128], F32, E)
    d["wgT"] = dt("wgT", [1024, 3072], I32, E)
    d["wuT"] = dt("wuT", [1024, 3072], I32, E)
    d["wdT"] = dt("wdT", [3072, 1024], I32, E)
    for nm in ("WqT", "WkT", "WvT", "WoT"):
        d[nm] = dt(nm, [F, F], F32, E)
    d["bqkv_hbm"] = dt("bqkv", [128, 24], F32, E)
    d["bo_hbm"] = dt("bo", [128, 8], F32, E)
    for nm in ("Wq", "Wk", "Wv"):
        d[nm] = dt(nm + "1", [F, F], F32, E)
    d["Wlr"] = dt("Wlr", [F, 32], F32, E)
    d["convr_h"] = dt("convr", [128, 96], F32, E)
    d["winbd0"] = dt("winbd0", [128, 64], F32, E)
    d["woutbd0"] = dt("woutbd0", [128, 64], F32, E)
    d["woutd40"] = dt("woutd40", [4, 1024], F32, E)
    d["maskT0"] = dt("maskT0", [16, 256], F32, E)
    d["WoL1"] = dt("WoL1", [F, F], F32, E)
    d["W1"] = dt("W1", [9, F, 256], F32, E)
    d["W2"] = dt("W2", [9, 256, F], F32, E)
    d["Wo1"] = dt("Wo1", [F, 256], F32, E)
    d["Wo2"] = dt("Wo2", [256, 9], F32, E)
    d["b1r"] = dt("b1r", [128, 18], F32, E)
    d["b2"] = dt("b2", [9, F], F32, E)
    d["bo1r"] = dt("bo1r", [128, 2], F32, E)
    d["bo2r"] = dt("bo2r", [1, 9], F32, E)
    d["gbr"] = dt("gbr", [128, 16], F32, E)
    d["mixr"] = dt("mixr", [128, 24], F32, E)
    d["selbc"] = dt("selbc", [9, 9 * 128], BF16, E)
    NI = O if debug_outs else N
    d["xf"] = dt("xf", [F, T], BF16, NI)
    d["xnf"] = dt("xnf", [F, T], BF16, NI)
    d["y1"] = dt("y1", [F, T], BF16, NI)
    d["x1"] = dt("x1", [F, T], BF16, NI)
    d["xn1"] = dt("xn1", [F, T], BF16, NI)
    d["qf"] = dt("qf", [F, T], BF16, NI)
    d["kf"] = dt("kf", [F, T], BF16, NI)
    d["vf"] = dt("vf", [F, T], BF16, NI)
    d["lrf"] = dt("lrf", [32, T], F32, NI)
    d["o_scan"] = dt("o_scan", [64, 16 * T], F32, NI)
    d["out"] = dt("out", [T, F], F32, O)

    n_state = {nm: 0 for nm in ALL_SEMS + DMA_SEMS}
    rr_state = {"sync": 0, "gpsimd": 0}

    with contextlib.ExitStack() as gstk:
        sems = {nm: gstk.enter_context(nc.semaphore(nm)) for nm in ALL_SEMS + DMA_SEMS}
        identsb = gstk.enter_context(nc.sbuf_tensor("g_identsb", [128, 128], BF16))
        identf = gstk.enter_context(nc.sbuf_tensor("g_identf", [128, 128], F32))
        block = gstk.enter_context(nc.Block())

        def run_stage(alloc_fn, emit_fn):
            with contextlib.ExitStack() as stk:
                bufs = alloc_fn(nc, stk)
                bufs["identsb"] = identsb
                bufs["identf"] = identf
                snap_n = dict(n_state)
                snap_rr = dict(rr_state)
                for eng_name in ENGINES:
                    n_state.clear(); n_state.update(snap_n)
                    rr_state.clear(); rr_state.update(snap_rr)

                    def body(eng, _nm=eng_name, _bufs=bufs, _emit=emit_fn):
                        c = FCtx(_nm, eng, sems, n_state, rr_state)
                        _emit(c, _bufs)
                        emit_barrier(c)
                    getattr(block, eng_name)(body)

        def alloc0(nc_, stk):
            return {}

        def emit0(c, b):
            p1 = c.sdma(identsb[:, :], d["ident"][:, :])
            p2 = c.sdma(identf[:, :], d["identf32"][:, :])
            c.wait("tensor", p1)
            c.wait("tensor", p2)
        run_stage(alloc0, emit0)

        def alloc1(nc_, stk):
            sb = lambda name, shape, dty: stk.enter_context(nc_.sbuf_tensor("s1_" + name, shape, dty))
            ps = lambda name, shape, dty: stk.enter_context(nc_.psum_tensor("s1_" + name, shape, dty))
            return {
                "x": d["x"], "xf": d["xf"], "xnf": d["xnf"], "ident": d["ident"],
                "xt": sb("xt", [128, 4096], BF16), "xnt": sb("xnt", [128, 4096], BF16),
                "st1": sb("st1", [128, 18], F32),
                "evac": sb("evac", [128, 512], BF16), "evac2": sb("evac2", [128, 512], BF16),
                "psT": ps("psT", [128, 1024], BF16), "psT2": ps("psT2", [128, 1024], BF16),
            }
        def alloc12(nc_, stk):
            bufs = alloc1(nc_, stk)
            sb = lambda name, shape, dty: stk.enter_context(nc_.sbuf_tensor("s2_" + name, shape, dty))
            ps = lambda name, shape, dty: stk.enter_context(nc_.psum_tensor("s2_" + name, shape, dty))
            bufs.update({
                "xnf": d["xnf"], "xf": d["xf"], "y1": d["y1"],
                "wgT": d["wgT"], "wuT": d["wuT"], "wdT": d["wdT"],
                "xn_sb": sb("xn_sb", [128, 8 * T], BF16),
                "wg_sb": sb("wg_sb", [128, 2 * 1024], BF16),
                "wu_sb": sb("wu_sb", [128, 2 * 1024], BF16),
                "wd_sb": sb("wd_sb", [128, 24 * 1024], BF16),
                "h_sb": sb("h_sb", [128, 24 * 1024], BF16),
                "sg_sb": sb("sg_sb", [128, 512], BF16),
                "xf_sb": sb("xf_sb", [128, 2 * 512], BF16),
                "y1_sb": sb("y1_sb", [128, 2 * 512], BF16),
                "psG": ps("psG", [128, 512], F32), "psU": ps("psU", [128, 512], F32),
                "psD": ps("psD", [128, 512], F32),
            })
            return bufs

        def emit12(c, b):
            wd_pre = [c.gdma(b["wd_sb"][:, hc * 1024:(hc + 1) * 1024],
                             b["wdT"][hc * 128:(hc + 1) * 128, :])
                      for hc in range(24)]
            emit_ln0_and_layout(c, b)
            emit_l0_ffn(c, b, wd_pre=wd_pre)
        run_stage(alloc12, emit12)

        def alloc3(nc_, stk):
            sb = lambda name, shape, dty: stk.enter_context(nc_.sbuf_tensor("s3_" + name, shape, dty))
            ps = lambda name, shape, dty: stk.enter_context(nc_.psum_tensor("s3_" + name, shape, dty))
            return {
                "xnf": d["xnf"], "y1": d["y1"], "x1": d["x1"],
                "WqT": d["WqT"], "WkT": d["WkT"], "WvT": d["WvT"], "WoT": d["WoT"],
                "bqkv_hbm": d["bqkv_hbm"], "bo_hbm": d["bo_hbm"],
                "bqkv": sb("bqkv", [128, 24], F32), "bo_sb": sb("bo_sb", [128, 8], F32),
                "xnc": sb("xnc", [128, 2 * 4096], BF16),
                "wq_sb": sb("wq_sb", [128, 1024], BF16), "wk_sb": sb("wk_sb", [128, 1024], BF16),
                "wv_sb": sb("wv_sb", [128, 1024], BF16), "wo_sb": sb("wo_sb", [128, 1024], BF16),
                "Qp": sb("Qp", [128, 4096], BF16), "Kp": sb("Kp", [128, 4096], BF16),
                "Vp": sb("Vp", [128, 4096], BF16),
                "vt": sb("vt", [128, 32 * 130], BF16),
                "eS": sb("eS", [128, 4096], BF16), "pA": sb("pA", [128, 4096], BF16),
                "pB": sb("pB", [128, 4096], BF16),
                "eS2": sb("eS2", [128, 4096], BF16), "pA2": sb("pA2", [128, 4096], BF16),
                "pB2": sb("pB2", [128, 4096], BF16),
                "t8": sb("t8", [128, 8], BF16),
                "zrow": sb("zrow", [1, 128], F32), "zbc": sb("zbc", [64, 128], F32),
                "zi": sb("zi", [128, 4], F32), "ones1": sb("ones1", [1, 64], F32),
                "e65": sb("e65", [65, 128], F32),
                "qn_sb": sb("qn_sb", [128, 64], BF16),
                "rzt": sb("rzt", [128, 1], F32), "lnt": sb("lnt", [128, 1], F32),
                "wacc": sb("wacc", [128, 8 * 4096], BF16),
                "wo_all": sb("wo_all", [128, 8 * 1024], BF16),
                "y1t": sb("y1t", [128, 2 * 512], BF16), "x1t": sb("x1t", [128, 2 * 512], BF16),
                "psS": ps("psS", [128, 2048], F32), "psT": ps("psT", [128, 1024], BF16),
                "psPV": ps("psPV", [128, 512], F32), "psB": ps("psB", [128, 512], F32),
                "psP": ps("psP", [128, 1024], BF16),
            }
        run_stage(alloc3, emit_l0_attn_v2)

        def alloc4(nc_, stk):
            sb = lambda name, shape, dty: stk.enter_context(nc_.sbuf_tensor("s4_" + name, shape, dty))
            ps = lambda name, shape, dty: stk.enter_context(nc_.psum_tensor("s4_" + name, shape, dty))
            return {
                "x1": d["x1"], "xn1": d["xn1"], "qf": d["qf"], "kf": d["kf"], "vf": d["vf"],
                "lrf": d["lrf"], "Wq": d["Wq"], "Wk": d["Wk"], "Wv": d["Wv"], "Wlr": d["Wlr"],
                "convr_h": d["convr_h"],
                "x1_sb": sb("x1_sb", [128, 8 * T], BF16),
                "xn_sb": sb("xn_sb", [128, 8 * T], BF16),
                "w_sb": sb("w_sb", [128, 2 * 1024], BF16),
                "wlr_sb": sb("wlr_sb", [128, 256], BF16),
                "ch_a": sb("ch_a", [128, T], BF16), "ch_b": sb("ch_b", [128, T], BF16),
                "lr_sb": sb("lr_sb", [32, T + 32], F32),
                "lrt1": sb("lrt1", [32, 512], F32), "lrt2": sb("lrt2", [32, 512], F32),
                "lrt3": sb("lrt3", [32, 512], F32),
                "convr": sb("convrs", [128, 96], F32),
                "rows": sb("rows", [1, 4 * 512], F32),
                "sq": sb("sq", [128, 512], BF16), "t1": sb("t1", [128, 512], BF16),
                "onesb": sb("onesb", [128, 1], BF16), "ones1rF": sb("ones1rF", [1, 128], F32),
                "eps1": sb("eps1", [1, 1], F32), "eps3": sb("eps3", [32, 1], F32),
                "psA": ps("psA", [128, 512], F32), "psB": ps("psB", [128, 512], F32),
            }

        def emit4(c, b):
            _sv = c.bump("vD")
            c.vector.memset(b["onesb"][:, :], 1.0)
            c.vector.memset(b["ones1rF"][:, :], 1.0)
            c.vector.memset(b["eps1"][:, :], 1e-5)
            c.vector.memset(b["eps3"][:, :], 0.001).then_inc(*_sv)
            cw = c.sdma(b["convr"][:, :], b["convr_h"][:, :])
            c.wait("vector", cw)
            c.tensor.wait_ge(c.sem["vD"], c.n["vD"])
            emit_l1_pre(c, b)
        run_stage(alloc4, emit4)

        def alloc5(nc_, stk):
            sb = lambda name, shape, dty: stk.enter_context(nc_.sbuf_tensor("s5_" + name, shape, dty))
            ps = lambda name, shape, dty: stk.enter_context(nc_.psum_tensor("s5_" + name, shape, dty))
            bufs = {
                "qf": d["qf"], "kf": d["kf"], "vf": d["vf"], "lrf": d["lrf"],
                "winbd0": d["winbd0"], "woutbd0": d["woutbd0"], "woutd40": d["woutd40"],
                "maskT0": d["maskT0"], "o_scan": d["o_scan"],
                "qseg": sb("qseg", [128, 2 * 4096], F32),
                "kseg": sb("kseg", [128, 2 * 4096], F32),
                "vseg": sb("vseg", [128, 2 * 4096], F32),
                "lr_sb": sb("lr_sb", [32, T], F32),
                "winbd": sb("winbd", [128, 64], F32), "woutbd": sb("woutbd", [128, 64], F32),
                "woutd4": sb("woutd4", [4, 1024], F32),
                "eA": sb("eA", [16, 128], F32), "eAr": sb("eAr", [16, 128], F32),
                "mxA": sb("mxA", [16, 32], F32), "mxr": sb("mxr", [16, 1], F32),
                "gneg": sb("gneg", [1, 1], F32),
                "sC": sb("sC", [16, 128], F32), "nmx": sb("nmx", [128, 1], F32),
                "eCT": sb("eCT", [128, 16], F32), "sCs": sb("sCs", [128, 1], F32),
                "lrN": sb("lrN", [128, 1], F32), "ones1x1": sb("ones1x1", [1, 1], F32),
                "lrrow": sb("lrrow", [1, 128], F32),
                "lrt": sb("lrt", [16, 32], F32), "sums": sb("sums", [16, 32], F32),
                "kqT": sb("kqT", [4, 512], F32), "recT": sb("recT", [1, 512], F32),
                "qkm": sb("qkm", [16, 256], F32),
                "ktok": sb("ktok", [16, 2 * 1024], F32), "vtok": sb("vtok", [16, 2 * 1024], F32),
                "eC": sb("eC", [16, 128], F32), "recC": sb("recC", [1, 128], F32),
                "aN": sb("aN", [16, 128], F32), "t128": sb("t128", [16, 128], F32),
                "oTs": sb("oTs", [64, 16 * 512], F32),
                "maskT": sb("maskT", [16, 256], F32),
                "ones4": sb("ones4", [4, 1], F32), "ones16": sb("ones16", [16, 1], F32),
                "ones1x4": sb("ones1x4", [1, 4], F32), "ones1x16": sb("ones1x16", [1, 16], F32),
                "psE": ps("psE", [16, 512], F32), "psT4": ps("psT4", [4, 512], F32),
                "psQ": ps("psQ", [16, 512], F32), "psO": ps("psO", [64, 512], F32),
                "psK": ps("psK", [16, 512], F32), "psU": ps("psU", [128, 512], F32),
                "psD": ps("psD", [4, 512], F32), "psR": ps("psR", [16, 512], F32),
            }
            return bufs

        def emit5(c, b):
            _sv = c.bump("vD")
            c.vector.memset(b["ones4"][:, :], 1.0)
            c.vector.memset(b["ones16"][:, :], 1.0)
            c.vector.memset(b["ones1x4"][:, :], 1.0)
            c.vector.memset(b["ones1x16"][:, :], 1.0)
            c.vector.memset(b["ones1x1"][:, :], 1.0).then_inc(*_sv)
            c.tensor.wait_ge(c.sem["vD"], c.n["vD"])
            emit_scan(c, b, n_seg=8)
        run_stage(alloc5, emit5)

        def alloc6(nc_, stk):
            sb = lambda name, shape, dty: stk.enter_context(nc_.sbuf_tensor("s6_" + name, shape, dty))
            ps = lambda name, shape, dty: stk.enter_context(nc_.psum_tensor("s6_" + name, shape, dty))
            return {
                "x1": d["x1"], "xn1": d["xn1"], "o_scan": d["o_scan"], "WoL1": d["WoL1"],
                "W1": d["W1"], "W2": d["W2"], "Wo1": d["Wo1"], "Wo2": d["Wo2"],
                "b1r": d["b1r"], "b2": d["b2"], "bo1r": d["bo1r"], "bo2r": d["bo2r"],
                "gbr": d["gbr"], "mixr": d["mixr"], "identf32": d["identf32"],
                "selbc": d["selbc"], "out": d["out"],
                "xntb": sb("xntb", [128, 8 * 520], BF16),
                "xm_sb": sb("xm_sb", [128, 8 * 512], BF16),
                "w1_sb": sb("w1_sb", [128, 2 * 2048], BF16),
                "w2_sb": sb("w2_sb", [128, 9 * 2048], BF16),
                "wo1_sb": sb("wo1_sb", [128, 2048], BF16),
                "wo2_sb": sb("wo2_sb", [128, 18], BF16),
                "wo_sb2": sb("wo_sb2", [64, 16 * 1024], BF16),
                "b1_sb": sb("b1_sb", [128, 18], F32), "b2_sb": sb("b2_sb", [9, 1024], BF16),
                "bo1_sb": sb("bo1_sb", [128, 2], F32), "bo2_sb": sb("bo2_sb", [1, 9], BF16),
                "gb_sb": sb("gb_sb", [128, 16], F32), "mixr_sb": sb("mixr_sb", [128, 24], F32),
                "th_sb": sb("th_sb", [128, 2 * 512], BF16),
                "mx_sb": sb("mx_sb", [128, 1], F32), "oh_sb": sb("oh_sb", [128, 9], BF16),
                "ohT_sb": sb("ohT_sb", [9, 512], BF16), "ohbc_sb": sb("ohbc_sb", [128, 512], BF16),
                "ht_sb": sb("ht_sb", [128, 9 * 1024], BF16),
                "zb_sb": sb("zb_sb", [128, 8 * 512], BF16),
                "fb_sb": sb("fb_sb", [128, 8 * 512], BF16),
                "osc_sb": sb("osc_sb", [64, 16 * 512], BF16),
                "x1t_sb": sb("x1t_sb", [128, 512], BF16),
                "of1_sb": sb("of1_sb", [128, 512], F32),
                "ofm_sb": sb("ofm_sb", [128, 8 * 512], F32),
                "otok_sb": sb("otok_sb", [128, 1024], F32),
                "rows": sb("rows", [1, 4 * 512], F32),
                "sq": sb("sq", [128, 512], BF16), "t1": sb("t1", [128, 512], F32),
                "onesb": sb("onesb", [128, 1], BF16), "ones1rF": sb("ones1rF", [1, 128], F32),
                "ones1rB": sb("ones1rB", [1, 128], BF16),
                "eps1": sb("eps1", [1, 1], F32),
                "selbc_sb": sb("selbc_sb", [9, 9 * 128], BF16),
                "psA": ps("psA", [128, 512], F32), "psB": ps("psB", [128, 512], F32),
                "psC": ps("psC", [128, 512], F32), "psF": ps("psF", [128, 512], F32),
                "psOH": ps("psOH", [9, 1024], BF16), "psTf": ps("psTf", [128, 512], F32),
            }

        def emit6(c, b):
            _sv = c.bump("vD")
            c.vector.memset(b["onesb"][:, :], 1.0)
            c.vector.memset(b["ones1rF"][:, :], 1.0)
            c.vector.memset(b["ones1rB"][:, :], 1.0)
            c.vector.memset(b["eps1"][:, :], 1e-5).then_inc(*_sv)
            c.tensor.wait_ge(c.sem["vD"], c.n["vD"])
            emit_l1_ffn_final(c, b)
        run_stage(alloc6, emit6)
    return nc


def kernel_fused(**inp):
    identv, identf32, selbc, maskT0 = prep_consts()
    x_all = np.asarray(inp["L0_attn_x"])
    bq = np.asarray(inp["L0_attn_bq"]); bk = np.asarray(inp["L0_attn_bk"])
    bv = np.asarray(inp["L0_attn_bv"]); bo = np.asarray(inp["L0_attn_bo"])
    Win0 = np.asarray(inp["L1_attn_W_in_init"])[0]
    Wout0 = np.asarray(inp["L1_attn_W_out_init"])[0]
    winbd0 = np.zeros((128, 64), np.float32)
    woutbd0 = np.zeros((128, 64), np.float32)
    woutd40 = np.zeros((4, 1024), np.float32)
    for h in range(16):
        j, h2 = h // 2, h % 2
        for D in range(4):
            winbd0[64 * h2:64 * h2 + 64, 8 * j + 4 * h2 + D] = Win0[D, h, :]
            woutbd0[64 * h2:64 * h2 + 64, 8 * j + 4 * h2 + D] = Wout0[D, h, :]
            woutd40[D, 64 * h:64 * h + 64] = Wout0[D, h, :]
    convr = np.zeros((128, 96), np.float32)
    for pi, key in enumerate(("L1_attn_conv_q", "L1_attn_conv_k", "L1_attn_conv_v")):
        cc_ = np.asarray(inp[key])
        for oc in range(8):
            convr[:, pi * 32 + oc * 4: pi * 32 + (oc + 1) * 4] = cc_[oc * 128:(oc + 1) * 128, :]
    b1 = np.asarray(inp["L1_ffn_b1"]); b2 = np.asarray(inp["L1_ffn_b2"])
    bo1 = np.asarray(inp["L1_ffn_bo1"]); bo2 = np.asarray(inp["L1_ffn_bo2"])
    gam = np.asarray(inp["L1_ffn_gamma"]); bet = np.asarray(inp["L1_ffn_beta"])
    mixk = np.asarray(inp["L1_ffn_mix_kernel"])
    b1r = np.zeros((128, 18), np.float32)
    for p in range(9):
        for mc in range(2):
            b1r[:, p * 2 + mc] = b1[p, mc * 128:(mc + 1) * 128]
    gbr = np.concatenate([gam.reshape(8, 128).T, bet.reshape(8, 128).T], axis=1).astype(np.float32)
    mixr = np.zeros((128, 24), np.float32)
    for ch in range(8):
        for j in range(3):
            mixr[:, ch * 3 + j] = mixk[j, ch * 128:(ch + 1) * 128]
    common = {
        "ident": identv, "identf32": identf32,
        "wgT": np.ascontiguousarray(np.asarray(inp["L0_ffn_w_gate"]).T).astype(np.int32),
        "wuT": np.ascontiguousarray(np.asarray(inp["L0_ffn_w_up"]).T).astype(np.int32),
        "wdT": np.ascontiguousarray(np.asarray(inp["L0_ffn_w_down"]).T).astype(np.int32),
        "WqT": np.ascontiguousarray(np.asarray(inp["L0_attn_Wq"]).T),
        "WkT": np.ascontiguousarray(np.asarray(inp["L0_attn_Wk"]).T),
        "WvT": np.ascontiguousarray(np.asarray(inp["L0_attn_Wv"]).T),
        "WoT": np.ascontiguousarray(np.asarray(inp["L0_attn_Wo"]).T),
        "bqkv": np.concatenate([bq.reshape(8, 128).T, bk.reshape(8, 128).T,
                                bv.reshape(8, 128).T], axis=1).astype(np.float32),
        "bo": np.ascontiguousarray(bo.reshape(8, 128).T).astype(np.float32),
        "Wq1": np.ascontiguousarray(np.asarray(inp["L1_attn_Wq"])),
        "Wk1": np.ascontiguousarray(np.asarray(inp["L1_attn_Wk"])),
        "Wv1": np.ascontiguousarray(np.asarray(inp["L1_attn_Wv"])),
        "Wlr": np.ascontiguousarray(np.asarray(inp["L1_attn_Wlr"])),
        "convr": convr, "winbd0": winbd0, "woutbd0": woutbd0, "woutd40": woutd40,
        "maskT0": maskT0,
        "WoL1": np.ascontiguousarray(np.asarray(inp["L1_attn_Wo"])),
        "W1": np.ascontiguousarray(np.asarray(inp["L1_ffn_W1"])),
        "W2": np.ascontiguousarray(np.asarray(inp["L1_ffn_W2"])),
        "Wo1": np.ascontiguousarray(np.asarray(inp["L1_ffn_Wo1"])),
        "Wo2": np.ascontiguousarray(np.asarray(inp["L1_ffn_Wo2"])),
        "b1r": b1r, "b2": b2,
        "bo1r": np.stack([bo1[0:128], bo1[128:256]], axis=1).astype(np.float32),
        "bo2r": np.ascontiguousarray(bo2.reshape(1, 9)).astype(np.float32),
        "gbr": gbr, "mixr": mixr, "selbc": selbc,
    }
    nc = build_fused()
    in_maps = [dict(common, x=np.ascontiguousarray(x_all[cc])) for cc in range(8)]
    res = run_bass_kernel_spmd(nc, in_maps, core_ids=list(range(8)))
    return np.stack([np.asarray(res.results[cc]["out"], dtype=np.float32)
                     for cc in range(8)], axis=0)


kernel = kernel_fused


# version 07
# speedup vs baseline: 1.0023x; 1.0023x over previous
import contextlib
import numpy as np
import ml_dtypes
import concourse.bass as bass
import concourse.mybir as mybir
from concourse.bass_utils import run_bass_kernel_spmd

F32 = mybir.dt.float32
BF16 = mybir.dt.bfloat16
I32 = mybir.dt.int32
AF = mybir.ActivationFunctionType
ALU = mybir.AluOpType
AX = mybir.AxisListType

T = 4096
F = 1024
H = 16
HD = 64


class _FakeInstr:
    def then_inc(self, sem, n):
        return self


_FAKE_INSTR = _FakeInstr()


class _FakeEng:
    def __getattr__(self, name):
        def f(*a, **k):
            return _FAKE_INSTR
        return f


class Ctx:

    RINGS = {"sync": tuple(f"s{i}" for i in range(8)),
             "gpsimd": tuple(f"g{i}" for i in range(6))}

    def __init__(self, live_name, eng, sems):
        fake = _FakeEng()
        for nm in ("sync", "tensor", "vector", "scalar", "gpsimd"):
            setattr(self, nm, fake)
        setattr(self, live_name, eng)
        self.live = live_name
        self.sem = sems
        self.n = {k: 0 for k in sems}
        self._rr = {"sync": 0, "gpsimd": 0}

    def bump(self, name, amount=1):
        self.n[name] += amount
        return self.sem[name], amount

    def dma(self, eng_name, out, in_):
        ring = self.RINGS[eng_name]
        k = self._rr[eng_name]
        self._rr[eng_name] = (k + 1) % len(ring)
        s = ring[k]
        eng = getattr(self, eng_name)
        eng.wait_ge(self.sem[s], self.n[s])
        eng.dma_start(out=out, in_=in_).then_inc(*self.bump(s, 16))
        return self.sem[s], self.n[s]

    def dma_group(self, eng_name, transfers):
        assert eng_name == "sync"
        ring = self.RINGS[eng_name]
        k = self._rr[eng_name]
        self._rr[eng_name] = (k + 1) % len(ring)
        s = ring[k]
        eng = getattr(self, eng_name)
        eng.wait_ge(self.sem[s], self.n[s])
        for out, in_ in transfers:
            eng.dma_start(out=out, in_=in_).then_inc(*self.bump(s, 16))
        return self.sem[s], self.n[s]

    def gdma(self, out, in_):
        return self.dma("gpsimd", out, in_)

    def sdma(self, out, in_):
        return self.dma("sync", out, in_)

    def wait(self, eng_name, pair):
        getattr(self, eng_name).wait_ge(*pair)

    def drain_dmas(self):
        for ring in self.RINGS.values():
            for s in ring:
                self.sync.wait_ge(self.sem[s], self.n[s])


DMA_SEMS = [f"s{i}" for i in range(8)] + [f"g{i}" for i in range(6)]


def run_program(nc, program, sem_names):
    with contextlib.ExitStack() as st:
        sems = {nm: st.enter_context(nc.semaphore(nm))
                for nm in list(sem_names) + DMA_SEMS}
        block = st.enter_context(nc.Block())
        for eng_name in ("sync", "tensor", "vector", "scalar", "gpsimd"):
            getattr(block, eng_name)(
                lambda eng, _nm=eng_name: program(Ctx(_nm, eng, sems)))


def _b(c, name, amount=1):
    return c.bump(name, amount)



def emit_ln0_and_layout(c, b):
    x_hbm = b["x"]
    xf_hbm = b["xf"]
    xnf_hbm = b["xnf"]
    xt = b["xt"]
    xnt = b["xnt"]
    st1 = b["st1"]
    psT = b["psT"]
    psT2 = b["psT2"]
    ev = b["evac"]
    ev2 = b["evac2"]

    id_w = c.sdma(b["identsb"][:, :], b["ident"][:, :])
    c.wait("tensor", id_w)

    _sv = c.bump("vs")
    c.vector.memset(st1[:, 14:15], 1e-5).then_inc(*_sv)

    for g in range(8):
        c.gpsimd.wait_ge(c.sem["pe"], c.n["pe"])
        lds = []
        for j in range(4):
            tt = g * 4 + j
            lds.append(c.gdma(xt[:, j * 1024:(j + 1) * 1024],
                              x_hbm[tt * 128:(tt + 1) * 128, :]))
        for p_ in lds:
            c.wait("vector", p_)
        c.vector.wait_ge(c.sem["pe"], c.n["pe"])
        for j in range(4):
            xs = xt[:, j * 1024:(j + 1) * 1024]
            xns = xnt[:, j * 1024:(j + 1) * 1024]
            c.vector.bn_stats(out=st1[:, 0:6], in_=xs[:, 0:512])
            _sv = c.bump("vs")
            c.vector.bn_stats(out=st1[:, 6:12], in_=xs[:, 512:1024]).then_inc(*_sv)
            c.vector.wait_ge(c.sem["vs"], c.n["vs"])
            _sv = c.bump("vs")
            c.vector.bn_aggr(out=st1[:, 12:14], in_=st1[:, 0:12]).then_inc(*_sv)
            c.scalar.wait_ge(c.sem["vs"], c.n["vs"])
            _ss = c.bump("ss")
            c.scalar.activation(out=st1[:, 15:16], in_=st1[:, 13:14], func=AF.Ln,
                                bias=st1[:, 14:15], scale=1.0).then_inc(*_ss)
            c.scalar.wait_ge(c.sem["ss"], c.n["ss"])
            _ss = c.bump("ss")
            c.scalar.activation(out=st1[:, 16:17], in_=st1[:, 15:16], func=AF.Exp,
                                bias=0.0, scale=-0.5).then_inc(*_ss)
            c.vector.wait_ge(c.sem["ss"], c.n["ss"])
            _sv = c.bump("vs")
            c.vector.tensor_scalar(out=xns, in0=xs, scalar1=st1[:, 12:13],
                                   scalar2=st1[:, 16:17],
                                   op0=ALU.subtract, op1=ALU.mult).then_inc(*_sv)
        c.tensor.wait_ge(c.sem["vs"], c.n["vs"])
        for src, dst_hbm, ps, evb in ((xt, xf_hbm, psT, ev), (xnt, xnf_hbm, psT2, ev2)):
            for ch in range(8):
                c.tensor.wait_ge(c.sem["ev"], c.n["ev"])
                for j in range(4):
                    _sp = c.bump("pe")
                    c.tensor.transpose(
                        ps[:, j * 128:(j + 1) * 128],
                        src[:, j * 1024 + ch * 128: j * 1024 + (ch + 1) * 128],
                        b["identsb"][:, :],
                    ).then_inc(*_sp)
                c.vector.wait_ge(c.sem["pe"], c.n["pe"])
                _se = c.bump("ev")
                c.vector.tensor_copy(out=evb[:, :], in_=ps[:, 0:512]).then_inc(*_se)
                c.sync.wait_ge(c.sem["ev"], c.n["ev"])
                st_w = c.sdma(dst_hbm[ch * 128:(ch + 1) * 128, g * 512:(g + 1) * 512],
                              evb[:, :])
                c.wait("vector", st_w)
    c.drain_dmas()


def build_launch1():
    nc = bass.Bass("TRN2", target_bir_lowering=False, debug=False)
    x = nc.dram_tensor("x", [T, F], F32, kind="ExternalInput").ap()
    ident = nc.dram_tensor("ident", [128, 128], BF16, kind="ExternalInput").ap()
    xf = nc.dram_tensor("xf", [F, T], BF16, kind="ExternalOutput").ap()
    xnf = nc.dram_tensor("xnf", [F, T], BF16, kind="ExternalOutput").ap()
    with contextlib.ExitStack() as stk:
        bufs = {
            "x": x, "xf": xf, "xnf": xnf, "ident": ident,
            "xt": stk.enter_context(nc.sbuf_tensor("xt", [128, 4096], BF16)),
            "xnt": stk.enter_context(nc.sbuf_tensor("xnt", [128, 4096], BF16)),
            "st1": stk.enter_context(nc.sbuf_tensor("st1", [128, 18], F32)),
            "identsb": stk.enter_context(nc.sbuf_tensor("identsb", [128, 128], BF16)),
            "evac": stk.enter_context(nc.sbuf_tensor("evac", [128, 512], BF16)),
            "evac2": stk.enter_context(nc.sbuf_tensor("evac2", [128, 512], BF16)),
            "psT": stk.enter_context(nc.psum_tensor("psT", [128, 1024], BF16)),
            "psT2": stk.enter_context(nc.psum_tensor("psT2", [128, 1024], BF16)),
        }
        run_program(nc, lambda c: emit_ln0_and_layout(c, bufs),
                    ["vs", "ss", "pe", "ev"])
    return nc


def host_launch1(x_all):
    nc = build_launch1()
    identv = np.eye(128, dtype=np.float32).astype(ml_dtypes.bfloat16)
    in_maps = [{"x": np.ascontiguousarray(x_all[cc]), "ident": identv} for cc in range(8)]
    res = run_bass_kernel_spmd(nc, in_maps, core_ids=list(range(8)))
    return res.results



def emit_l0_ffn(c, b, wd_pre=None):
    xnf_hbm = b["xnf"]; xf_hbm = b["xf"]; y1_hbm = b["y1"]
    wgT = b["wgT"]; wuT = b["wuT"]; wdT = b["wdT"]
    xn = b["xn_sb"]
    wg_sb = b["wg_sb"]
    wu_sb = b["wu_sb"]
    wd_sb = b["wd_sb"]
    hbuf = b["h_sb"]
    sg = b["sg_sb"]
    xfa = b["xf_sb"]
    y1sb = b["y1_sb"]
    psG = b["psG"]; psU = b["psU"]; psD = b["psD"]
    TTILE = 1024
    INV127 = 1.0 / 127.0

    xn_w = c.dma_group("sync", [
        (xn[:, ch * T:(ch + 1) * T], xnf_hbm[ch * 128:(ch + 1) * 128, :])
        for ch in range(8)])
    if wd_pre is not None:
        wd_waits = wd_pre
    else:
        wd_waits = []
        for hc in range(24):
            wd_waits.append(c.gdma(wd_sb[:, hc * 1024:(hc + 1) * 1024],
                                   wdT[hc * 128:(hc + 1) * 128, :]))

    wbuf_free_at = {0: 0, 1: 0}

    for tt in range(4):
        t0 = tt * TTILE
        for hc in range(24):
            par = hc % 2
            c.gpsimd.wait_ge(c.sem["mm"], wbuf_free_at[par])
            w1 = c.gdma(wg_sb[:, par * 1024:(par + 1) * 1024].rearrange("p (c h) -> p c h", h=128),
                        wgT[:, hc * 128:(hc + 1) * 128].rearrange("(c p) h -> p c h", p=128))
            w2 = c.gdma(wu_sb[:, par * 1024:(par + 1) * 1024].rearrange("p (c h) -> p c h", h=128),
                        wuT[:, hc * 128:(hc + 1) * 128].rearrange("(c p) h -> p c h", p=128))
            c.wait("tensor", w1)
            c.wait("tensor", w2)
            if tt == 0 and hc == 0:
                c.wait("tensor", xn_w)
            for tb in range(2):
                tb0 = t0 + tb * 512
                c.tensor.wait_ge(c.sem["ss"], c.n["ss"])
                c.tensor.wait_ge(c.sem["hv"], c.n["hv"])
                for cc in range(8):
                    c.tensor.matmul(psG[:, :],
                                    wg_sb[:, par * 1024 + cc * 128:par * 1024 + (cc + 1) * 128],
                                    xn[:, cc * T + tb0: cc * T + tb0 + 512],
                                    start=(cc == 0), stop=(cc == 7))
                ins = None
                for cc in range(8):
                    ins = c.tensor.matmul(psU[:, :],
                                          wu_sb[:, par * 1024 + cc * 128:par * 1024 + (cc + 1) * 128],
                                          xn[:, cc * T + tb0: cc * T + tb0 + 512],
                                          start=(cc == 0), stop=(cc == 7))
                ins.then_inc(*_b(c, "mm"))
                c.scalar.wait_ge(c.sem["mm"], c.n["mm"])
                c.scalar.wait_ge(c.sem["hv"], c.n["hv"])
                _ss = c.bump("ss")
                c.scalar.activation(out=sg[:, :], in_=psG[:, :], func=AF.Silu,
                                    bias=0.0, scale=INV127).then_inc(*_ss)
                c.vector.wait_ge(c.sem["ss"], c.n["ss"])
                _sh = c.bump("hv")
                c.vector.scalar_tensor_tensor(
                    out=hbuf[:, hc * 1024 + tb * 512: hc * 1024 + tb * 512 + 512],
                    in0=psU[:, :], scalar=INV127, in1=sg[:, :],
                    op0=ALU.mult, op1=ALU.mult).then_inc(*_sh)
            wbuf_free_at[par] = c.n["mm"]
        if tt == 0:
            for p_ in wd_waits:
                c.wait("tensor", p_)
        c.tensor.wait_ge(c.sem["hv"], c.n["hv"])
        for fo in range(8):
            for tb in range(2):
                tb0 = t0 + tb * 512
                pp = (fo * 2 + tb) % 2
                xfw = c.sdma(xfa[:, pp * 512:(pp + 1) * 512],
                             xf_hbm[fo * 128:(fo + 1) * 128, tb0:tb0 + 512])
                c.tensor.wait_ge(c.sem["yv"], c.n["yv"])
                ins = None
                for hc in range(24):
                    ins = c.tensor.matmul(psD[:, :],
                                          wd_sb[:, hc * 1024 + fo * 128: hc * 1024 + (fo + 1) * 128],
                                          hbuf[:, hc * 1024 + tb * 512: hc * 1024 + tb * 512 + 512],
                                          start=(hc == 0), stop=(hc == 23))
                ins.then_inc(*_b(c, "mm"))
                c.vector.wait_ge(c.sem["mm"], c.n["mm"])
                c.wait("vector", xfw)
                _sy = c.bump("yv")
                c.vector.scalar_tensor_tensor(
                    out=y1sb[:, pp * 512:(pp + 1) * 512],
                    in0=psD[:, :], scalar=INV127, in1=xfa[:, pp * 512:(pp + 1) * 512],
                    op0=ALU.mult, op1=ALU.add).then_inc(*_sy)
                c.sync.wait_ge(c.sem["yv"], c.n["yv"])
                stw = c.sdma(y1_hbm[fo * 128:(fo + 1) * 128, tb0:tb0 + 512],
                             y1sb[:, pp * 512:(pp + 1) * 512])
                c.wait("vector", stw)
        c.vector.wait_ge(c.sem["mm"], c.n["mm"])
    c.drain_dmas()


def build_launch2():
    nc = bass.Bass("TRN2", target_bir_lowering=False, debug=False)
    xnf = nc.dram_tensor("xnf", [F, T], BF16, kind="ExternalInput").ap()
    xf = nc.dram_tensor("xf", [F, T], BF16, kind="ExternalInput").ap()
    wgT = nc.dram_tensor("wgT", [1024, 3072], I32, kind="ExternalInput").ap()
    wuT = nc.dram_tensor("wuT", [1024, 3072], I32, kind="ExternalInput").ap()
    wdT = nc.dram_tensor("wdT", [3072, 1024], I32, kind="ExternalInput").ap()
    y1 = nc.dram_tensor("y1", [F, T], BF16, kind="ExternalOutput").ap()
    with contextlib.ExitStack() as stk:
        bufs = {
            "xnf": xnf, "xf": xf, "y1": y1, "wgT": wgT, "wuT": wuT, "wdT": wdT,
            "xn_sb": stk.enter_context(nc.sbuf_tensor("xn_sb", [128, 8 * T], BF16)),
            "wg_sb": stk.enter_context(nc.sbuf_tensor("wg_sb", [128, 2 * 1024], BF16)),
            "wu_sb": stk.enter_context(nc.sbuf_tensor("wu_sb", [128, 2 * 1024], BF16)),
            "wd_sb": stk.enter_context(nc.sbuf_tensor("wd_sb", [128, 24 * 1024], BF16)),
            "h_sb": stk.enter_context(nc.sbuf_tensor("h_sb", [128, 24 * 1024], BF16)),
            "sg_sb": stk.enter_context(nc.sbuf_tensor("sg_sb", [128, 512], BF16)),
            "xf_sb": stk.enter_context(nc.sbuf_tensor("xf_sb", [128, 2 * 512], BF16)),
            "y1_sb": stk.enter_context(nc.sbuf_tensor("y1_sb", [128, 2 * 512], BF16)),
            "psG": stk.enter_context(nc.psum_tensor("psG", [128, 512], F32)),
            "psU": stk.enter_context(nc.psum_tensor("psU", [128, 512], F32)),
            "psD": stk.enter_context(nc.psum_tensor("psD", [128, 512], F32)),
        }
        run_program(nc, lambda c: emit_l0_ffn(c, bufs),
                    ["mm", "ss", "hv", "yv"])
    return nc



def emit_l0_attn(c, b, n_pairs=8, n_qt=32, stage=99):
    xnf_hbm = b["xnf"]; y1_hbm = b["y1"]; x1_hbm = b["x1"]
    WqT = b["WqT"]; WkT = b["WkT"]; WvT = b["WvT"]; WoT = b["WoT"]
    bqkv = b["bqkv"]
    bo_sb = b["bo_sb"]
    xnc = b["xnc"]
    wq_sb = b["wq_sb"]
    wk_sb = b["wk_sb"]; wv_sb = b["wv_sb"]
    wo_sb = b["wo_sb"]
    Qp = b["Qp"]; Kp = b["Kp"]; Vp = b["Vp"]
    vt = b["vt"]
    eS = b["eS"]; pA = b["pA"]; pB = b["pB"]
    t8 = b["t8"]
    zi = b["zi"]
    zrow = b["zrow"]
    zbc = b["zbc"]
    ones1 = b["ones1"]
    acc = b["acc"]
    y1t = b["y1t"]
    x1t = b["x1t"]
    psS = b["psS"]
    psT = b["psT"]
    psPV = b["psPV"]
    psB = b["psB"]
    psP = b["psP"]

    _sv = c.bump("vD")
    c.vector.memset(vt[:, :].rearrange("p (kc h d) -> p kc h d", h=2, d=65)[:, :, :, 64:65], 1.0)
    c.vector.memset(ones1[:, :], 1.0).then_inc(*_sv)
    bw = c.dma_group("sync", [(bqkv[:, :], b["bqkv_hbm"][:, :]),
                              (bo_sb[:, :], b["bo_hbm"][:, :])])
    c.wait("vector", bw)

    xnc_free_at = {0: 0, 1: 0}
    for p in range(n_pairs):
        c.gpsimd.wait_ge(c.sem["tP"], c.n["tP"])
        wqw = c.gdma(wq_sb.rearrange("p (cc o) -> p cc o", o=128),
                     WqT[:, p * 128:(p + 1) * 128].rearrange("(cc p2) o -> p2 cc o", p2=128))
        wkw = c.gdma(wk_sb.rearrange("p (cc o) -> p cc o", o=128),
                     WkT[:, p * 128:(p + 1) * 128].rearrange("(cc p2) o -> p2 cc o", p2=128))
        wvw = c.gdma(wv_sb.rearrange("p (cc o) -> p cc o", o=128),
                     WvT[:, p * 128:(p + 1) * 128].rearrange("(cc p2) o -> p2 cc o", p2=128))
        wow = c.gdma(wo_sb[:, :], WoT[p * 128:(p + 1) * 128, :])
        for pair_ in (wqw, wkw, wvw):
            c.wait("tensor", pair_)

        for tb in range(8):
            tb0 = tb * 512
            parx = tb % 2
            c.sync.wait_ge(c.sem["tP"], xnc_free_at[parx])
            xw = c.dma_group("sync", [
                (xnc[:, parx * 4096 + cc * 512: parx * 4096 + (cc + 1) * 512],
                 xnf_hbm[cc * 128:(cc + 1) * 128, tb0:tb0 + 512]) for cc in range(8)])
            c.wait("tensor", xw)
            for dst, wsb, bcol, scale in ((Qp, wq_sb, 0, 0.125), (Kp, wk_sb, 8, None), (Vp, wv_sb, 16, None)):
                c.tensor.wait_ge(c.sem["vD"], c.n["vD"])
                c.tensor.wait_ge(c.sem["wv"], c.n["wv"])
                ins = None
                for cc in range(8):
                    ins = c.tensor.matmul(psP[:, :], wsb[:, cc * 128:(cc + 1) * 128],
                                          xnc[:, parx * 4096 + cc * 512: parx * 4096 + (cc + 1) * 512],
                                          start=(cc == 0), stop=(cc == 7))
                ins.then_inc(*_b(c, "tP"))
                c.vector.wait_ge(c.sem["tP"], c.n["tP"])
                _sv = c.bump("vD")
                if scale is None:
                    c.vector.tensor_scalar(out=dst[:, tb0:tb0 + 512], in0=psP[:, :],
                                           scalar1=bqkv[:, bcol + p:bcol + p + 1], scalar2=None,
                                           op0=ALU.add).then_inc(*_sv)
                else:
                    c.vector.tensor_scalar(out=dst[:, tb0:tb0 + 512], in0=psP[:, :],
                                           scalar1=bqkv[:, bcol + p:bcol + p + 1], scalar2=scale,
                                           op0=ALU.add, op1=ALU.mult).then_inc(*_sv)
            xnc_free_at[parx] = c.n["tP"]

        if stage < 2:
            continue
        c.tensor.wait_ge(c.sem["vD"], c.n["vD"])
        for kc in range(32):
            for hh in range(2):
                _sp = c.bump("tP")
                c.tensor.transpose(psT[:, hh * 512:hh * 512 + 64],
                                   Vp[hh * 64:(hh + 1) * 64, kc * 128:(kc + 1) * 128],
                                   b["identsb"][hh * 64:(hh + 1) * 64, hh * 64:(hh + 1) * 64]).then_inc(*_sp)
                c.tensor.wait_ge(c.sem["tP"], c.n["tP"])
            c.vector.wait_ge(c.sem["tP"], c.n["tP"])
            _sv = c.bump("vD")
            c.vector.tensor_copy(out=vt[:, kc * 130: kc * 130 + 64], in_=psT[:, 0:64]).then_inc(*_sv)
            _sv = c.bump("vD")
            c.vector.tensor_copy(out=vt[:, kc * 130 + 65: kc * 130 + 129], in_=psT[:, 512:576]).then_inc(*_sv)
            c.tensor.wait_ge(c.sem["vD"], c.n["vD"])

        if stage < 3:
            continue
        for hh in range(2):
            hbase = hh * 64
            for qt in range(n_qt):
                q0 = qt * 128
                for half in range(2):
                    k0 = half * 2048
                    c.tensor.wait_ge(c.sem["sA"], c.n["sA"])
                    ins = None
                    for kk in range(4):
                        ins = c.tensor.matmul(psS[:, kk * 512:(kk + 1) * 512],
                                              Qp[hbase:hbase + 64, q0:q0 + 128],
                                              Kp[hbase:hbase + 64, k0 + kk * 512: k0 + (kk + 1) * 512],
                                              start=True, stop=True)
                    ins.then_inc(*_b(c, "tP"))
                    c.scalar.wait_ge(c.sem["tP"], c.n["tP"])
                    _ss = c.bump("sA")
                    c.scalar.activation(out=eS[:, k0:k0 + 2048], in_=psS[:, :],
                                        func=AF.Exp, bias=0.0, scale=1.0).then_inc(*_ss)
                if stage < 4:
                    continue
                c.vector.wait_ge(c.sem["sA"], c.n["sA"])
                src = eS
                dst = pA
                for rnd in range(4):
                    _sv = c.bump("vD")
                    c.vector.max(out=t8[:, :], in_=src[:, :]).then_inc(*_sv)
                    c.vector.wait_ge(c.sem["vD"], c.n["vD"])
                    _sv = c.bump("vD")
                    c.vector.match_replace(out=dst[:, :], in_to_replace=t8[:, :],
                                           in_values=src[:, :], imm_value=0.0).then_inc(*_sv)
                    c.vector.wait_ge(c.sem["vD"], c.n["vD"])
                    src, dst = dst, (pB if dst is pA else pA)
                _sv = c.bump("vD")
                c.vector.tensor_tensor(out=dst[:, :], in0=eS[:, :], in1=src[:, :],
                                       op=ALU.subtract).then_inc(*_sv)
                Pq = dst
                if stage < 5:
                    continue
                c.tensor.wait_ge(c.sem["vD"], c.n["vD"])
                for kg in range(8):
                    for j in range(4):
                        _sp = c.bump("tP")
                        c.tensor.transpose(psT[:, j * 128:(j + 1) * 128],
                                           Pq[:, (kg * 4 + j) * 128:(kg * 4 + j + 1) * 128],
                                           b["identsb"][:, :]).then_inc(*_sp)
                    c.vector.wait_ge(c.sem["tP"], c.n["tP"])
                    _sv = c.bump("vD")
                    c.vector.tensor_copy(out=pA[:, kg * 512:(kg + 1) * 512] if Pq is pB else pB[:, kg * 512:(kg + 1) * 512],
                                         in_=psT[:, 0:512]).then_inc(*_sv)
                    c.tensor.wait_ge(c.sem["vD"], c.n["vD"])
                Pt = pA if Pq is pB else pB
                if stage < 6:
                    continue
                c.tensor.wait_ge(c.sem["vD"], c.n["vD"])
                ins = None
                for kc in range(32):
                    ins = c.tensor.matmul(psPV[0:65, 0:128],
                                          vt[:, kc * 130 + hh * 65: kc * 130 + hh * 65 + 65],
                                          Pt[:, kc * 128:(kc + 1) * 128],
                                          start=(kc == 0), stop=(kc == 31))
                ins.then_inc(*_b(c, "tP"))
                if stage < 7:
                    continue
                c.vector.wait_ge(c.sem["tP"], c.n["tP"])
                _sv = c.bump("vD")
                c.vector.reciprocal(out=zrow[:, :], in_=psPV[64:65, 0:128]).then_inc(*_sv)
                c.tensor.wait_ge(c.sem["vD"], c.n["vD"])
                _sp = c.bump("tP")
                c.tensor.matmul(psB[0:64, 0:128], ones1[:, :], zrow[:, :],
                                start=True, stop=True).then_inc(*_sp)
                c.vector.wait_ge(c.sem["tP"], c.n["tP"])
                _sv = c.bump("vD")
                c.vector.tensor_copy(out=zbc[:, :], in_=psB[0:64, 0:128]).then_inc(*_sv)
                c.vector.wait_ge(c.sem["vD"], c.n["vD"])
                _sv = c.bump("vD")
                c.vector.tensor_tensor(out=acc[hbase:hbase + 64, q0:q0 + 128],
                                       in0=psPV[0:64, 0:128], in1=zbc[:, :], op=ALU.mult).then_inc(*_sv)
        if stage < 8:
            continue
        c.wait("tensor", wow)
        c.tensor.wait_ge(c.sem["vD"], c.n["vD"])
        for fo in range(8):
            for tb in range(8):
                tb0 = tb * 512
                c.tensor.wait_ge(c.sem["wv"], c.n["wv"])
                _sp = c.bump("tP")
                c.tensor.matmul(psP[:, :], wo_sb[:, fo * 128:(fo + 1) * 128],
                                acc[:, :][0:128, tb0:tb0 + 512],
                                start=True, stop=True).then_inc(*_sp)
                c.vector.wait_ge(c.sem["tP"], c.n["tP"])
                _sw = c.bump("wv")
                if p == 0:
                    c.vector.tensor_copy(out=b["wacc"][:, fo * 4096 + tb0: fo * 4096 + tb0 + 512],
                                         in_=psP[:, :]).then_inc(*_sw)
                else:
                    c.vector.tensor_tensor(out=b["wacc"][:, fo * 4096 + tb0: fo * 4096 + tb0 + 512],
                                           in0=b["wacc"][:, fo * 4096 + tb0: fo * 4096 + tb0 + 512],
                                           in1=psP[:, :], op=ALU.add).then_inc(*_sw)
    for fo in range(8):
        for tb in range(8):
            tb0 = tb * 512
            pp = tb % 2
            yw = c.sdma(y1t[:, pp * 512:(pp + 1) * 512],
                        y1_hbm[fo * 128:(fo + 1) * 128, tb0:tb0 + 512])
            c.wait("vector", yw)
            c.vector.wait_ge(c.sem["wv"], c.n["wv"])
            _sv = c.bump("vD")
            c.vector.scalar_tensor_tensor(
                out=x1t[:, pp * 512:(pp + 1) * 512],
                in0=b["wacc"][:, fo * 4096 + tb0: fo * 4096 + tb0 + 512],
                scalar=bo_sb[:, fo:fo + 1], in1=y1t[:, pp * 512:(pp + 1) * 512],
                op0=ALU.add, op1=ALU.add).then_inc(*_sv)
            c.sync.wait_ge(c.sem["vD"], c.n["vD"])
            stw = c.sdma(x1_hbm[fo * 128:(fo + 1) * 128, tb0:tb0 + 512],
                         x1t[:, pp * 512:(pp + 1) * 512])
            c.wait("vector", stw)
    c.drain_dmas()


def build_launch3(n_pairs=8, n_qt=32, stage=99, v2=False):
    nc = bass.Bass("TRN2", target_bir_lowering=False, debug=False)
    xnf = nc.dram_tensor("xnf", [F, T], BF16, kind="ExternalInput").ap()
    y1 = nc.dram_tensor("y1", [F, T], BF16, kind="ExternalInput").ap()
    WqT = nc.dram_tensor("WqT", [F, F], F32, kind="ExternalInput").ap()
    WkT = nc.dram_tensor("WkT", [F, F], F32, kind="ExternalInput").ap()
    WvT = nc.dram_tensor("WvT", [F, F], F32, kind="ExternalInput").ap()
    WoT = nc.dram_tensor("WoT", [F, F], F32, kind="ExternalInput").ap()
    bqkv_hbm = nc.dram_tensor("bqkv", [128, 24], F32, kind="ExternalInput").ap()
    bo_hbm = nc.dram_tensor("bo", [128, 8], F32, kind="ExternalInput").ap()
    ident = nc.dram_tensor("ident", [128, 128], BF16, kind="ExternalInput").ap()
    identf32_h = nc.dram_tensor("identf32", [128, 128], F32, kind="ExternalInput").ap()
    x1 = nc.dram_tensor("x1", [F, T], BF16, kind="ExternalOutput").ap()
    with contextlib.ExitStack() as stk:
        sb = lambda name, shape, dt: stk.enter_context(nc.sbuf_tensor(name, shape, dt))
        bufs = {
            "xnf": xnf, "y1": y1, "x1": x1,
            "WqT": WqT, "WkT": WkT, "WvT": WvT, "WoT": WoT,
            "bqkv_hbm": bqkv_hbm, "bo_hbm": bo_hbm,
            "bqkv": sb("bqkvs", [128, 24], F32),
            "bo_sb": sb("bos", [128, 8], F32),
            "xnc": sb("xnc", [128, 2 * 4096], BF16),
            "wq_sb": sb("wq_sb", [128, 1024], BF16),
            "wk_sb": sb("wk_sb", [128, 1024], BF16),
            "wv_sb": sb("wv_sb", [128, 1024], BF16),
            "wo_sb": sb("wo_sb", [128, 1024], BF16),
            "Qp": sb("Qp", [128, 4096], BF16),
            "Kp": sb("Kp", [128, 4096], BF16),
            "Vp": sb("Vp", [128, 4096], BF16),
            "vt": sb("vt", [128, 32 * 130], BF16),
            "eS": sb("eS", [128, 4096], BF16),
            "pA": sb("pA", [128, 4096], BF16),
            "pB": sb("pB", [128, 4096], BF16),
            "eS2": sb("eS2", [128, 4096], BF16),
            "pA2": sb("pA2", [128, 4096], BF16),
            "pB2": sb("pB2", [128, 4096], BF16),
            "t8": sb("t8", [128, 8], BF16),
            "zrow": sb("zrow", [1, 128], F32),
            "zbc": sb("zbc", [64, 128], F32),
            "zi": sb("zi", [128, 4], F32),
            "ones1": sb("ones1", [1, 64], F32),
            "e65": sb("e65", [65, 128], F32),
            "qn_sb": sb("qn_sb", [128, 64], BF16),
            "rzt": sb("rzt", [128, 1], F32),
            "lnt": sb("lnt", [128, 1], F32),
            "wacc": sb("wacc", [128, 8 * 4096], BF16),
            "wo_all": sb("wo_all", [128, 8 * 1024], BF16),
            "y1t": sb("y1t", [128, 2 * 512], BF16),
            "x1t": sb("x1t", [128, 2 * 512], BF16),
            "identsb": sb("identsb", [128, 128], BF16),
            "identf": sb("idf_sb", [128, 128], F32),
            "psS": stk.enter_context(nc.psum_tensor("psS", [128, 2048], F32)),
            "psT": stk.enter_context(nc.psum_tensor("psT", [128, 1024], BF16)),
            "psPV": stk.enter_context(nc.psum_tensor("psPV", [128, 512], F32)),
            "psB": stk.enter_context(nc.psum_tensor("psB", [128, 512], F32)),
            "psP": stk.enter_context(nc.psum_tensor("psP", [128, 1024], BF16)),
        }
        bufs["ident"] = ident

        def prog(c):
            idw = c.sdma(bufs["identsb"][:, :], ident[:, :])
            idw2 = c.sdma(bufs["identf"][:, :], identf32_h[:, :])
            c.wait("tensor", idw)
            c.wait("tensor", idw2)
            if v2:
                emit_l0_attn_v2(c, bufs, n_pairs=n_pairs)
            else:
                emit_l0_attn(c, bufs, n_pairs=n_pairs, n_qt=n_qt, stage=stage)

        run_program(nc, prog, ["tP", "vD", "sA", "wv", "scA", "plV", "tv", "ev2"])
    return nc



def emit_ln_fm(c, b, x_tiles, xn_tiles):
    psA = b["psA"]; psB = b["psB"]
    rows = b["rows"]
    sq = b["sq"]
    t1 = b["t1"]
    onesb = b["onesb"]
    ones1rF = b["ones1rF"]
    eps1 = b["eps1"]
    S = 512

    for tb in range(8):
        t0 = tb * S
        c.tensor.wait_ge(c.sem["vD"], c.n["vD"])
        c.tensor.wait_ge(c.sem["sA"], c.n["sA"])
        ins = None
        for ch in range(8):
            ins = c.tensor.matmul(psA[0:1, :], onesb[:, :], x_tiles[ch][:, t0:t0 + S],
                                  start=(ch == 0), stop=(ch == 7))
        ins.then_inc(*_b(c, "tP"))
        for ch in range(8):
            if ch > 0:
                c.vector.wait_ge(c.sem["tP"], c.n["tP"])
            _sv = c.bump("vD")
            c.vector.tensor_tensor(out=sq[:, :], in0=x_tiles[ch][:, t0:t0 + S],
                                   in1=x_tiles[ch][:, t0:t0 + S], op=ALU.mult).then_inc(*_sv)
            c.tensor.wait_ge(c.sem["vD"], c.n["vD"])
            ins = c.tensor.matmul(psB[0:1, :], onesb[:, :], sq[:, :],
                                  start=(ch == 0), stop=(ch == 7))
            ins.then_inc(*_b(c, "tP"))
        c.vector.wait_ge(c.sem["tP"], c.n["tP"])
        _sv = c.bump("vD")
        c.vector.tensor_scalar(out=rows[0:1, 2 * S:3 * S], in0=psA[0:1, :], scalar1=1.0 / F,
                               scalar2=None, op0=ALU.mult).then_inc(*_sv)
        c.vector.wait_ge(c.sem["vD"], c.n["vD"])
        _sv = c.bump("vD")
        c.vector.tensor_tensor(out=rows[0:1, 3 * S:4 * S], in0=rows[0:1, 2 * S:3 * S],
                               in1=rows[0:1, 2 * S:3 * S], op=ALU.mult).then_inc(*_sv)
        c.vector.wait_ge(c.sem["vD"], c.n["vD"])
        _sv = c.bump("vD")
        c.vector.scalar_tensor_tensor(out=rows[0:1, 1 * S:2 * S], in0=psB[0:1, :], scalar=1.0 / F,
                                      in1=rows[0:1, 3 * S:4 * S], op0=ALU.mult,
                                      op1=ALU.subtract).then_inc(*_sv)
        c.scalar.wait_ge(c.sem["vD"], c.n["vD"])
        _ss = c.bump("sA")
        c.scalar.activation(out=rows[0:1, 0:S], in_=rows[0:1, 1 * S:2 * S], func=AF.Ln,
                            bias=eps1[:, :], scale=1.0).then_inc(*_ss)
        c.scalar.wait_ge(c.sem["sA"], c.n["sA"])
        _ss = c.bump("sA")
        c.scalar.activation(out=rows[0:1, 3 * S:4 * S], in_=rows[0:1, 0:S], func=AF.Exp,
                            bias=0.0, scale=-0.5).then_inc(*_ss)
        c.vector.wait_ge(c.sem["sA"], c.n["sA"])
        _sv = c.bump("vD")
        c.vector.tensor_tensor(out=rows[0:1, 1 * S:2 * S], in0=rows[0:1, 2 * S:3 * S],
                               in1=rows[0:1, 3 * S:4 * S], op=ALU.mult).then_inc(*_sv)
        c.tensor.wait_ge(c.sem["vD"], c.n["vD"])
        _sp = c.bump("tP")
        c.tensor.matmul(psA[:, :], ones1rF[:, :], rows[0:1, 3 * S:4 * S],
                        start=True, stop=True)
        c.tensor.matmul(psB[:, :], ones1rF[:, :], rows[0:1, 1 * S:2 * S],
                        start=True, stop=True).then_inc(*_sp)
        c.vector.wait_ge(c.sem["tP"], c.n["tP"])
        for ch in range(8):
            c.vector.wait_ge(c.sem["vD"], c.n["vD"])
            _sv = c.bump("vD")
            c.vector.tensor_tensor(out=t1[:, :], in0=x_tiles[ch][:, t0:t0 + S],
                                   in1=psA[:, :], op=ALU.mult).then_inc(*_sv)
            c.vector.wait_ge(c.sem["vD"], c.n["vD"])
            _sv = c.bump("vD")
            c.vector.tensor_tensor(out=xn_tiles[ch][:, t0:t0 + S], in0=t1[:, :],
                                   in1=psB[:, :], op=ALU.subtract).then_inc(*_sv)



def emit_l1_pre(c, b, stage=99):
    x1_hbm = b["x1"]; xn1_hbm = b["xn1"]
    q_hbm = {"q": b["qf"], "k": b["kf"], "v": b["vf"]}
    Wq = {"q": b["Wq"], "k": b["Wk"], "v": b["Wv"]}
    convr = b["convr"]
    x1_t = [b["x1_sb"][:, ch * T:(ch + 1) * T] for ch in range(8)]
    xn_t = [b["xn_sb"][:, ch * T:(ch + 1) * T] for ch in range(8)]
    w_sb = b["w_sb"]
    ch_a = b["ch_a"]
    ch_b = b["ch_b"]
    lrs = b["lr_sb"]
    eps3 = b["eps3"]
    psP = b["psA"]
    psL = b["psB"]

    x1w = c.dma_group("sync", [(x1_t[ch], x1_hbm[ch * 128:(ch + 1) * 128, :])
                               for ch in range(8)])
    c.wait("vector", x1w)
    c.wait("tensor", x1w)
    emit_ln_fm(c, b, x1_t, xn_t)
    for ch in range(8):
        c.sync.wait_ge(c.sem["vD"], c.n["vD"])
        c.sdma(xn1_hbm[ch * 128:(ch + 1) * 128, :], xn_t[ch])

    if stage < 2:
        c.drain_dmas()
        return
    wlrw = c.gdma(b["wlr_sb"][:, :].rearrange("p (cc o) -> p cc o", o=32),
                  b["Wlr"][:, :].rearrange("(cc p) o -> p cc o", p=128))
    c.wait("tensor", wlrw)
    for tb in range(8):
        t0 = tb * 512
        c.tensor.wait_ge(c.sem["sA"], c.n["sA"])
        c.tensor.wait_ge(c.sem["vD"], c.n["vD"])
        ins = None
        for cc in range(8):
            ins = c.tensor.matmul(psL[0:32, :], b["wlr_sb"][:, cc * 32:(cc + 1) * 32],
                                  xn_t[cc][:, t0:t0 + 512], start=(cc == 0), stop=(cc == 7))
        ins.then_inc(*_b(c, "tP"))
        c.vector.wait_ge(c.sem["tP"], c.n["tP"])
        _sv = c.bump("vD")
        c.vector.tensor_scalar(out=b["lrt3"][:, 0:512], in0=psL[0:32, :], scalar1=eps3[:, :],
                               scalar2=0.0, op0=ALU.add, op1=ALU.max).then_inc(*_sv)
        c.scalar.wait_ge(c.sem["vD"], c.n["vD"])
        _ss = c.bump("sA")
        c.scalar.activation(out=b["lrt1"][:, 0:512],
                            in_=psL[0:32, :], func=AF.Abs, bias=eps3[:, :], scale=1.0).then_inc(*_ss)
        c.scalar.wait_ge(c.sem["sA"], c.n["sA"])
        _ss = c.bump("sA")
        c.scalar.activation(out=b["lrt2"][:, 0:512], in_=b["lrt1"][:, 0:512], func=AF.Exp,
                            bias=0.0, scale=-1.0).then_inc(*_ss)
        c.scalar.wait_ge(c.sem["sA"], c.n["sA"])
        _ss = c.bump("sA")
        c.scalar.activation(out=b["lrt1"][:, 0:512], in_=b["lrt2"][:, 0:512], func=AF.Ln,
                            bias=1.0, scale=1.0).then_inc(*_ss)
        c.vector.wait_ge(c.sem["sA"], c.n["sA"])
        c.vector.wait_ge(c.sem["vD"], c.n["vD"])
        _sv = c.bump("vD")
        c.vector.tensor_tensor(out=lrs[0:32, t0:t0 + 512], in0=b["lrt3"][:, 0:512],
                               in1=b["lrt1"][:, 0:512], op=ALU.add).then_inc(*_sv)
    c.sync.wait_ge(c.sem["vD"], c.n["vD"])
    c.sdma(b["lrf"][:, :], lrs[0:32, 0:T])

    if stage < 3:
        c.drain_dmas()
        return
    for pi, pk in enumerate(("q", "k", "v")):
        for oc in range(8):
            par = (pi * 8 + oc) % 2
            c.gpsimd.wait_ge(c.sem["tP"], c.n["tP"])
            ww = c.gdma(w_sb[:, par * 1024:(par + 1) * 1024].rearrange("p (cc o) -> p cc o", o=128),
                        Wq[pk][:, oc * 128:(oc + 1) * 128].rearrange("(cc p) o -> p cc o", p=128))
            c.wait("tensor", ww)
            for tb in range(8):
                t0 = tb * 512
                c.tensor.wait_ge(c.sem["vD"], c.n["vD"])
                ins = None
                for cc in range(8):
                    ins = c.tensor.matmul(psP[:, :], w_sb[:, par * 1024 + cc * 128:par * 1024 + (cc + 1) * 128],
                                          xn_t[cc][:, t0:t0 + 512], start=(cc == 0), stop=(cc == 7))
                ins.then_inc(*_b(c, "tP"))
                c.vector.wait_ge(c.sem["tP"], c.n["tP"])
                _sv = c.bump("vD")
                c.vector.tensor_copy(out=ch_a[:, t0:t0 + 512], in_=psP[:, :]).then_inc(*_sv)
            base = pi * 32 + oc * 4
            c.vector.wait_ge(c.sem["vD"], c.n["vD"])
            _sv = c.bump("vD")
            c.vector.scalar_tensor_tensor(out=ch_b[:, :], in0=ch_a[:, :],
                                          scalar=convr[:, base + 3:base + 4], in1=ch_a[:, :],
                                          op0=ALU.mult, op1=ALU.add).then_inc(*_sv)
            for j in (2, 1, 0):
                sh = 3 - j
                c.vector.wait_ge(c.sem["vD"], c.n["vD"])
                _sv = c.bump("vD")
                c.vector.scalar_tensor_tensor(out=ch_b[:, sh:T], in0=ch_a[:, 0:T - sh],
                                              scalar=convr[:, base + j:base + j + 1],
                                              in1=ch_b[:, sh:T],
                                              op0=ALU.mult, op1=ALU.add).then_inc(*_sv)
            c.sync.wait_ge(c.sem["vD"], c.n["vD"])
            stw = c.sdma(q_hbm[pk][oc * 128:(oc + 1) * 128, :], ch_b[:, :])
            c.wait("vector", stw)
    c.drain_dmas()


def build_launch4(stage=99):
    nc = bass.Bass("TRN2", target_bir_lowering=False, debug=False)
    x1 = nc.dram_tensor("x1", [F, T], BF16, kind="ExternalInput").ap()
    Wq_ = nc.dram_tensor("Wq", [F, F], F32, kind="ExternalInput").ap()
    Wk_ = nc.dram_tensor("Wk", [F, F], F32, kind="ExternalInput").ap()
    Wv_ = nc.dram_tensor("Wv", [F, F], F32, kind="ExternalInput").ap()
    Wlr = nc.dram_tensor("Wlr", [F, 32], F32, kind="ExternalInput").ap()
    convr_h = nc.dram_tensor("convr", [128, 96], F32, kind="ExternalInput").ap()
    xn1 = nc.dram_tensor("xn1", [F, T], BF16, kind="ExternalOutput").ap()
    qf = nc.dram_tensor("qf", [F, T], BF16, kind="ExternalOutput").ap()
    kf = nc.dram_tensor("kf", [F, T], BF16, kind="ExternalOutput").ap()
    vf = nc.dram_tensor("vf", [F, T], BF16, kind="ExternalOutput").ap()
    lrf = nc.dram_tensor("lrf", [32, T], F32, kind="ExternalOutput").ap()
    with contextlib.ExitStack() as stk:
        sb = lambda name, shape, dt: stk.enter_context(nc.sbuf_tensor(name, shape, dt))
        bufs = {
            "x1": x1, "xn1": xn1, "qf": qf, "kf": kf, "vf": vf, "lrf": lrf,
            "Wq": Wq_, "Wk": Wk_, "Wv": Wv_, "Wlr": Wlr,
            "x1_sb": sb("x1_sb", [128, 8 * T], BF16),
            "xn_sb": sb("xn_sb", [128, 8 * T], BF16),
            "w_sb": sb("w_sb", [128, 2 * 1024], BF16),
            "wlr_sb": sb("wlr_sb", [128, 256], BF16),
            "ch_a": sb("ch_a", [128, T], BF16),
            "ch_b": sb("ch_b", [128, T], BF16),
            "lr_sb": sb("lr_sb", [32, T + 32], F32),
            "lrt1": sb("lrt1", [32, 512], F32),
            "lrt2": sb("lrt2", [32, 512], F32),
            "lrt3": sb("lrt3", [32, 512], F32),
            "convr": sb("convrs", [128, 96], F32),
            "rows": sb("rows", [1, 4 * 512], F32),
            "sq": sb("sq", [128, 512], BF16),
            "t1": sb("t1", [128, 512], BF16),
            "onesb": sb("onesb", [128, 1], BF16),
            "ones1rF": sb("ones1rF", [1, 128], F32),
            "eps1": sb("eps1", [1, 1], F32),
            "eps3": sb("eps3", [32, 1], F32),
            "psA": stk.enter_context(nc.psum_tensor("psA", [128, 512], F32)),
            "psB": stk.enter_context(nc.psum_tensor("psB", [128, 512], F32)),
        }

        def prog(c):
            _sv = c.bump("vD")
            c.vector.memset(bufs["onesb"][:, :], 1.0)
            c.vector.memset(bufs["ones1rF"][:, :], 1.0)
            c.vector.memset(bufs["eps1"][:, :], 1e-5)
            c.vector.memset(bufs["eps3"][:, :], 0.001).then_inc(*_sv)
            cw = c.sdma(bufs["convr"][:, :], convr_h[:, :])
            c.wait("vector", cw)
            c.tensor.wait_ge(c.sem["vD"], c.n["vD"])
            emit_l1_pre(c, bufs, stage=stage)

        run_program(nc, prog, ["tP", "vD", "sA"])
    return nc



def emit_l1_ffn_final(c, b):
    xm = b["xm_sb"]
    S = 512
    w1_free = {0: 0, 1: 0}
    x1t_guard = [0]
    w2ws = []
    for p9 in range(9):
        w2ws.append(c.gdma(b["w2_sb"][:, p9 * 2048:(p9 + 1) * 2048].rearrange("p (mc o) -> p mc o", o=1024),
                           b["W2"][p9].rearrange("(mc p) o -> p mc o", p=128)))
    wo1w = c.gdma(b["wo1_sb"][:, :].rearrange("p (cc m) -> p cc m", m=256),
                  b["Wo1"][:, :].rearrange("(cc p) m -> p cc m", p=128))
    wo2w = c.gdma(b["wo2_sb"][:, :].rearrange("p (mc s) -> p mc s", s=9),
                  b["Wo2"][:, :].rearrange("(mc p) s -> p mc s", p=128))
    b2w = c.gdma(b["b2_sb"][0:9, :], b["b2"][:, :])
    wow = c.gdma(b["wo_sb2"][:, :].rearrange("p (h o) -> p h o", o=1024),
                 b["WoL1"][:, :].rearrange("(h p) o -> p h o", p=64))
    smallw = c.dma_group("sync", [
        (b["b1_sb"][:, :], b["b1r"][:, :]),
        (b["gb_sb"][:, :], b["gbr"][:, :]),
        (b["mixr_sb"][:, :], b["mixr"][:, :]),
        (b["bo1_sb"][:, :], b["bo1r"][:, :]),
        (b["identf"][:, :], b["identf32"][:, :]),
        (b["selbc_sb"][0:9, :], b["selbc"][:, :]),
    ])
    bo2w = c.gdma(b["bo2_sb"][0:1, :], b["bo2r"][:, :])
    c.wait("tensor", bo2w)
    c.wait("vector", smallw)
    c.wait("tensor", smallw)
    c.wait("scalar", smallw)

    for tb in range(8):
        t0 = tb * S
        hl = 1 if t0 > 0 else 0
        hr = 1 if t0 + S < T else 0
        ln_ = S + hl + hr
        c.sync.wait_ge(c.sem["zz"], c.n["zz"])
        c.sync.wait_ge(c.sem["vD"], c.n["vD"])
        xnw = c.dma_group("sync", [
            (b["xntb"][:, ch * 520:ch * 520 + ln_],
             b["xn1"][ch * 128:(ch + 1) * 128, t0 - hl:t0 - hl + ln_]) for ch in range(8)])
        c.wait("vector", xnw)
        c.wait("tensor", xnw)
        xn_t = [b["xntb"][:, ch * 520 + hl: ch * 520 + hl + S] for ch in range(8)]
        for ch in range(8):
            xs_full = b["xntb"][:, ch * 520: ch * 520 + ln_]
            xmc = xm[:, ch * S:(ch + 1) * S]
            c.vector.wait_ge(c.sem["tP"], c.n["tP"])
            _sv = c.bump("vD")
            c.vector.tensor_scalar(out=xmc, in0=xn_t[ch],
                                   scalar1=b["mixr_sb"][:, ch * 3 + 1:ch * 3 + 2], scalar2=None,
                                   op0=ALU.mult).then_inc(*_sv)
            j0 = 1 - hl if hl == 0 else 0
            j0 = 0 if hl == 1 else 1
            nl = S - j0
            c.vector.wait_ge(c.sem["vD"], c.n["vD"])
            _sv = c.bump("vD")
            c.vector.scalar_tensor_tensor(out=xmc[:, j0:j0 + nl],
                                          in0=b["xntb"][:, ch * 520 + hl + j0 - 1: ch * 520 + hl + j0 - 1 + nl],
                                          scalar=b["mixr_sb"][:, ch * 3 + 0:ch * 3 + 1],
                                          in1=xmc[:, j0:j0 + nl],
                                          op0=ALU.mult, op1=ALU.add).then_inc(*_sv)
            nr = S - (1 - hr)
            c.vector.wait_ge(c.sem["vD"], c.n["vD"])
            _sv = c.bump("vD")
            c.vector.scalar_tensor_tensor(out=xmc[:, 0:nr],
                                          in0=b["xntb"][:, ch * 520 + hl + 1: ch * 520 + hl + 1 + nr],
                                          scalar=b["mixr_sb"][:, ch * 3 + 2:ch * 3 + 3],
                                          in1=xmc[:, 0:nr],
                                          op0=ALU.mult, op1=ALU.add).then_inc(*_sv)
        if tb == 0:
            c.wait("tensor", wo1w)
            c.wait("tensor", wo2w)
            c.wait("tensor", b2w)
        for mc in range(2):
            c.tensor.wait_ge(c.sem["vD"], c.n["vD"])
            c.tensor.wait_ge(c.sem["sA"], c.n["sA"])
            ins = None
            for cc in range(8):
                ins = c.tensor.matmul(b["psA"][:, :],
                                      b["wo1_sb"][:, cc * 256 + mc * 128: cc * 256 + (mc + 1) * 128],
                                      xm[:, cc * S:(cc + 1) * S],
                                      start=(cc == 0), stop=(cc == 7))
            ins.then_inc(*_b(c, "tP"))
            c.scalar.wait_ge(c.sem["tP"], c.n["tP"])
            _ss = c.bump("sA")
            c.scalar.activation(out=b["th_sb"][:, mc * S:(mc + 1) * S], in_=b["psA"][:, :],
                                func=AF.Tanh, bias=b["bo1_sb"][:, mc:mc + 1], scale=1.0).then_inc(*_ss)
        c.tensor.wait_ge(c.sem["sA"], c.n["sA"])
        for ts4 in range(4):
            c.tensor.wait_ge(c.sem["vD"], c.n["vD"])
            for mc in range(2):
                c.tensor.matmul(b["psB"][0:128, 0:9],
                                b["th_sb"][:, mc * S + ts4 * 128: mc * S + (ts4 + 1) * 128],
                                b["wo2_sb"][:, mc * 9:(mc + 1) * 9],
                                start=(mc == 0), stop=False)
            ins = c.tensor.matmul(b["psB"][0:128, 0:9], b["ones1rB"][:, :],
                                  b["bo2_sb"][0:1, :], start=False, stop=True)
            ins.then_inc(*_b(c, "tP"))
            c.vector.wait_ge(c.sem["tP"], c.n["tP"])
            _sv = c.bump("vD")
            c.vector.reduce_max(out=b["mx_sb"][:, 0:1], in_=b["psB"][0:128, 0:9], axis=AX.X).then_inc(*_sv)
            c.vector.wait_ge(c.sem["vD"], c.n["vD"])
            _sv = c.bump("vD")
            c.vector.tensor_scalar(out=b["oh_sb"][:, 0:9], in0=b["psB"][0:128, 0:9],
                                   scalar1=b["mx_sb"][:, 0:1], scalar2=None,
                                   op0=ALU.is_ge).then_inc(*_sv)
            c.tensor.wait_ge(c.sem["vD"], c.n["vD"])
            _sp = c.bump("tP")
            c.tensor.transpose(b["psOH"][0:9, ts4 * 128:(ts4 + 1) * 128], b["oh_sb"][:, 0:9],
                               b["identsb"][:, :]).then_inc(*_sp)
        c.vector.wait_ge(c.sem["tP"], c.n["tP"])
        _sv = c.bump("vD")
        c.vector.tensor_copy(out=b["ohT_sb"][0:9, 0:S], in_=b["psOH"][0:9, 0:S]).then_inc(*_sv)
        for p9 in range(9):
            par = p9 % 2
            c.gpsimd.wait_ge(c.sem["tP"], w1_free[par])
            w1w = c.gdma(b["w1_sb"][:, par * 2048:(par + 1) * 2048].rearrange("p (cc m) -> p cc m", m=256),
                         b["W1"][p9].rearrange("(cc p) m -> p cc m", p=128))
            c.tensor.wait_ge(c.sem["vD"], c.n["vD"])
            _sp = c.bump("tP")
            c.tensor.matmul(b["psB"][:, :], b["selbc_sb"][0:9, p9 * 128:(p9 + 1) * 128],
                            b["ohT_sb"][0:9, 0:S],
                            start=True, stop=True).then_inc(*_sp)
            c.vector.wait_ge(c.sem["tP"], c.n["tP"])
            _sv = c.bump("vD")
            c.vector.tensor_copy(out=b["ohbc_sb"][:, :], in_=b["psB"][:, :]).then_inc(*_sv)
            c.wait("tensor", w1w)
            for mc in range(2):
                c.tensor.wait_ge(c.sem["vD"], c.n["vD"])
                c.tensor.wait_ge(c.sem["sA"], c.n["sA"])
                ins = None
                for cc in range(8):
                    ins = c.tensor.matmul(b["psA"][:, :],
                                          b["w1_sb"][:, par * 2048 + cc * 256 + mc * 128: par * 2048 + cc * 256 + (mc + 1) * 128],
                                          xm[:, cc * S:(cc + 1) * S],
                                          start=(cc == 0), stop=(cc == 7))
                ins.then_inc(*_b(c, "tP"))
                c.vector.wait_ge(c.sem["tP"], c.n["tP"])
                _sv = c.bump("vD")
                c.vector.scalar_tensor_tensor(
                    out=b["ht_sb"][:, p9 * 1024 + mc * S: p9 * 1024 + mc * S + S],
                    in0=b["psA"][:, :], scalar=b["b1_sb"][:, p9 * 2 + mc:p9 * 2 + mc + 1],
                    in1=b["ohbc_sb"][:, :], op0=ALU.add, op1=ALU.mult).then_inc(*_sv)
            w1_free[par] = c.n["tP"]
        for p_ in w2ws:
            if tb == 0:
                c.wait("tensor", p_)
        c.tensor.wait_ge(c.sem["vD"], c.n["vD"])
        for fo in range(8):
            c.tensor.wait_ge(c.sem["zz"], c.n["zz"])
            ins = None
            for p9 in range(9):
                for mc in range(2):
                    ins = c.tensor.matmul(b["psC"][:, :],
                                          b["w2_sb"][:, p9 * 2048 + mc * 1024 + fo * 128: p9 * 2048 + mc * 1024 + (fo + 1) * 128],
                                          b["ht_sb"][:, p9 * 1024 + mc * S: p9 * 1024 + mc * S + S],
                                          start=(p9 == 0 and mc == 0), stop=False)
            ins = c.tensor.matmul(b["psC"][:, :], b["b2_sb"][0:9, fo * 128:(fo + 1) * 128],
                                  b["ohT_sb"][0:9, 0:S], start=False, stop=True)
            ins.then_inc(*_b(c, "tP"))
            c.vector.wait_ge(c.sem["tP"], c.n["tP"])
            _sz = c.bump("zz")
            c.vector.tensor_tensor(out=b["zb_sb"][:, fo * S:(fo + 1) * S],
                                   in0=b["psC"][:, :], in1=xn_t[fo],
                                   op=ALU.add).then_inc(*_sz)
        zt = [b["zb_sb"][:, fo * S:(fo + 1) * S] for fo in range(8)]
        emit_ln_gb(c, b, zt, tb)
        if tb == 0:
            c.wait("tensor", wow)
        osw = c.gdma(b["osc_sb"][0:64, :].rearrange("p (h t2) -> p h t2", t2=S),
                     b["o_scan"][:, :].rearrange("p (h t2) -> p h t2", t2=T)[:, :, t0:t0 + S])
        c.wait("tensor", osw)
        for fo in range(8):
            c.sync.wait_ge(c.sem["vD"], x1t_guard[0])
            x1w = c.sdma(b["x1t_sb"][:, :], b["x1"][fo * 128:(fo + 1) * 128, t0:t0 + S])
            c.tensor.wait_ge(c.sem["fz"], c.n["fz"])
            ins = None
            for hh in range(16):
                ins = c.tensor.matmul(b["psF"][:, :],
                                      b["wo_sb2"][:, hh * 1024 + fo * 128: hh * 1024 + (fo + 1) * 128],
                                      b["osc_sb"][0:64, hh * S:(hh + 1) * S],
                                      start=(hh == 0), stop=(hh == 15))
            ins.then_inc(*_b(c, "tP"))
            c.vector.wait_ge(c.sem["tP"], c.n["tP"])
            c.wait("vector", x1w)
            _sv = c.bump("vD")
            c.vector.tensor_tensor(out=b["of1_sb"][:, :], in0=b["psF"][:, :],
                                   in1=b["x1t_sb"][:, :], op=ALU.add).then_inc(*_sv)
            x1t_guard[0] = c.n["vD"]
            c.vector.wait_ge(c.sem["vD"], c.n["vD"])
            _sf = c.bump("fz")
            c.vector.tensor_tensor(out=b["ofm_sb"][:, fo * S:(fo + 1) * S].bitcast(F32) if False else b["ofm_sb"][:, fo * S:(fo + 1) * S],
                                   in0=b["of1_sb"][:, :], in1=b["fb_sb"][:, fo * S:(fo + 1) * S],
                                   op=ALU.add).then_inc(*_sf)
        c.tensor.wait_ge(c.sem["fz"], c.n["fz"])
        for ts4 in range(4):
            for fo in range(8):
                c.tensor.wait_ge(c.sem["ev"], c.n["ev"])
                _sp = c.bump("tP")
                c.tensor.transpose(b["psTf"][:, fo * 64:(fo + 1) * 64][0:128, 0:64] if False else b["psTf"][:, (fo % 4) * 128:(fo % 4 + 1) * 128],
                                   b["ofm_sb"][:, fo * S + ts4 * 128: fo * S + (ts4 + 1) * 128],
                                   b["identf"][:, :]).then_inc(*_sp)
                if fo % 4 == 3:
                    c.vector.wait_ge(c.sem["tP"], c.n["tP"])
                    _se = c.bump("ev")
                    c.vector.tensor_copy(out=b["otok_sb"][:, (fo - 3) * 128:(fo + 1) * 128],
                                         in_=b["psTf"][:, :]).then_inc(*_se)
            c.sync.wait_ge(c.sem["ev"], c.n["ev"])
            ow = c.sdma(b["out"][t0 + ts4 * 128: t0 + (ts4 + 1) * 128, :], b["otok_sb"][:, :])
            c.wait("vector", ow)
    c.drain_dmas()


def emit_ln_gb(c, b, z_tiles, tb):
    S = 512
    rows = b["rows"]
    c.tensor.wait_ge(c.sem["lnv"], c.n["lnv"])
    c.tensor.wait_ge(c.sem["vD"], c.n["vD"])
    c.tensor.wait_ge(c.sem["zz"], c.n["zz"])
    c.vector.wait_ge(c.sem["zz"], c.n["zz"])
    ins = None
    for ch in range(8):
        ins = c.tensor.matmul(b["psA"][0:1, :], b["onesb"][:, :], z_tiles[ch],
                              start=(ch == 0), stop=(ch == 7))
    ins.then_inc(*_b(c, "tP"))
    for ch in range(8):
        if ch > 0:
            c.vector.wait_ge(c.sem["tP"], c.n["tP"])
        _sv = c.bump("vD")
        c.vector.tensor_tensor(out=b["sq"][:, :], in0=z_tiles[ch], in1=z_tiles[ch],
                               op=ALU.mult).then_inc(*_sv)
        c.tensor.wait_ge(c.sem["vD"], c.n["vD"])
        ins = c.tensor.matmul(b["psB"][0:1, :], b["onesb"][:, :], b["sq"][:, :],
                              start=(ch == 0), stop=(ch == 7))
        ins.then_inc(*_b(c, "tP"))
    c.vector.wait_ge(c.sem["tP"], c.n["tP"])
    _sv = c.bump("vD")
    c.vector.tensor_scalar(out=rows[0:1, 2 * S:3 * S], in0=b["psA"][0:1, :], scalar1=1.0 / F,
                           scalar2=None, op0=ALU.mult).then_inc(*_sv)
    c.vector.wait_ge(c.sem["vD"], c.n["vD"])
    _sv = c.bump("vD")
    c.vector.tensor_tensor(out=rows[0:1, 3 * S:4 * S], in0=rows[0:1, 2 * S:3 * S],
                           in1=rows[0:1, 2 * S:3 * S], op=ALU.mult).then_inc(*_sv)
    c.vector.wait_ge(c.sem["vD"], c.n["vD"])
    _sv = c.bump("vD")
    c.vector.scalar_tensor_tensor(out=rows[0:1, 1 * S:2 * S], in0=b["psB"][0:1, :], scalar=1.0 / F,
                                  in1=rows[0:1, 3 * S:4 * S], op0=ALU.mult,
                                  op1=ALU.subtract).then_inc(*_sv)
    c.scalar.wait_ge(c.sem["vD"], c.n["vD"])
    _ss = c.bump("sA")
    c.scalar.activation(out=rows[0:1, 0:S], in_=rows[0:1, 1 * S:2 * S], func=AF.Ln,
                        bias=b["eps1"][:, :], scale=1.0).then_inc(*_ss)
    c.scalar.wait_ge(c.sem["sA"], c.n["sA"])
    _ss = c.bump("sA")
    c.scalar.activation(out=rows[0:1, 3 * S:4 * S], in_=rows[0:1, 0:S], func=AF.Exp,
                        bias=0.0, scale=-0.5).then_inc(*_ss)
    c.vector.wait_ge(c.sem["sA"], c.n["sA"])
    _sv = c.bump("vD")
    c.vector.tensor_tensor(out=rows[0:1, 1 * S:2 * S], in0=rows[0:1, 2 * S:3 * S],
                           in1=rows[0:1, 3 * S:4 * S], op=ALU.mult).then_inc(*_sv)
    c.tensor.wait_ge(c.sem["vD"], c.n["vD"])
    _sp = c.bump("tP")
    c.tensor.matmul(b["psA"][:, :], b["ones1rF"][:, :], rows[0:1, 3 * S:4 * S],
                    start=True, stop=True)
    c.tensor.matmul(b["psB"][:, :], b["ones1rF"][:, :], rows[0:1, 1 * S:2 * S],
                    start=True, stop=True).then_inc(*_sp)
    c.vector.wait_ge(c.sem["tP"], c.n["tP"])
    for ch in range(8):
        c.vector.wait_ge(c.sem["vD"], c.n["vD"])
        c.vector.wait_ge(c.sem["lnv"], c.n["lnv"])
        _sv = c.bump("vD")
        c.vector.tensor_tensor(out=b["t1"][:, :], in0=z_tiles[ch],
                               in1=b["psA"][:, :], op=ALU.mult).then_inc(*_sv)
        c.vector.wait_ge(c.sem["vD"], c.n["vD"])
        _sv = c.bump("vD")
        c.vector.tensor_tensor(out=b["t1"][:, :], in0=b["t1"][:, :],
                               in1=b["psB"][:, :], op=ALU.subtract).then_inc(*_sv)
        c.vector.wait_ge(c.sem["vD"], c.n["vD"])
        _sl = c.bump("lnv")
        c.vector.tensor_scalar(out=b["fb_sb"][:, ch * S:(ch + 1) * S], in0=b["t1"][:, :],
                               scalar1=b["gb_sb"][:, ch:ch + 1],
                               scalar2=b["gb_sb"][:, 8 + ch:9 + ch],
                               op0=ALU.mult, op1=ALU.add).then_inc(*_sl)


def build_launch6():
    nc = bass.Bass("TRN2", target_bir_lowering=False, debug=False)
    x1 = nc.dram_tensor("x1", [F, T], BF16, kind="ExternalInput").ap()
    xn1 = nc.dram_tensor("xn1", [F, T], BF16, kind="ExternalInput").ap()
    o_scan = nc.dram_tensor("o_scan", [64, 16 * T], F32, kind="ExternalInput").ap()
    WoL1 = nc.dram_tensor("WoL1", [F, F], F32, kind="ExternalInput").ap()
    W1 = nc.dram_tensor("W1", [9, F, 256], F32, kind="ExternalInput").ap()
    W2 = nc.dram_tensor("W2", [9, 256, F], F32, kind="ExternalInput").ap()
    Wo1 = nc.dram_tensor("Wo1", [F, 256], F32, kind="ExternalInput").ap()
    Wo2 = nc.dram_tensor("Wo2", [256, 9], F32, kind="ExternalInput").ap()
    b1r = nc.dram_tensor("b1r", [128, 18], F32, kind="ExternalInput").ap()
    b2 = nc.dram_tensor("b2", [9, F], F32, kind="ExternalInput").ap()
    bo1r = nc.dram_tensor("bo1r", [128, 2], F32, kind="ExternalInput").ap()
    bo2r = nc.dram_tensor("bo2r", [1, 9], F32, kind="ExternalInput").ap()
    gbr = nc.dram_tensor("gbr", [128, 16], F32, kind="ExternalInput").ap()
    mixr = nc.dram_tensor("mixr", [128, 24], F32, kind="ExternalInput").ap()
    ident = nc.dram_tensor("ident", [128, 128], BF16, kind="ExternalInput").ap()
    identf32 = nc.dram_tensor("identf32", [128, 128], F32, kind="ExternalInput").ap()
    selbc = nc.dram_tensor("selbc", [9, 9 * 128], BF16, kind="ExternalInput").ap()
    out = nc.dram_tensor("out", [T, F], F32, kind="ExternalOutput").ap()
    with contextlib.ExitStack() as stk:
        sb = lambda name, shape, dt: stk.enter_context(nc.sbuf_tensor(name, shape, dt))
        bufs = {
            "x1": x1, "xn1": xn1, "o_scan": o_scan, "WoL1": WoL1,
            "W1": W1, "W2": W2, "Wo1": Wo1, "Wo2": Wo2,
            "b1r": b1r, "b2": b2, "bo1r": bo1r, "bo2r": bo2r, "gbr": gbr,
            "mixr": mixr, "identf32": identf32, "selbc": selbc, "out": out,
            "xntb": sb("xntb", [128, 8 * 520], BF16),
            "xm_sb": sb("xm_sb", [128, 8 * 512], BF16),
            "w1_sb": sb("w1_sb", [128, 2 * 2048], BF16),
            "w2_sb": sb("w2_sb", [128, 9 * 2048], BF16),
            "wo1_sb": sb("wo1_sb", [128, 2048], BF16),
            "wo2_sb": sb("wo2_sb", [128, 18], BF16),
            "wo_sb2": sb("wo_sb2", [64, 16 * 1024], BF16),
            "b1_sb": sb("b1_sb", [128, 18], F32),
            "b2_sb": sb("b2_sb", [9, 1024], BF16),
            "bo1_sb": sb("bo1_sb", [128, 2], F32),
            "bo2_sb": sb("bo2_sb", [1, 9], BF16),
            "gb_sb": sb("gb_sb", [128, 16], F32),
            "mixr_sb": sb("mixr_sb", [128, 24], F32),
            "th_sb": sb("th_sb", [128, 2 * 512], BF16),
            "mx_sb": sb("mx_sb", [128, 1], F32),
            "oh_sb": sb("oh_sb", [128, 9], BF16),
            "ohT_sb": sb("ohT_sb", [9, 512], BF16),
            "ohbc_sb": sb("ohbc_sb", [128, 512], BF16),
            "ht_sb": sb("ht_sb", [128, 9 * 1024], BF16),
            "zb_sb": sb("zb_sb", [128, 8 * 512], BF16),
            "fb_sb": sb("fb_sb", [128, 8 * 512], BF16),
            "osc_sb": sb("osc_sb", [64, 16 * 512], BF16),
            "x1t_sb": sb("x1t_sb", [128, 512], BF16),
            "of1_sb": sb("of1_sb", [128, 512], F32),
            "ofm_sb": sb("ofm_sb", [128, 8 * 512], F32),
            "otok_sb": sb("otok_sb", [128, 1024], F32),
            "rows": sb("rows", [1, 4 * 512], F32),
            "sq": sb("sq", [128, 512], BF16),
            "t1": sb("t1", [128, 512], F32),
            "onesb": sb("onesb", [128, 1], BF16),
            "ones1rF": sb("ones1rF", [1, 128], F32),
            "ones1rB": sb("ones1rB", [1, 128], BF16),
            "eps1": sb("eps1", [1, 1], F32),
            "identsb": sb("identsb", [128, 128], BF16),
            "identf": sb("idf_sb", [128, 128], F32),
            "identf": sb("identf", [128, 128], F32),
            "selbc_sb": sb("selbc_sb", [9, 9 * 128], BF16),
            "psA": stk.enter_context(nc.psum_tensor("psA", [128, 512], F32)),
            "psB": stk.enter_context(nc.psum_tensor("psB", [128, 512], F32)),
            "psC": stk.enter_context(nc.psum_tensor("psC", [128, 512], F32)),
            "psF": stk.enter_context(nc.psum_tensor("psF", [128, 512], F32)),
            "psOH": stk.enter_context(nc.psum_tensor("psOH", [9, 1024], BF16)),
            "psTf": stk.enter_context(nc.psum_tensor("psTf", [128, 512], F32)),
            
        }

        def prog(c):
            _sv = c.bump("vD")
            c.vector.memset(bufs["onesb"][:, :], 1.0)
            c.vector.memset(bufs["ones1rF"][:, :], 1.0)
            c.vector.memset(bufs["ones1rB"][:, :], 1.0)
            c.vector.memset(bufs["eps1"][:, :], 1e-5).then_inc(*_sv)
            idw = c.sdma(bufs["identsb"][:, :], ident[:, :])
            c.wait("tensor", idw)
            c.tensor.wait_ge(c.sem["vD"], c.n["vD"])
            emit_l1_ffn_final(c, bufs)

        run_program(nc, prog, ["tP", "vD", "sA", "zz", "fz", "ev", "lnv"])
    return nc



def apx(handle, n_part, dims, col_off=0, base_part=0):
    a = handle[:, :]
    pstep = a.ap[0][0]
    ap = [[pstep, n_part]] + [[s, n] for (s, n) in dims]
    return bass.AP(tensor=a.tensor, offset=base_part * pstep + col_off, ap=ap)


def emit_scan(c, b, n_seg=8):
    SEG = 512
    NCH = SEG // 16
    lrs = b["lr_sb"]
    winbd = b["winbd"]; woutbd = b["woutbd"]; woutd4 = b["woutd4"]
    eA = b["eA"]; lrt = b["lrt"]; sums = b["sums"]
    kqT = b["kqT"]; qkm = b["qkm"]
    ktok = b["ktok"]; vtok = b["vtok"]
    eC = b["eC"]; aN = b["aN"]; t128 = b["t128"]
    oTs = b["oTs"]
    psE = b["psE"]; psT4 = b["psT4"]; psQ = b["psQ"]; psO = b["psO"]
    psK = b["psK"]; psU = b["psU"]; psD = b["psD"]; psR = b["psR"]
    idf = b["identf"]
    ones4 = b["ones4"]; ones16 = b["ones16"]
    ones1x4 = b["ones1x4"]; ones1x16 = b["ones1x16"]
    maskT = b["maskT"]

    stw = c.dma_group("sync", [
        (winbd[:, :], b["winbd0"][:, :]),
        (woutbd[:, :], b["woutbd0"][:, :]),
        (woutd4[0:4, :], b["woutd40"][:, :]),
        (lrs[0:32, :], b["lrf"][:, :]),
        (maskT[0:16, :], b["maskT0"][:, :]),
    ])
    c.wait("tensor", stw)
    c.wait("vector", stw)

    segw = {}

    def prefetch(s):
        par = s % 2
        qof = par * 4096
        ws = []
        for (dst, src) in ((b["qseg"], b["qf"]), (b["kseg"], b["kf"]), (b["vseg"], b["vf"])):
            for ch in range(8):
                ws.append(c.gdma(dst[:, qof + ch * SEG:qof + (ch + 1) * SEG],
                                 src[ch * 128:(ch + 1) * 128, s * SEG:(s + 1) * SEG]))
        segw[s] = ws

    prefetch(0)
    if n_seg > 1:
        prefetch(1)

    for s in range(n_seg):
        par = s % 2
        qof = par * 4096
        for p_ in segw.pop(s):
            c.wait("tensor", p_)
        oseg_guard = c.n["ov"]
        for tl in range(NCH):
            t = s * NCH + tl
            col = tl * 16
            kch = [b["kseg"][:, qof + j * SEG + col: qof + j * SEG + col + 16] for j in range(8)]
            qch = [b["qseg"][:, qof + j * SEG + col: qof + j * SEG + col + 16] for j in range(8)]
            vch = [b["vseg"][:, qof + j * SEG + col: qof + j * SEG + col + 16] for j in range(8)]
            c.tensor.wait_ge(c.sem["sA"], c.n["sA"])
            c.tensor.wait_ge(c.sem["vD"], c.n["vD"])
            for j in range(8):
                c.tensor.matmul(psE[0:16, 8 * j:8 * j + 8], kch[j], winbd[:, 8 * j:8 * j + 8],
                                start=True, stop=True)
            for j in range(8):
                c.tensor.matmul(psE[0:16, 64 + 8 * j:64 + 8 * j + 8], qch[j],
                                winbd[:, 8 * j:8 * j + 8], start=True, stop=True)
            ins = c.tensor.transpose(psE[0:16, 128:160], lrs[0:32, 16 * t:16 * t + 16],
                                     idf[0:32, 0:32])
            ins.then_inc(*_b(c, "tP"))
            c.vector.wait_ge(c.sem["tP"], c.n["tP"])
            _sv = c.bump("vD")
            c.vector.tensor_copy(out=lrt[0:16, :], in_=psE[0:16, 128:160]).then_inc(*_sv)
            _sv = c.bump("vD")
            c.vector.tensor_reduce(out=b["mxA"][0:16, :],
                                   in_=psE[0:16, 0:128].rearrange("p (g d) -> p g d", d=4),
                                   axis=AX.X, op=ALU.max).then_inc(*_sv)
            c.vector.wait_ge(c.sem["vD"], c.n["vD"])
            _sv = c.bump("vD")
            c.vector.tensor_tensor(out=b["eAr"][0:16, :], in0=psE[0:16, 0:128],
                                   in1=apx(b["mxA"], 16, [(1, 32), (0, 4)]),
                                   op=ALU.subtract).then_inc(*_sv)
            c.scalar.wait_ge(c.sem["vD"], c.n["vD"])
            _ss = c.bump("sA")
            c.scalar.activation(out=eA[0:16, :], in_=b["eAr"][0:16, :], func=AF.Exp,
                                bias=0.0, scale=1.0).then_inc(*_ss)
            c.vector.wait_ge(c.sem["sA"], c.n["sA"])
            _sv = c.bump("vD")
            c.vector.tensor_reduce(out=sums[0:16, 0:32],
                                   in_=eA[0:16, :].rearrange("p (g d) -> p g d", d=4),
                                   axis=AX.X, op=ALU.add).then_inc(*_sv)
            c.vector.wait_ge(c.sem["vD"], c.n["vD"])
            _sv = c.bump("vD")
            c.vector.reciprocal(out=sums[0:16, 0:32], in_=sums[0:16, 0:32]).then_inc(*_sv)
            c.vector.wait_ge(c.sem["vD"], c.n["vD"])
            _sv = c.bump("vD")
            c.vector.tensor_tensor(out=eA[0:16, :], in0=eA[0:16, :],
                                   in1=apx(sums, 16, [(1, 32), (0, 4)]),
                                   op=ALU.mult).then_inc(*_sv)
            c.tensor.wait_ge(c.sem["dv"], c.n["dv"])
            c.tensor.wait_ge(c.sem["vD"], c.n["vD"])
            ins = None
            for h in range(16):
                j, h2 = h // 2, h % 2
                ins = c.tensor.transpose(psT4[0:4, 16 * h:16 * h + 16],
                                         eA[0:16, 8 * j + 4 * h2:8 * j + 4 * h2 + 4],
                                         idf[0:16, 0:16])
            for h in range(16):
                j, h2 = h // 2, h % 2
                ins = c.tensor.transpose(psT4[0:4, 256 + 16 * h:256 + 16 * h + 16],
                                         eA[0:16, 64 + 8 * j + 4 * h2:64 + 8 * j + 4 * h2 + 4],
                                         idf[0:16, 0:16])
            ins.then_inc(*_b(c, "tP"))
            c.vector.wait_ge(c.sem["tP"], c.n["tP"])
            _sv = c.bump("dv")
            c.vector.tensor_copy(out=kqT[0:4, :], in_=psT4[0:4, :]).then_inc(*_sv)
            c.vector.wait_ge(c.sem["dv"], c.n["dv"])
            _sv = c.bump("vD")
            c.vector.tensor_tensor(out=eA[0:16, 0:64], in0=eA[0:16, 0:64],
                                   in1=apx(lrt, 16, [(4, 8), (2, 2), (0, 4)], col_off=1),
                                   op=ALU.mult).then_inc(*_sv)
            for (tok, src_ch) in ((ktok, kch), (vtok, vch)):
                for half in range(2):
                    c.tensor.wait_ge(c.sem["kv"], c.n["kv"])
                    ins = None
                    for j4 in range(4):
                        j = half * 4 + j4
                        ins = c.tensor.transpose(psK[0:16, j4 * 128:(j4 + 1) * 128],
                                                 src_ch[j], idf[:, :])
                    ins.then_inc(*_b(c, "tP"))
                    c.vector.wait_ge(c.sem["tP"], c.n["tP"])
                    _sv = c.bump("kv")
                    c.vector.tensor_copy(out=tok[0:16, half * 512:(half + 1) * 512],
                                         in_=psK[0:16, 0:512]).then_inc(*_sv)
            c.tensor.wait_ge(c.sem["dv"], c.n["dv"])
            c.tensor.wait_ge(c.sem["qv"], c.n["qv"])
            ins = None
            for h in range(16):
                ins = c.tensor.matmul(psQ[0:16, 16 * h:16 * h + 16],
                                      kqT[0:4, 16 * h:16 * h + 16],
                                      kqT[0:4, 256 + 16 * h:256 + 16 * h + 16],
                                      start=True, stop=True)
            ins.then_inc(*_b(c, "tP"))
            c.vector.wait_ge(c.sem["tP"], c.n["tP"])
            c.vector.wait_ge(c.sem["kv"], c.n["kv"])
            _sv = c.bump("vD")
            c.vector.tensor_tensor(out=qkm[0:16, :], in0=psQ[0:16, 0:256], in1=maskT[0:16, :],
                                   op=ALU.mult).then_inc(*_sv)
            c.vector.wait_ge(c.sem["vD"], c.n["vD"])
            _sv = c.bump("qv")
            c.vector.tensor_tensor(out=qkm[0:16, :], in0=qkm[0:16, :],
                                   in1=apx(lrt, 16, [(2, 16), (0, 16)], col_off=1),
                                   op=ALU.mult).then_inc(*_sv)
            c.tensor.wait_ge(c.sem["ov"], c.n["ov"])
            c.tensor.wait_ge(c.sem["qv"], c.n["qv"])
            ins = None
            for h in range(16):
                c.tensor.matmul(psO[0:64, 16 * h:16 * h + 16],
                                woutd4[0:4, 64 * h:64 * h + 64],
                                kqT[0:4, 256 + 16 * h:256 + 16 * h + 16],
                                start=True, stop=False)
                ins = c.tensor.matmul(psO[0:64, 16 * h:16 * h + 16],
                                      vtok[0:16, 64 * h:64 * h + 64],
                                      qkm[0:16, 16 * h:16 * h + 16],
                                      start=False, stop=True)
            ins.then_inc(*_b(c, "tP"))
            c.vector.wait_ge(c.sem["tP"], c.n["tP"])
            _sv = c.bump("ov")
            c.vector.tensor_copy(out=apx(oTs, 64, [(512, 16), (1, 16)], col_off=col),
                                 in_=psO[0:64, 0:256]).then_inc(*_sv)
            c.tensor.wait_ge(c.sem["uv"], c.n["uv"])
            ins = None
            for h in range(16):
                j, h2 = h // 2, h % 2
                ins = c.tensor.matmul(psU[64 * h2:64 * h2 + 64, 4 * j:4 * j + 4],
                                      vtok[0:16, 64 * h:64 * h + 64],
                                      eA[0:16, 8 * j + 4 * h2:8 * j + 4 * h2 + 4],
                                      start=True, stop=True)
            ins.then_inc(*_b(c, "tP"))
            for half in range(2):
                c.tensor.wait_ge(c.sem["uv"], c.n["uv"])
                ins = None
                for h in range(half * 8, half * 8 + 8):
                    j, h2 = h // 2, h % 2
                    ins = c.tensor.matmul(psD[0:4, (h - half * 8) * 64:(h - half * 8) * 64 + 64],
                                          eA[0:16, 8 * j + 4 * h2:8 * j + 4 * h2 + 4],
                                          vtok[0:16, 64 * h:64 * h + 64],
                                          start=True, stop=True)
                ins.then_inc(*_b(c, "tP"))
                c.vector.wait_ge(c.sem["tP"], c.n["tP"])
                _sv = c.bump("uv")
                c.vector.tensor_tensor(out=woutd4[0:4, half * 512:(half + 1) * 512],
                                       in0=woutd4[0:4, half * 512:(half + 1) * 512],
                                       in1=psD[0:4, 0:512], op=ALU.add).then_inc(*_sv)
            c.vector.wait_ge(c.sem["tP"], c.n["tP"])
            for h2 in range(2):
                _sv = c.bump("uv")
                c.vector.tensor_tensor(
                    out=apx(woutbd, 64, [(8, 8), (1, 4)], col_off=4 * h2, base_part=64 * h2),
                    in0=apx(woutbd, 64, [(8, 8), (1, 4)], col_off=4 * h2, base_part=64 * h2),
                    in1=apx(psU, 64, [(4, 8), (1, 4)], base_part=64 * h2),
                    op=ALU.add).then_inc(*_sv)
            c.vector.wait_ge(c.sem["rv"], c.n["rv"])
            _sv = c.bump("vD")
            c.vector.tensor_copy(out=b["lrrow"][0:1, :],
                                 in_=apx(lrt, 1, [(-1, 2), (2, 16), (0, 4)], col_off=1)).then_inc(*_sv)
            c.tensor.wait_ge(c.sem["uv"], c.n["uv"])
            c.tensor.wait_ge(c.sem["vD"], c.n["vD"])
            _sp = c.bump("tP")
            c.tensor.matmul(psU[:, 16:17], b["lrrow"][0:1, :],
                            b["ones1x1"][0:1, 0:1], start=True, stop=True).then_inc(*_sp)
            c.vector.wait_ge(c.sem["tP"], c.n["tP"])
            _sv = c.bump("rv")
            c.vector.tensor_copy(out=b["lrN"][0:128, 0:1], in_=psU[:, 16:17]).then_inc(*_sv)
            for it in range(2):
                c.tensor.wait_ge(c.sem["sA"], c.n["sA"])
                c.tensor.wait_ge(c.sem["uv"], c.n["uv"])
                ins = None
                for h2sel in range(2):
                    for hp in range(32):
                        h = hp % 16
                        j, h2 = h // 2, h % 2
                        if h2 != h2sel:
                            continue
                        wsrc = winbd if hp < 16 else woutbd
                        ksrc = kch[j] if hp < 16 else vch[j]
                        ins = c.tensor.matmul(psE[0:16, 4 * hp:4 * hp + 4],
                                              ksrc[64 * h2:64 * h2 + 64, :],
                                              wsrc[64 * h2:64 * h2 + 64, 8 * j + 4 * h2:8 * j + 4 * h2 + 4],
                                              start=True, stop=True)
                    if h2sel == 0:
                        ins.then_inc(*_b(c, "tP"))
                        c.tensor.wait_ge(c.sem["tP"], c.n["tP"])
                ins.then_inc(*_b(c, "tP"))
                c.vector.wait_ge(c.sem["tP"], c.n["tP"])
                _sv = c.bump("cv")
                c.vector.tensor_copy(out=b["sC"][0:16, :], in_=psE[0:16, 0:128]).then_inc(*_sv)
                c.tensor.wait_ge(c.sem["cv"], c.n["cv"])
                c.tensor.wait_ge(c.sem["uv"], c.n["uv"])
                c.tensor.wait_ge(c.sem["rv"], c.n["rv"])
                _sp = c.bump("tP")
                c.tensor.transpose(psU[:, 0:16], b["sC"][0:16, :],
                                   idf[0:16, 0:16]).then_inc(*_sp)
                c.vector.wait_ge(c.sem["tP"], c.n["tP"])
                _sv = c.bump("cv")
                c.vector.tensor_reduce(out=b["nmx"][0:128, 0:1], in_=psU[:, 0:16],
                                       axis=AX.X, op=ALU.max, negate=True).then_inc(*_sv)
                c.vector.wait_ge(c.sem["cv"], c.n["cv"])
                _sv = c.bump("cv")
                c.vector.tensor_scalar(out=b["nmx"][0:128, 0:1], in0=b["nmx"][0:128, 0:1],
                                       scalar1=0.125, scalar2=None, op0=ALU.mult).then_inc(*_sv)
                c.scalar.wait_ge(c.sem["cv"], c.n["cv"])
                _ss = c.bump("sA")
                c.scalar.activation(out=b["eCT"][0:128, :], in_=psU[:, 0:16], func=AF.Exp,
                                    bias=b["nmx"][0:128, 0:1], scale=0.125).then_inc(*_ss)
                c.vector.wait_ge(c.sem["sA"], c.n["sA"])
                _sv = c.bump("cv")
                c.vector.tensor_reduce(out=b["sCs"][0:128, 0:1], in_=b["eCT"][0:128, :],
                                       axis=AX.X, op=ALU.add).then_inc(*_sv)
                c.vector.wait_ge(c.sem["cv"], c.n["cv"])
                _sv = c.bump("cv")
                c.vector.reciprocal(out=b["sCs"][0:128, 0:1], in_=b["sCs"][0:128, 0:1]).then_inc(*_sv)
                c.vector.wait_ge(c.sem["cv"], c.n["cv"])
                c.vector.wait_ge(c.sem["rv"], c.n["rv"])
                _sv = c.bump("cv")
                c.vector.tensor_scalar(out=b["eCT"][0:128, :], in0=b["eCT"][0:128, :],
                                       scalar1=b["sCs"][0:128, 0:1], scalar2=b["lrN"][0:128, 0:1],
                                       op0=ALU.mult, op1=ALU.mult).then_inc(*_sv)
                c.tensor.wait_ge(c.sem["cv"], c.n["cv"])
                c.tensor.wait_ge(c.sem["sA"], c.n["sA"])
                _sp = c.bump("tP")
                c.tensor.transpose(psE[0:16, 0:128], b["eCT"][0:128, :],
                                   idf[:, :]).then_inc(*_sp)
                c.vector.wait_ge(c.sem["tP"], c.n["tP"])
                _sv = c.bump("rv")
                c.vector.tensor_copy(out=aN[0:16, :], in_=psE[0:16, 0:128]).then_inc(*_sv)
                c.vector.wait_ge(c.sem["rv"], c.n["rv"])
                c.tensor.wait_ge(c.sem["rv"], c.n["rv"])
                c.tensor.wait_ge(c.sem["uv"], c.n["uv"])
                ins = None
                for hp in range(16):
                    h = hp
                    j, h2 = h // 2, h % 2
                    ins = c.tensor.matmul(psU[64 * h2:64 * h2 + 64, 4 * j:4 * j + 4],
                                          vtok[0:16, 64 * h:64 * h + 64],
                                          aN[0:16, 4 * hp:4 * hp + 4],
                                          start=True, stop=True)
                ins.then_inc(*_b(c, "tP"))
                c.vector.wait_ge(c.sem["tP"], c.n["tP"])
                for h2 in range(2):
                    _sv = c.bump("uv")
                    c.vector.tensor_tensor(
                        out=apx(woutbd, 64, [(8, 8), (1, 4)], col_off=4 * h2, base_part=64 * h2),
                        in0=apx(woutbd, 64, [(8, 8), (1, 4)], col_off=4 * h2, base_part=64 * h2),
                        in1=apx(psU, 64, [(4, 8), (1, 4)], base_part=64 * h2),
                        op=ALU.add).then_inc(*_sv)
                c.tensor.wait_ge(c.sem["uv"], c.n["uv"])
                ins = None
                for hp in range(16, 32):
                    h = hp - 16
                    j, h2 = h // 2, h % 2
                    ins = c.tensor.matmul(psU[64 * h2:64 * h2 + 64, 4 * j:4 * j + 4],
                                          ktok[0:16, 64 * h:64 * h + 64],
                                          aN[0:16, 4 * hp:4 * hp + 4],
                                          start=True, stop=True)
                ins.then_inc(*_b(c, "tP"))
                c.vector.wait_ge(c.sem["tP"], c.n["tP"])
                for h2 in range(2):
                    _sv = c.bump("uv")
                    c.vector.tensor_tensor(
                        out=apx(winbd, 64, [(8, 8), (1, 4)], col_off=4 * h2, base_part=64 * h2),
                        in0=apx(winbd, 64, [(8, 8), (1, 4)], col_off=4 * h2, base_part=64 * h2),
                        in1=apx(psU, 64, [(4, 8), (1, 4)], base_part=64 * h2),
                        op=ALU.add).then_inc(*_sv)
                for half in range(2):
                    c.tensor.wait_ge(c.sem["uv"], c.n["uv"])
                    ins = None
                    for h in range(half * 8, half * 8 + 8):
                        ins = c.tensor.matmul(psD[0:4, (h - half * 8) * 64:(h - half * 8) * 64 + 64],
                                              aN[0:16, 4 * h:4 * h + 4],
                                              vtok[0:16, 64 * h:64 * h + 64],
                                              start=True, stop=True)
                    ins.then_inc(*_b(c, "tP"))
                    c.vector.wait_ge(c.sem["tP"], c.n["tP"])
                    _sv = c.bump("uv")
                    c.vector.tensor_tensor(out=woutd4[0:4, half * 512:(half + 1) * 512],
                                           in0=woutd4[0:4, half * 512:(half + 1) * 512],
                                           in1=psD[0:4, 0:512], op=ALU.add).then_inc(*_sv)
        c.sync.wait_ge(c.sem["ov"], c.n["ov"])
        osw = c.sdma(b["o_scan"][:, :].rearrange("p (h t2) -> p h t2", t2=T)[:, :, s * SEG:(s + 1) * SEG],
                     oTs[0:64, :].rearrange("p (h t2) -> p h t2", t2=SEG))
        c.wait("vector", osw)
        if s + 2 < n_seg:
            c.gpsimd.wait_ge(c.sem["tP"], c.n["tP"])
            prefetch(s + 2)
    c.drain_dmas()


def build_launch5(n_seg=8):
    nc = bass.Bass("TRN2", target_bir_lowering=False, debug=False)
    qf = nc.dram_tensor("qf", [F, T], BF16, kind="ExternalInput").ap()
    kf = nc.dram_tensor("kf", [F, T], BF16, kind="ExternalInput").ap()
    vf = nc.dram_tensor("vf", [F, T], BF16, kind="ExternalInput").ap()
    lrf = nc.dram_tensor("lrf", [32, T], F32, kind="ExternalInput").ap()
    winbd0 = nc.dram_tensor("winbd0", [128, 64], F32, kind="ExternalInput").ap()
    woutbd0 = nc.dram_tensor("woutbd0", [128, 64], F32, kind="ExternalInput").ap()
    woutd40 = nc.dram_tensor("woutd40", [4, 1024], F32, kind="ExternalInput").ap()
    maskT0 = nc.dram_tensor("maskT0", [16, 256], F32, kind="ExternalInput").ap()
    identf32 = nc.dram_tensor("identf32", [128, 128], F32, kind="ExternalInput").ap()
    o_scan = nc.dram_tensor("o_scan", [64, 16 * T], F32, kind="ExternalOutput").ap()
    with contextlib.ExitStack() as stk:
        sb = lambda name, shape, dt: stk.enter_context(nc.sbuf_tensor(name, shape, dt))
        bufs = {
            "qf": qf, "kf": kf, "vf": vf, "lrf": lrf,
            "winbd0": winbd0, "woutbd0": woutbd0, "woutd40": woutd40,
            "maskT0": maskT0, "o_scan": o_scan,
            "qseg": sb("qseg", [128, 2 * 4096], F32),
            "kseg": sb("kseg", [128, 2 * 4096], F32),
            "vseg": sb("vseg", [128, 2 * 4096], F32),
            "lr_sb": sb("lr_sb", [32, T], F32),
            "winbd": sb("winbd", [128, 64], F32),
            "woutbd": sb("woutbd", [128, 64], F32),
            "woutd4": sb("woutd4", [4, 1024], F32),
            "eA": sb("eA", [16, 128], F32),
            "eAr": sb("eAr", [16, 128], F32),
            "mxA": sb("mxA", [16, 32], F32),
            "mxr": sb("mxr", [16, 1], F32),
            "gneg": sb("gneg", [1, 1], F32),
            "sC": sb("sC", [16, 128], F32),
            "nmx": sb("nmx", [128, 1], F32),
            "eCT": sb("eCT", [128, 16], F32),
            "sCs": sb("sCs", [128, 1], F32),
            "lrN": sb("lrN", [128, 1], F32),
            "ones1x1": sb("ones1x1", [1, 1], F32),
            "lrrow": sb("lrrow", [1, 128], F32),
            "negb": sb("negb", [16, 1], F32),
            "lrt": sb("lrt", [16, 32], F32),
            "sums": sb("sums", [16, 32], F32),
            "kqT": sb("kqT", [4, 512], F32),
            "recT": sb("recT", [1, 512], F32),
            "qkm": sb("qkm", [16, 256], F32),
            "ktok": sb("ktok", [16, 1024], F32),
            "vtok": sb("vtok", [16, 1024], F32),
            "eC": sb("eC", [16, 128], F32),
            "recC": sb("recC", [1, 128], F32),
            "aN": sb("aN", [16, 128], F32),
            "t128": sb("t128", [16, 128], F32),
            "oTs": sb("oTs", [64, 16 * 512], F32),
            "maskT": sb("maskT", [16, 256], F32),
            "ones4": sb("ones4", [4, 1], F32),
            "ones16": sb("ones16", [16, 1], F32),
            "ones1x4": sb("ones1x4", [1, 4], F32),
            "ones1x16": sb("ones1x16", [1, 16], F32),
            "identf": sb("identf", [128, 128], F32),
            "psE": stk.enter_context(nc.psum_tensor("psE", [16, 512], F32)),
            "psT4": stk.enter_context(nc.psum_tensor("psT4", [4, 512], F32)),
            "psQ": stk.enter_context(nc.psum_tensor("psQ", [16, 512], F32)),
            "psO": stk.enter_context(nc.psum_tensor("psO", [64, 512], F32)),
            "psK": stk.enter_context(nc.psum_tensor("psK", [16, 512], F32)),
            "psU": stk.enter_context(nc.psum_tensor("psU", [128, 512], F32)),
            "psD": stk.enter_context(nc.psum_tensor("psD", [4, 512], F32)),
            "psR": stk.enter_context(nc.psum_tensor("psR", [16, 512], F32)),
        }

        def prog(c):
            _sv = c.bump("vD")
            c.vector.memset(bufs["ones4"][:, :], 1.0)
            c.vector.memset(bufs["ones16"][:, :], 1.0)
            c.vector.memset(bufs["ones1x4"][:, :], 1.0)
            c.vector.memset(bufs["ones1x16"][:, :], 1.0)
            c.vector.memset(bufs["ones1x1"][:, :], 1.0).then_inc(*_sv)
            idw = c.sdma(bufs["identf"][:, :], identf32[:, :])
            c.wait("tensor", idw)
            c.tensor.wait_ge(c.sem["vD"], c.n["vD"])
            emit_scan(c, bufs, n_seg=n_seg)

        run_program(nc, prog, ["tP", "vD", "sA", "dv", "rv", "kv", "qv", "ov", "uv", "cv"])
    return nc



def emit_l0_attn_v2(c, b, n_pairs=8):
    xnf_hbm = b["xnf"]; y1_hbm = b["y1"]; x1_hbm = b["x1"]
    WqT = b["WqT"]; WkT = b["WkT"]; WvT = b["WvT"]; WoT = b["WoT"]
    bqkv = b["bqkv"]; bo_sb = b["bo_sb"]
    xnc = b["xnc"]; wq_sb = b["wq_sb"]; wk_sb = b["wk_sb"]; wv_sb = b["wv_sb"]
    wo_sb = b["wo_sb"]
    Qp = b["Qp"]; Kp = b["Kp"]; Vp = b["Vp"]; vt = b["vt"]
    eS = [b["eS"], b["eS2"]]
    pA = [b["pA"], b["pA2"]]
    pB = [b["pB"], b["pB2"]]
    t8 = b["t8"]
    zrow = b["zrow"]; zbc = b["zbc"]; ones1 = b["ones1"]
    psS = b["psS"]; psT = b["psT"]; psPV = b["psPV"]; psB = b["psB"]; psP = b["psP"]

    _sv = c.bump("vD")
    c.vector.memset(vt[:, :].rearrange("p (kc h d) -> p kc h d", h=2, d=65)[:, :, :, 64:65], 1.0)
    c.vector.memset(ones1[:, :], 1.0).then_inc(*_sv)
    bw = c.dma_group("sync", [(bqkv[:, :], b["bqkv_hbm"][:, :]),
                              (bo_sb[:, :], b["bo_hbm"][:, :])])
    c.wait("vector", bw)

    xnc_free_at = {0: 0, 1: 0}
    for p in range(n_pairs):
        c.gpsimd.wait_ge(c.sem["tP"], c.n["tP"])
        wqw = c.gdma(wq_sb[:, :].rearrange("p (cc o) -> p cc o", o=128),
                     WqT[:, p * 128:(p + 1) * 128].rearrange("(cc p2) o -> p2 cc o", p2=128))
        wkw = c.gdma(wk_sb[:, :].rearrange("p (cc o) -> p cc o", o=128),
                     WkT[:, p * 128:(p + 1) * 128].rearrange("(cc p2) o -> p2 cc o", p2=128))
        wvw = c.gdma(wv_sb[:, :].rearrange("p (cc o) -> p cc o", o=128),
                     WvT[:, p * 128:(p + 1) * 128].rearrange("(cc p2) o -> p2 cc o", p2=128))
        wow = c.gdma(wo_sb[:, :], WoT[p * 128:(p + 1) * 128, :])
        for pair_ in (wqw, wkw, wvw):
            c.wait("tensor", pair_)
        vd_free = [0, 0, 0, 0]
        for tb in range(8):
            tb0 = tb * 512
            parx = tb % 2
            c.sync.wait_ge(c.sem["tP"], xnc_free_at[parx])
            xw = c.dma_group("sync", [
                (xnc[:, parx * 4096 + cc * 512: parx * 4096 + (cc + 1) * 512],
                 xnf_hbm[cc * 128:(cc + 1) * 128, tb0:tb0 + 512]) for cc in range(8)])
            c.wait("tensor", xw)
            for jj, (dst, wsb, bcol, scale) in enumerate(
                    ((Qp, wq_sb, 0, 0.125), (Kp, wk_sb, 8, None), (Vp, wv_sb, 16, None))):
                jb = (tb * 3 + jj) % 4
                psJ = psS[:, jb * 512:(jb + 1) * 512]
                c.tensor.wait_ge(c.sem["vD"], vd_free[jb])
                ins = None
                for cc in range(8):
                    ins = c.tensor.matmul(psJ,
                                          wsb[:, cc * 128:(cc + 1) * 128],
                                          xnc[:, parx * 4096 + cc * 512: parx * 4096 + (cc + 1) * 512],
                                          start=(cc == 0), stop=(cc == 7))
                ins.then_inc(*_b(c, "tP"))
                c.vector.wait_ge(c.sem["tP"], c.n["tP"])
                _sv = c.bump("vD")
                if scale is None:
                    c.vector.tensor_scalar(out=dst[:, tb0:tb0 + 512], in0=psJ,
                                           scalar1=bqkv[:, bcol + p:bcol + p + 1], scalar2=None,
                                           op0=ALU.add).then_inc(*_sv)
                else:
                    c.vector.tensor_scalar(out=dst[:, tb0:tb0 + 512], in0=psJ,
                                           scalar1=bqkv[:, bcol + p:bcol + p + 1], scalar2=scale,
                                           op0=ALU.add, op1=ALU.mult).then_inc(*_sv)
                vd_free[jb] = c.n["vD"]
            xnc_free_at[parx] = c.n["tP"]
        c.tensor.wait_ge(c.sem["vD"], c.n["vD"])
        vt4 = vt[:, :].rearrange("p (kc h d) -> p kc h d", h=2, d=65)
        for g4 in range(4):
            kc0 = g4 * 8
            c.tensor.wait_ge(c.sem["ev2"], c.n["ev2"])
            for hh2 in range(2):
                if hh2 == 1:
                    c.tensor.wait_ge(c.sem["tP"], c.n["tP"])
                ins = None
                for kci in range(8):
                    ins = c.tensor.transpose(
                        psT[:, hh2 * 512 + kci * 64: hh2 * 512 + (kci + 1) * 64],
                        Vp[hh2 * 64:(hh2 + 1) * 64, (kc0 + kci) * 128:(kc0 + kci + 1) * 128],
                        b["identsb"][hh2 * 64:(hh2 + 1) * 64, hh2 * 64:(hh2 + 1) * 64])
                    ins.then_inc(*_b(c, "tP"))
            c.scalar.wait_ge(c.sem["tP"], c.n["tP"])
            for hh2 in range(2):
                _se = c.bump("ev2")
                c.scalar.copy(out=vt4[:, kc0:kc0 + 8, hh2, 0:64],
                              in_=psT[:, hh2 * 512:(hh2 + 1) * 512]
                              .rearrange("p (kc d) -> p kc d", d=64)).then_inc(*_se)
        c.tensor.wait_ge(c.sem["ev2"], c.n["ev2"])

        tiles = [(hh, qt) for hh in range(2) for qt in range(32)]
        ntile = len(tiles)
        base_scA = c.n["scA"]
        base_plV = c.n["plV"]
        base_tv = c.n["tv"]

        def S_phase(k):
            hh, qt = tiles[k]
            hbase = hh * 64
            q0 = qt * 128
            par = k % 2
            for half in range(2):
                k0 = half * 2048
                c.tensor.wait_ge(c.sem["scA"], base_scA + 2 * k + half)
                ins = None
                for kk in range(4):
                    ins = c.tensor.matmul(psS[:, kk * 512:(kk + 1) * 512],
                                          Qp[hbase:hbase + 64, q0:q0 + 128],
                                          Kp[hbase:hbase + 64, k0 + kk * 512: k0 + (kk + 1) * 512],
                                          start=True, stop=True)
                ins.then_inc(*_b(c, "tP"))
                if k >= 2:
                    c.scalar.wait_ge(c.sem["plV"], base_plV + (k - 2) + 1)
                c.scalar.wait_ge(c.sem["tP"], c.n["tP"])
                _ss = c.bump("scA")
                c.scalar.activation(out=eS[par][:, k0:k0 + 2048], in_=psS[:, :],
                                    func=AF.Exp, bias=0.0, scale=1.0).then_inc(*_ss)

        def L_phase(k):
            par = k % 2
            c.vector.wait_ge(c.sem["scA"], base_scA + 2 * (k + 1))
            if k >= 2:
                c.vector.wait_ge(c.sem["tv"], base_tv + (k - 2) + 1)
            src = eS[par]
            dst = pA[par]
            for rnd in range(3):
                _sv = c.bump("vD")
                c.vector.max(out=t8[:, :], in_=src[:, :]).then_inc(*_sv)
                c.vector.wait_ge(c.sem["vD"], c.n["vD"])
                _sv = c.bump("vD")
                c.vector.match_replace(out=dst[:, :], in_to_replace=t8[:, :],
                                       in_values=src[:, :], imm_value=0.0).then_inc(*_sv)
                c.vector.wait_ge(c.sem["vD"], c.n["vD"])
                src, dst = dst, (pB[par] if dst is pA[par] else pA[par])
            _sv = c.bump("vD")
            c.vector.max(out=t8[:, :], in_=src[:, :]).then_inc(*_sv)
            c.vector.wait_ge(c.sem["vD"], c.n["vD"])
            _sp2 = c.bump("plV")
            c.vector.scalar_tensor_tensor(out=dst[:, :], in0=eS[par][:, :],
                                          scalar=t8[:, 7:8], in1=eS[par][:, :],
                                          op0=ALU.is_ge, op1=ALU.mult).then_inc(*_sp2)
            c.vector.wait_ge(c.sem["plV"], c.n["plV"])
            return dst

        P_of = {}
        nrm_guard = [0, 0]

        def V_phase(k, Pq):
            hh, qt = tiles[k]
            hbase = hh * 64
            q0 = qt * 128
            par = k % 2
            Pt = pA[par] if Pq is pB[par] else pB[par]
            c.tensor.wait_ge(c.sem["plV"], base_plV + k + 1)
            for kg in range(8):
                for j in range(4):
                    _sp = c.bump("tP")
                    c.tensor.transpose(psT[:, j * 128:(j + 1) * 128],
                                       Pq[:, (kg * 4 + j) * 128:(kg * 4 + j + 1) * 128],
                                       b["identsb"][:, :]).then_inc(*_sp)
                c.scalar.wait_ge(c.sem["tP"], c.n["tP"])
                _se = c.bump("ev2")
                c.scalar.copy(out=Pt[:, kg * 512:(kg + 1) * 512],
                              in_=psT[:, 0:512]).then_inc(*_se)
                c.tensor.wait_ge(c.sem["ev2"], c.n["ev2"])
            c.tensor.wait_ge(c.sem["ev2"], c.n["ev2"])
            ins = None
            for kc in range(32):
                ins = c.tensor.matmul(psPV[0:65, 0:128],
                                      vt[:, kc * 130 + hh * 65: kc * 130 + hh * 65 + 65],
                                      Pt[:, kc * 128:(kc + 1) * 128],
                                      start=(kc == 0), stop=(kc == 31))
            ins.then_inc(*_b(c, "tv"))
            c.scalar.wait_ge(c.sem["tv"], c.n["tv"])
            c.scalar.wait_ge(c.sem["tP"], nrm_guard[0])
            _se = c.bump("ev2")
            c.scalar.copy(out=b["e65"][:, :], in_=psPV[0:65, 0:128]).then_inc(*_se)
            c.tensor.wait_ge(c.sem["ev2"], c.n["ev2"])
            _sp = c.bump("tP")
            c.tensor.transpose(psB[:, 0:65], b["e65"][:, :],
                               b["identf"][0:65, 0:65]).then_inc(*_sp)
            nrm_guard[0] = c.n["tP"]
            c.scalar.wait_ge(c.sem["tP"], c.n["tP"])
            _se = c.bump("ev2")
            c.scalar.activation(out=b["lnt"][:, :], in_=psB[:, 64:65],
                                func=AF.Ln, bias=0.0, scale=1.0).then_inc(*_se)
            c.scalar.wait_ge(c.sem["ev2"], c.n["ev2"])
            _se = c.bump("ev2")
            c.scalar.activation(out=b["rzt"][:, :], in_=b["lnt"][:, :],
                                func=AF.Exp, bias=0.0, scale=-1.0).then_inc(*_se)
            c.scalar.wait_ge(c.sem["ev2"], c.n["ev2"])
            c.scalar.wait_ge(c.sem["tP"], nrm_guard[1])
            _se = c.bump("ev2")
            c.scalar.activation(out=b["qn_sb"][:, :], in_=psB[:, 0:64],
                                func=AF.Copy, bias=0.0,
                                scale=b["rzt"][:, 0:1]).then_inc(*_se)
            c.tensor.wait_ge(c.sem["ev2"], c.n["ev2"])
            _sp = c.bump("tP")
            c.tensor.transpose(psP[0:64, 0:128], b["qn_sb"][:, :],
                               b["identsb"][:, :]).then_inc(*_sp)
            nrm_guard[1] = c.n["tP"]
            c.scalar.wait_ge(c.sem["tP"], c.n["tP"])
            _se = c.bump("ev2")
            c.scalar.copy(out=b["wacc"][hbase:hbase + 64, p * 4096 + q0: p * 4096 + q0 + 128],
                          in_=psP[0:64, 0:128]).then_inc(*_se)


        for k in range(ntile + 2):
            if k < ntile:
                S_phase(k)
            if 1 <= k <= ntile:
                P_of[k - 1] = L_phase(k - 1)
            if k >= 2:
                V_phase(k - 2, P_of.pop(k - 2))
        c.wait("scalar", wow)
        _se = c.bump("ev2")
        c.scalar.copy(out=b["wo_all"][:, p * 1024:(p + 1) * 1024],
                      in_=wo_sb[:, :]).then_inc(*_se)
        c.gpsimd.wait_ge(c.sem["ev2"], c.n["ev2"])
    c.tensor.wait_ge(c.sem["vD"], c.n["vD"])
    c.tensor.wait_ge(c.sem["ev2"], c.n["ev2"])
    x1t_guard = [0, 0]
    ps_free = [0, 0]
    for fo in range(8):
        for tb in range(8):
            tb0 = tb * 512
            pp = tb % 2
            c.sync.wait_ge(c.sem["vD"], x1t_guard[pp])
            yw = c.sdma(b["y1t"][:, pp * 512:(pp + 1) * 512],
                        y1_hbm[fo * 128:(fo + 1) * 128, tb0:tb0 + 512])
            c.tensor.wait_ge(c.sem["vD"], ps_free[pp])
            ins = None
            for pr in range(n_pairs):
                ins = c.tensor.matmul(psS[:, pp * 512:(pp + 1) * 512],
                                      b["wo_all"][:, pr * 1024 + fo * 128: pr * 1024 + (fo + 1) * 128],
                                      b["wacc"][:, pr * 4096 + tb0: pr * 4096 + tb0 + 512],
                                      start=(pr == 0), stop=(pr == n_pairs - 1))
            ins.then_inc(*_b(c, "tP"))
            c.wait("vector", yw)
            c.vector.wait_ge(c.sem["tP"], c.n["tP"])
            _sv = c.bump("vD")
            c.vector.scalar_tensor_tensor(
                out=b["x1t"][:, pp * 512:(pp + 1) * 512],
                in0=psS[:, pp * 512:(pp + 1) * 512],
                scalar=bo_sb[:, fo:fo + 1], in1=b["y1t"][:, pp * 512:(pp + 1) * 512],
                op0=ALU.add, op1=ALU.add).then_inc(*_sv)
            x1t_guard[pp] = c.n["vD"]
            ps_free[pp] = c.n["vD"]
            c.sync.wait_ge(c.sem["vD"], c.n["vD"])
            stw = c.sdma(x1_hbm[fo * 128:(fo + 1) * 128, tb0:tb0 + 512],
                         b["x1t"][:, pp * 512:(pp + 1) * 512])
            c.wait("vector", stw)
    c.drain_dmas()


BFC = ml_dtypes.bfloat16


def prep_consts():
    identv = np.eye(128, dtype=np.float32).astype(BFC)
    identf32 = np.eye(128, dtype=np.float32)
    selbc = np.zeros((9, 9 * 128), np.float32)
    for p in range(9):
        selbc[p, p * 128:(p + 1) * 128] = 1.0
    maskT0 = np.zeros((16, 256), np.float32)
    for kk in range(16):
        for h in range(16):
            maskT0[kk, 16 * h + kk:16 * h + 16] = 1.0
    return identv, identf32, selbc.astype(BFC), maskT0

"""Fused single-launch megakernel: all 6 stages in ONE program.
Intermediates go through Internal DRAM (first fusion step); stage boundaries
are full barriers; SBUF/psum tiles are stage-scoped (freed + reused)."""

ENGINES = ("sync", "tensor", "vector", "scalar", "gpsimd")

ALL_SEMS = sorted(set(
    ["vs", "ss", "pe", "ev",
     "mm", "hv", "yv",
     "tP", "vD", "sA", "wv", "scA", "plV", "tv", "ev2",
     "zz", "fz", "lnv",
     "dv", "rv", "kv", "qv", "ov", "uv", "cv"]))


class FCtx(Ctx):

    def __init__(self, live_name, eng, sems, n_shared, rr_shared):
        super().__init__(live_name, eng, sems)
        self.n = n_shared
        self._rr = rr_shared


def emit_barrier(c):
    for nm in sorted(c.n.keys()):
        for e in ENGINES:
            getattr(c, e).wait_ge(c.sem[nm], c.n[nm])


def build_fused(debug_outs=True):
    nc = bass.Bass("TRN2", target_bir_lowering=False, debug=False)
    dt = lambda name, shape, dty, kind: nc.dram_tensor(name, shape, dty, kind=kind).ap()
    E, O, N = "ExternalInput", "ExternalOutput", "Internal"
    d = {}
    d["x"] = dt("x", [T, F], F32, E)
    d["ident"] = dt("ident", [128, 128], BF16, E)
    d["identf32"] = dt("identf32", [128, 128], F32, E)
    d["wgT"] = dt("wgT", [1024, 3072], I32, E)
    d["wuT"] = dt("wuT", [1024, 3072], I32, E)
    d["wdT"] = dt("wdT", [3072, 1024], I32, E)
    for nm in ("WqT", "WkT", "WvT", "WoT"):
        d[nm] = dt(nm, [F, F], F32, E)
    d["bqkv_hbm"] = dt("bqkv", [128, 24], F32, E)
    d["bo_hbm"] = dt("bo", [128, 8], F32, E)
    for nm in ("Wq", "Wk", "Wv"):
        d[nm] = dt(nm + "1", [F, F], F32, E)
    d["Wlr"] = dt("Wlr", [F, 32], F32, E)
    d["convr_h"] = dt("convr", [128, 96], F32, E)
    d["winbd0"] = dt("winbd0", [128, 64], F32, E)
    d["woutbd0"] = dt("woutbd0", [128, 64], F32, E)
    d["woutd40"] = dt("woutd40", [4, 1024], F32, E)
    d["maskT0"] = dt("maskT0", [16, 256], F32, E)
    d["WoL1"] = dt("WoL1", [F, F], F32, E)
    d["W1"] = dt("W1", [9, F, 256], F32, E)
    d["W2"] = dt("W2", [9, 256, F], F32, E)
    d["Wo1"] = dt("Wo1", [F, 256], F32, E)
    d["Wo2"] = dt("Wo2", [256, 9], F32, E)
    d["b1r"] = dt("b1r", [128, 18], F32, E)
    d["b2"] = dt("b2", [9, F], F32, E)
    d["bo1r"] = dt("bo1r", [128, 2], F32, E)
    d["bo2r"] = dt("bo2r", [1, 9], F32, E)
    d["gbr"] = dt("gbr", [128, 16], F32, E)
    d["mixr"] = dt("mixr", [128, 24], F32, E)
    d["selbc"] = dt("selbc", [9, 9 * 128], BF16, E)
    NI = O if debug_outs else N
    d["xf"] = dt("xf", [F, T], BF16, NI)
    d["xnf"] = dt("xnf", [F, T], BF16, NI)
    d["y1"] = dt("y1", [F, T], BF16, NI)
    d["x1"] = dt("x1", [F, T], BF16, NI)
    d["xn1"] = dt("xn1", [F, T], BF16, NI)
    d["qf"] = dt("qf", [F, T], BF16, NI)
    d["kf"] = dt("kf", [F, T], BF16, NI)
    d["vf"] = dt("vf", [F, T], BF16, NI)
    d["lrf"] = dt("lrf", [32, T], F32, NI)
    d["o_scan"] = dt("o_scan", [64, 16 * T], F32, NI)
    d["out"] = dt("out", [T, F], F32, O)

    n_state = {nm: 0 for nm in ALL_SEMS + DMA_SEMS}
    rr_state = {"sync": 0, "gpsimd": 0}

    with contextlib.ExitStack() as gstk:
        sems = {nm: gstk.enter_context(nc.semaphore(nm)) for nm in ALL_SEMS + DMA_SEMS}
        identsb = gstk.enter_context(nc.sbuf_tensor("g_identsb", [128, 128], BF16))
        identf = gstk.enter_context(nc.sbuf_tensor("g_identf", [128, 128], F32))
        block = gstk.enter_context(nc.Block())

        def run_stage(alloc_fn, emit_fn):
            with contextlib.ExitStack() as stk:
                bufs = alloc_fn(nc, stk)
                bufs["identsb"] = identsb
                bufs["identf"] = identf
                snap_n = dict(n_state)
                snap_rr = dict(rr_state)
                for eng_name in ENGINES:
                    n_state.clear(); n_state.update(snap_n)
                    rr_state.clear(); rr_state.update(snap_rr)

                    def body(eng, _nm=eng_name, _bufs=bufs, _emit=emit_fn):
                        c = FCtx(_nm, eng, sems, n_state, rr_state)
                        _emit(c, _bufs)
                        emit_barrier(c)
                    getattr(block, eng_name)(body)

        def alloc0(nc_, stk):
            return {}

        def emit0(c, b):
            p1 = c.sdma(identsb[:, :], d["ident"][:, :])
            p2 = c.sdma(identf[:, :], d["identf32"][:, :])
            c.wait("tensor", p1)
            c.wait("tensor", p2)
        run_stage(alloc0, emit0)

        def alloc1(nc_, stk):
            sb = lambda name, shape, dty: stk.enter_context(nc_.sbuf_tensor("s1_" + name, shape, dty))
            ps = lambda name, shape, dty: stk.enter_context(nc_.psum_tensor("s1_" + name, shape, dty))
            return {
                "x": d["x"], "xf": d["xf"], "xnf": d["xnf"], "ident": d["ident"],
                "xt": sb("xt", [128, 4096], BF16), "xnt": sb("xnt", [128, 4096], BF16),
                "st1": sb("st1", [128, 18], F32),
                "evac": sb("evac", [128, 512], BF16), "evac2": sb("evac2", [128, 512], BF16),
                "psT": ps("psT", [128, 1024], BF16), "psT2": ps("psT2", [128, 1024], BF16),
            }
        def alloc12(nc_, stk):
            bufs = alloc1(nc_, stk)
            sb = lambda name, shape, dty: stk.enter_context(nc_.sbuf_tensor("s2_" + name, shape, dty))
            ps = lambda name, shape, dty: stk.enter_context(nc_.psum_tensor("s2_" + name, shape, dty))
            bufs.update({
                "xnf": d["xnf"], "xf": d["xf"], "y1": d["y1"],
                "wgT": d["wgT"], "wuT": d["wuT"], "wdT": d["wdT"],
                "xn_sb": sb("xn_sb", [128, 8 * T], BF16),
                "wg_sb": sb("wg_sb", [128, 2 * 1024], BF16),
                "wu_sb": sb("wu_sb", [128, 2 * 1024], BF16),
                "wd_sb": sb("wd_sb", [128, 24 * 1024], BF16),
                "h_sb": sb("h_sb", [128, 24 * 1024], BF16),
                "sg_sb": sb("sg_sb", [128, 512], BF16),
                "xf_sb": sb("xf_sb", [128, 2 * 512], BF16),
                "y1_sb": sb("y1_sb", [128, 2 * 512], BF16),
                "psG": ps("psG", [128, 512], F32), "psU": ps("psU", [128, 512], F32),
                "psD": ps("psD", [128, 512], F32),
            })
            return bufs

        def emit12(c, b):
            wd_pre = [c.gdma(b["wd_sb"][:, hc * 1024:(hc + 1) * 1024],
                             b["wdT"][hc * 128:(hc + 1) * 128, :])
                      for hc in range(24)]
            emit_ln0_and_layout(c, b)
            emit_l0_ffn(c, b, wd_pre=wd_pre)
        run_stage(alloc12, emit12)

        def alloc3(nc_, stk):
            sb = lambda name, shape, dty: stk.enter_context(nc_.sbuf_tensor("s3_" + name, shape, dty))
            ps = lambda name, shape, dty: stk.enter_context(nc_.psum_tensor("s3_" + name, shape, dty))
            return {
                "xnf": d["xnf"], "y1": d["y1"], "x1": d["x1"],
                "WqT": d["WqT"], "WkT": d["WkT"], "WvT": d["WvT"], "WoT": d["WoT"],
                "bqkv_hbm": d["bqkv_hbm"], "bo_hbm": d["bo_hbm"],
                "bqkv": sb("bqkv", [128, 24], F32), "bo_sb": sb("bo_sb", [128, 8], F32),
                "xnc": sb("xnc", [128, 2 * 4096], BF16),
                "wq_sb": sb("wq_sb", [128, 1024], BF16), "wk_sb": sb("wk_sb", [128, 1024], BF16),
                "wv_sb": sb("wv_sb", [128, 1024], BF16), "wo_sb": sb("wo_sb", [128, 1024], BF16),
                "Qp": sb("Qp", [128, 4096], BF16), "Kp": sb("Kp", [128, 4096], BF16),
                "Vp": sb("Vp", [128, 4096], BF16),
                "vt": sb("vt", [128, 32 * 130], BF16),
                "eS": sb("eS", [128, 4096], BF16), "pA": sb("pA", [128, 4096], BF16),
                "pB": sb("pB", [128, 4096], BF16),
                "eS2": sb("eS2", [128, 4096], BF16), "pA2": sb("pA2", [128, 4096], BF16),
                "pB2": sb("pB2", [128, 4096], BF16),
                "t8": sb("t8", [128, 8], BF16),
                "zrow": sb("zrow", [1, 128], F32), "zbc": sb("zbc", [64, 128], F32),
                "zi": sb("zi", [128, 4], F32), "ones1": sb("ones1", [1, 64], F32),
                "e65": sb("e65", [65, 128], F32),
                "qn_sb": sb("qn_sb", [128, 64], BF16),
                "rzt": sb("rzt", [128, 1], F32), "lnt": sb("lnt", [128, 1], F32),
                "wacc": sb("wacc", [128, 8 * 4096], BF16),
                "wo_all": sb("wo_all", [128, 8 * 1024], BF16),
                "y1t": sb("y1t", [128, 2 * 512], BF16), "x1t": sb("x1t", [128, 2 * 512], BF16),
                "psS": ps("psS", [128, 2048], F32), "psT": ps("psT", [128, 1024], BF16),
                "psPV": ps("psPV", [128, 512], F32), "psB": ps("psB", [128, 512], F32),
                "psP": ps("psP", [128, 1024], BF16),
            }
        run_stage(alloc3, emit_l0_attn_v2)

        def alloc4(nc_, stk):
            sb = lambda name, shape, dty: stk.enter_context(nc_.sbuf_tensor("s4_" + name, shape, dty))
            ps = lambda name, shape, dty: stk.enter_context(nc_.psum_tensor("s4_" + name, shape, dty))
            return {
                "x1": d["x1"], "xn1": d["xn1"], "qf": d["qf"], "kf": d["kf"], "vf": d["vf"],
                "lrf": d["lrf"], "Wq": d["Wq"], "Wk": d["Wk"], "Wv": d["Wv"], "Wlr": d["Wlr"],
                "convr_h": d["convr_h"],
                "x1_sb": sb("x1_sb", [128, 8 * T], BF16),
                "xn_sb": sb("xn_sb", [128, 8 * T], BF16),
                "w_sb": sb("w_sb", [128, 2 * 1024], BF16),
                "wlr_sb": sb("wlr_sb", [128, 256], BF16),
                "ch_a": sb("ch_a", [128, T], BF16), "ch_b": sb("ch_b", [128, T], BF16),
                "lr_sb": sb("lr_sb", [32, T + 32], F32),
                "lrt1": sb("lrt1", [32, 512], F32), "lrt2": sb("lrt2", [32, 512], F32),
                "lrt3": sb("lrt3", [32, 512], F32),
                "convr": sb("convrs", [128, 96], F32),
                "rows": sb("rows", [1, 4 * 512], F32),
                "sq": sb("sq", [128, 512], BF16), "t1": sb("t1", [128, 512], BF16),
                "onesb": sb("onesb", [128, 1], BF16), "ones1rF": sb("ones1rF", [1, 128], F32),
                "eps1": sb("eps1", [1, 1], F32), "eps3": sb("eps3", [32, 1], F32),
                "psA": ps("psA", [128, 512], F32), "psB": ps("psB", [128, 512], F32),
            }

        def emit4(c, b):
            _sv = c.bump("vD")
            c.vector.memset(b["onesb"][:, :], 1.0)
            c.vector.memset(b["ones1rF"][:, :], 1.0)
            c.vector.memset(b["eps1"][:, :], 1e-5)
            c.vector.memset(b["eps3"][:, :], 0.001).then_inc(*_sv)
            cw = c.sdma(b["convr"][:, :], b["convr_h"][:, :])
            c.wait("vector", cw)
            c.tensor.wait_ge(c.sem["vD"], c.n["vD"])
            emit_l1_pre(c, b)
        run_stage(alloc4, emit4)

        def alloc5(nc_, stk):
            sb = lambda name, shape, dty: stk.enter_context(nc_.sbuf_tensor("s5_" + name, shape, dty))
            ps = lambda name, shape, dty: stk.enter_context(nc_.psum_tensor("s5_" + name, shape, dty))
            bufs = {
                "qf": d["qf"], "kf": d["kf"], "vf": d["vf"], "lrf": d["lrf"],
                "winbd0": d["winbd0"], "woutbd0": d["woutbd0"], "woutd40": d["woutd40"],
                "maskT0": d["maskT0"], "o_scan": d["o_scan"],
                "qseg": sb("qseg", [128, 2 * 4096], F32),
                "kseg": sb("kseg", [128, 2 * 4096], F32),
                "vseg": sb("vseg", [128, 2 * 4096], F32),
                "lr_sb": sb("lr_sb", [32, T], F32),
                "winbd": sb("winbd", [128, 64], F32), "woutbd": sb("woutbd", [128, 64], F32),
                "woutd4": sb("woutd4", [4, 1024], F32),
                "eA": sb("eA", [16, 128], F32), "eAr": sb("eAr", [16, 128], F32),
                "mxA": sb("mxA", [16, 32], F32), "mxr": sb("mxr", [16, 1], F32),
                "gneg": sb("gneg", [1, 1], F32),
                "sC": sb("sC", [16, 128], F32), "nmx": sb("nmx", [128, 1], F32),
                "eCT": sb("eCT", [128, 16], F32), "sCs": sb("sCs", [128, 1], F32),
                "lrN": sb("lrN", [128, 1], F32), "ones1x1": sb("ones1x1", [1, 1], F32),
                "lrrow": sb("lrrow", [1, 128], F32),
                "lrt": sb("lrt", [16, 32], F32), "sums": sb("sums", [16, 32], F32),
                "kqT": sb("kqT", [4, 512], F32), "recT": sb("recT", [1, 512], F32),
                "qkm": sb("qkm", [16, 256], F32),
                "ktok": sb("ktok", [16, 1024], F32), "vtok": sb("vtok", [16, 1024], F32),
                "eC": sb("eC", [16, 128], F32), "recC": sb("recC", [1, 128], F32),
                "aN": sb("aN", [16, 128], F32), "t128": sb("t128", [16, 128], F32),
                "oTs": sb("oTs", [64, 16 * 512], F32),
                "maskT": sb("maskT", [16, 256], F32),
                "ones4": sb("ones4", [4, 1], F32), "ones16": sb("ones16", [16, 1], F32),
                "ones1x4": sb("ones1x4", [1, 4], F32), "ones1x16": sb("ones1x16", [1, 16], F32),
                "psE": ps("psE", [16, 512], F32), "psT4": ps("psT4", [4, 512], F32),
                "psQ": ps("psQ", [16, 512], F32), "psO": ps("psO", [64, 512], F32),
                "psK": ps("psK", [16, 512], F32), "psU": ps("psU", [128, 512], F32),
                "psD": ps("psD", [4, 512], F32), "psR": ps("psR", [16, 512], F32),
            }
            return bufs

        def emit5(c, b):
            _sv = c.bump("vD")
            c.vector.memset(b["ones4"][:, :], 1.0)
            c.vector.memset(b["ones16"][:, :], 1.0)
            c.vector.memset(b["ones1x4"][:, :], 1.0)
            c.vector.memset(b["ones1x16"][:, :], 1.0)
            c.vector.memset(b["ones1x1"][:, :], 1.0).then_inc(*_sv)
            c.tensor.wait_ge(c.sem["vD"], c.n["vD"])
            emit_scan(c, b, n_seg=8)
        run_stage(alloc5, emit5)

        def alloc6(nc_, stk):
            sb = lambda name, shape, dty: stk.enter_context(nc_.sbuf_tensor("s6_" + name, shape, dty))
            ps = lambda name, shape, dty: stk.enter_context(nc_.psum_tensor("s6_" + name, shape, dty))
            return {
                "x1": d["x1"], "xn1": d["xn1"], "o_scan": d["o_scan"], "WoL1": d["WoL1"],
                "W1": d["W1"], "W2": d["W2"], "Wo1": d["Wo1"], "Wo2": d["Wo2"],
                "b1r": d["b1r"], "b2": d["b2"], "bo1r": d["bo1r"], "bo2r": d["bo2r"],
                "gbr": d["gbr"], "mixr": d["mixr"], "identf32": d["identf32"],
                "selbc": d["selbc"], "out": d["out"],
                "xntb": sb("xntb", [128, 8 * 520], BF16),
                "xm_sb": sb("xm_sb", [128, 8 * 512], BF16),
                "w1_sb": sb("w1_sb", [128, 2 * 2048], BF16),
                "w2_sb": sb("w2_sb", [128, 9 * 2048], BF16),
                "wo1_sb": sb("wo1_sb", [128, 2048], BF16),
                "wo2_sb": sb("wo2_sb", [128, 18], BF16),
                "wo_sb2": sb("wo_sb2", [64, 16 * 1024], BF16),
                "b1_sb": sb("b1_sb", [128, 18], F32), "b2_sb": sb("b2_sb", [9, 1024], BF16),
                "bo1_sb": sb("bo1_sb", [128, 2], F32), "bo2_sb": sb("bo2_sb", [1, 9], BF16),
                "gb_sb": sb("gb_sb", [128, 16], F32), "mixr_sb": sb("mixr_sb", [128, 24], F32),
                "th_sb": sb("th_sb", [128, 2 * 512], BF16),
                "mx_sb": sb("mx_sb", [128, 1], F32), "oh_sb": sb("oh_sb", [128, 9], BF16),
                "ohT_sb": sb("ohT_sb", [9, 512], BF16), "ohbc_sb": sb("ohbc_sb", [128, 512], BF16),
                "ht_sb": sb("ht_sb", [128, 9 * 1024], BF16),
                "zb_sb": sb("zb_sb", [128, 8 * 512], BF16),
                "fb_sb": sb("fb_sb", [128, 8 * 512], BF16),
                "osc_sb": sb("osc_sb", [64, 16 * 512], BF16),
                "x1t_sb": sb("x1t_sb", [128, 512], BF16),
                "of1_sb": sb("of1_sb", [128, 512], F32),
                "ofm_sb": sb("ofm_sb", [128, 8 * 512], F32),
                "otok_sb": sb("otok_sb", [128, 1024], F32),
                "rows": sb("rows", [1, 4 * 512], F32),
                "sq": sb("sq", [128, 512], BF16), "t1": sb("t1", [128, 512], F32),
                "onesb": sb("onesb", [128, 1], BF16), "ones1rF": sb("ones1rF", [1, 128], F32),
                "ones1rB": sb("ones1rB", [1, 128], BF16),
                "eps1": sb("eps1", [1, 1], F32),
                "selbc_sb": sb("selbc_sb", [9, 9 * 128], BF16),
                "psA": ps("psA", [128, 512], F32), "psB": ps("psB", [128, 512], F32),
                "psC": ps("psC", [128, 512], F32), "psF": ps("psF", [128, 512], F32),
                "psOH": ps("psOH", [9, 1024], BF16), "psTf": ps("psTf", [128, 512], F32),
            }

        def emit6(c, b):
            _sv = c.bump("vD")
            c.vector.memset(b["onesb"][:, :], 1.0)
            c.vector.memset(b["ones1rF"][:, :], 1.0)
            c.vector.memset(b["ones1rB"][:, :], 1.0)
            c.vector.memset(b["eps1"][:, :], 1e-5).then_inc(*_sv)
            c.tensor.wait_ge(c.sem["vD"], c.n["vD"])
            emit_l1_ffn_final(c, b)
        run_stage(alloc6, emit6)
    return nc


def kernel_fused(**inp):
    identv, identf32, selbc, maskT0 = prep_consts()
    x_all = np.asarray(inp["L0_attn_x"])
    bq = np.asarray(inp["L0_attn_bq"]); bk = np.asarray(inp["L0_attn_bk"])
    bv = np.asarray(inp["L0_attn_bv"]); bo = np.asarray(inp["L0_attn_bo"])
    Win0 = np.asarray(inp["L1_attn_W_in_init"])[0]
    Wout0 = np.asarray(inp["L1_attn_W_out_init"])[0]
    winbd0 = np.zeros((128, 64), np.float32)
    woutbd0 = np.zeros((128, 64), np.float32)
    woutd40 = np.zeros((4, 1024), np.float32)
    for h in range(16):
        j, h2 = h // 2, h % 2
        for D in range(4):
            winbd0[64 * h2:64 * h2 + 64, 8 * j + 4 * h2 + D] = Win0[D, h, :]
            woutbd0[64 * h2:64 * h2 + 64, 8 * j + 4 * h2 + D] = Wout0[D, h, :]
            woutd40[D, 64 * h:64 * h + 64] = Wout0[D, h, :]
    convr = np.zeros((128, 96), np.float32)
    for pi, key in enumerate(("L1_attn_conv_q", "L1_attn_conv_k", "L1_attn_conv_v")):
        cc_ = np.asarray(inp[key])
        for oc in range(8):
            convr[:, pi * 32 + oc * 4: pi * 32 + (oc + 1) * 4] = cc_[oc * 128:(oc + 1) * 128, :]
    b1 = np.asarray(inp["L1_ffn_b1"]); b2 = np.asarray(inp["L1_ffn_b2"])
    bo1 = np.asarray(inp["L1_ffn_bo1"]); bo2 = np.asarray(inp["L1_ffn_bo2"])
    gam = np.asarray(inp["L1_ffn_gamma"]); bet = np.asarray(inp["L1_ffn_beta"])
    mixk = np.asarray(inp["L1_ffn_mix_kernel"])
    b1r = np.zeros((128, 18), np.float32)
    for p in range(9):
        for mc in range(2):
            b1r[:, p * 2 + mc] = b1[p, mc * 128:(mc + 1) * 128]
    gbr = np.concatenate([gam.reshape(8, 128).T, bet.reshape(8, 128).T], axis=1).astype(np.float32)
    mixr = np.zeros((128, 24), np.float32)
    for ch in range(8):
        for j in range(3):
            mixr[:, ch * 3 + j] = mixk[j, ch * 128:(ch + 1) * 128]
    common = {
        "ident": identv, "identf32": identf32,
        "wgT": np.ascontiguousarray(np.asarray(inp["L0_ffn_w_gate"]).T).astype(np.int32),
        "wuT": np.ascontiguousarray(np.asarray(inp["L0_ffn_w_up"]).T).astype(np.int32),
        "wdT": np.ascontiguousarray(np.asarray(inp["L0_ffn_w_down"]).T).astype(np.int32),
        "WqT": np.ascontiguousarray(np.asarray(inp["L0_attn_Wq"]).T),
        "WkT": np.ascontiguousarray(np.asarray(inp["L0_attn_Wk"]).T),
        "WvT": np.ascontiguousarray(np.asarray(inp["L0_attn_Wv"]).T),
        "WoT": np.ascontiguousarray(np.asarray(inp["L0_attn_Wo"]).T),
        "bqkv": np.concatenate([bq.reshape(8, 128).T, bk.reshape(8, 128).T,
                                bv.reshape(8, 128).T], axis=1).astype(np.float32),
        "bo": np.ascontiguousarray(bo.reshape(8, 128).T).astype(np.float32),
        "Wq1": np.ascontiguousarray(np.asarray(inp["L1_attn_Wq"])),
        "Wk1": np.ascontiguousarray(np.asarray(inp["L1_attn_Wk"])),
        "Wv1": np.ascontiguousarray(np.asarray(inp["L1_attn_Wv"])),
        "Wlr": np.ascontiguousarray(np.asarray(inp["L1_attn_Wlr"])),
        "convr": convr, "winbd0": winbd0, "woutbd0": woutbd0, "woutd40": woutd40,
        "maskT0": maskT0,
        "WoL1": np.ascontiguousarray(np.asarray(inp["L1_attn_Wo"])),
        "W1": np.ascontiguousarray(np.asarray(inp["L1_ffn_W1"])),
        "W2": np.ascontiguousarray(np.asarray(inp["L1_ffn_W2"])),
        "Wo1": np.ascontiguousarray(np.asarray(inp["L1_ffn_Wo1"])),
        "Wo2": np.ascontiguousarray(np.asarray(inp["L1_ffn_Wo2"])),
        "b1r": b1r, "b2": b2,
        "bo1r": np.stack([bo1[0:128], bo1[128:256]], axis=1).astype(np.float32),
        "bo2r": np.ascontiguousarray(bo2.reshape(1, 9)).astype(np.float32),
        "gbr": gbr, "mixr": mixr, "selbc": selbc,
    }
    nc = build_fused()
    in_maps = [dict(common, x=np.ascontiguousarray(x_all[cc])) for cc in range(8)]
    res = run_bass_kernel_spmd(nc, in_maps, core_ids=list(range(8)))
    return np.stack([np.asarray(res.results[cc]["out"], dtype=np.float32)
                     for cc in range(8)], axis=0)


kernel = kernel_fused
